# version 01
# baseline (speedup 1.0000x reference)
import numpy as np
import ml_dtypes
import concourse.bass as bass
import concourse.mybir as mybir
from concourse.bass_utils import run_bass_kernel_spmd

F32 = mybir.dt.float32
BF16 = mybir.dt.bfloat16
I32 = mybir.dt.int32
AF = mybir.ActivationFunctionType
ALU = mybir.AluOpType
AX = mybir.AxisListType.X

T = 2048
D = 1024
RPC = 2
B_TOT = 16
NT = T // 128
NSEL = 1024
DFFL = 512
LOGK = float(np.log(np.float32(1152.0)))
LN32 = float(np.log(np.float32(32.0)))
EPS_LN = 1e-5
SOFT_STAB = 2.0

STAGE = "A"
LAST_RES = None


class Ctx:
    def __init__(self, nc):
        self.nc = nc
        self.sem = {}
        self.cnt = {}
        self.t = {}

    def sems(self, *names):
        for n in names:
            self.sem[n] = self.nc.semaphore(n).__enter__()
            self.cnt[n] = 0

    def dma(self, eng, name, out, in_, **kw):
        eng.dma_start(out, in_, **kw).then_inc(self.sem[name], 16)
        self.cnt[name] += 16
        return self.cnt[name]

    def idma(self, eng, name, out, out_off, in_, in_off):
        eng.indirect_dma_start(out, out_off, in_, in_off).then_inc(
            self.sem[name], 16)
        self.cnt[name] += 16
        return self.cnt[name]

    def tick(self, name, ins):
        ins.then_inc(self.sem[name], 1)
        self.cnt[name] += 1
        return self.cnt[name]

    def sb(self, name, shape, dtype):
        h = self.nc.sbuf_tensor(name, shape, dtype).__enter__()
        self.t[name] = h
        return h

    def ps(self, name):
        h = self.nc.psum_tensor(name, [128, 1024], F32).__enter__()
        self.t[name] = h
        return h

    def dr(self, name, shape, dtype, kind=None):
        h = (self.nc.dram_tensor(name, shape, dtype, kind=kind) if kind
             else self.nc.dram_tensor(name, shape, dtype))
        self.t[name] = h
        return h


def wge(eng, c, name, val):
    eng.wait_ge(c.sem[name], val)



def alloc_common(c, with_ffn=True):
    nc = c.nc
    if with_ffn:
        c.dr("w1lp", [128, 8 * DFFL], F32, "ExternalInput")
        c.dr("w2lp", [128, 4 * D], F32, "ExternalInput")
        c.dr("b1lp", [128, 4], F32, "ExternalInput")
        c.dr("b2lp", [1, D], F32, "ExternalInput")
        c.dr("w1hp", [128, 32 * D], F32, "ExternalInput")
        c.dr("w2hp", [128, 32 * D], F32, "ExternalInput")
        c.dr("b1hp", [128, 32], F32, "ExternalInput")
        c.dr("b2hp", [1, D], F32, "ExternalInput")
        c.dr("glp", [128, 8], F32, "ExternalInput")
        c.dr("ghp", [128, 8], F32, "ExternalInput")
        c.dr("rbc", [128, D], F32, "ExternalInput")
        c.dr("trif", [128, 128], F32, "ExternalInput")
        c.dr("ynorm_d", [RPC * T, D], BF16)
        c.dr("ycomp_d", [NSEL + 1, D], BF16)
        c.dr("hcomp_d", [NSEL + 1, D], F32)

    c.sb("ones128", [128, 1], F32)
    c.sb("ones128b", [128, 1], BF16)
    c.sb("zero128", [128, 1], F32)
    c.sb("ln32t", [128, 1], F32)
    c.sb("zero1", [1, 1], F32)
    c.sb("negstab", [128, 1], F32)
    c.sb("onesrow_f", [1, 128], F32)
    c.sb("onesrow_b", [1, 128], BF16)
    c.sb("r1bc", [128, 1], F32)
    if with_ffn:
        c.sb("tri_b", [128, 128], BF16)
        c.sb("rbc_s", [128, D], F32)
        c.sb("w1l_s", [128, 8 * DFFL], BF16)
        c.sb("w2l_s", [128, 4 * D], BF16)
        c.sb("b1l_s", [128, 4], F32)
        c.sb("b2l_s", [1, D], BF16)
        c.sb("b1h_s", [128, 32], F32)
        c.sb("b2h_s", [1, D], BF16)
        c.sb("gl_s", [128, 8], F32)
        c.sb("gh_s", [128, 8], F32)
    for i in range(2):
        c.sb("wbuf%d" % i, [128, D], BF16)
    for i in range(3):
        c.sb("xt%d" % i, [128, D], F32)
    for i in range(2):
        c.sb("ynt%d" % i, [128, D], BF16)
        c.sb("x3t%d" % i, [128, D], F32)
        c.sb("hg%d" % i, [128, D], F32)
    c.sb("acc", [128, D], F32)
    c.sb("sq_scr", [128, D], F32)
    c.t["zrow"] = c.t["sq_scr"][0:1, 0:D]
    c.t["gp_scr"] = c.t["sq_scr"]
    for n in ["stat_sum", "stat_ssq", "stat_m", "stat_k", "stat_bb", "stat_rv",
              "xr_t", "sfin", "sbt", "satf", "mtmp", "cumf"]:
        c.sb(n, [128, RPC * NT], F32)
    c.sb("satb", [128, RPC * NT], BF16)
    c.sb("cumidx", [128, RPC * NT], I32)
    c.sb("etile", [128, RPC * NT], F32)
    c.sb("rowred", [128, RPC], F32)
    c.sb("na_row", [1, RPC], F32)
    c.sb("nab", [128, RPC], F32)
    c.sb("oscr", [1, NT], F32)
    c.sb("oscr2", [1, NT], F32)
    c.sb("st_a", [128, 1], F32)
    c.sb("acc_sq0", [128, 1], F32)
    c.sb("acc_sq1", [128, 1], F32)
    c.sb("st_b", [128, 1], F32)
    c.sb("st_c", [128, 1], F32)
    c.sb("ycT", [128, 8 * D], BF16)
    c.sb("hl", [128, 4 * T], BF16)
    c.sb("hh", [128, 32 * NSEL], BF16)
    c.t["hc_s"] = c.t["acc"]
    for i in range(4):
        c.ps("qb%d" % i)
    for i in range(8):
        c.t["pb%d" % i] = c.t["qb%d" % (i // 2)][:, (i % 2) * 512:(i % 2) * 512 + 512]
    c.sems("dX", "dY", "dW", "dS", "dG", "dH", "dO", "dT", "tn", "ve", "se", "gp")


def alloc_attn(c):
    c.sb("V_t", [128, 16 * 16 * 65], BF16)
    c.sb("opab", [128, T], BF16)
    c.sb("normcols", [128, 16], F32)
    c.t["oscr3"] = c.t["x3t0"]
    c.t["vwbuf"] = c.t["hl"][:, 0:4096]
    c.sb("bqp_s", [128, 8], F32)
    c.sb("bkp_s", [128, 8], F32)
    c.sb("bvp_s", [1, 1024], BF16)
    c.dr("wqp", [128, 8192], F32, "ExternalInput")
    c.dr("wkp", [128, 8192], F32, "ExternalInput")
    c.dr("wvp", [128, 8192], F32, "ExternalInput")
    c.dr("bqp", [128, 8], F32, "ExternalInput")
    c.dr("bkp", [128, 8], F32, "ExternalInput")
    c.dr("bvp", [1, 1024], F32, "ExternalInput")


def phase_attn_consts(c):
    t = c.t
    with c.nc.Block() as blk:
        def g1(g):
            c.dma(g, "dH", t["bqp_s"][:, :], t["bqp"][:, :])
            c.dma(g, "dH", t["bkp_s"][:, :], t["bkp"][:, :])
            c.dma(g, "dH", t["bvp_s"][:, :], t["bvp"][:, :])
            g.wait_ge(c.sem["dH"], c.cnt["dH"])
        blk.gpsimd(g1)



def phase_consts(c, with_ffn=True):
    t = c.t
    with c.nc.Block() as blk:
        @blk.gpsimd
        def _(g):
            g.memset(t["ones128"][:, :], 1.0)
            g.memset(t["ones128b"][:, :], 1.0)
            g.memset(t["zero128"][:, :], 0.0)
            g.memset(t["ln32t"][:, :], LN32)
            g.memset(t["zero1"][:, :], 0.0)
            g.memset(t["negstab"][:, :], -SOFT_STAB)
            g.memset(t["onesrow_f"][:, :], 1.0)
            g.memset(t["onesrow_b"][:, :], 1.0)
            g.memset(t["zrow"][:, :], 0.0)
            if with_ffn:
                c.dma(g, "dH", t["rbc_s"][:, :], t["rbc"][:, :])
                c.dma(g, "dH", t["tri_b"][:, :], t["trif"][:, :])
                c.dma(g, "dH", t["w1l_s"][:, :], t["w1lp"][:, :])
                c.dma(g, "dH", t["w2l_s"][:, :], t["w2lp"][:, :])
                c.dma(g, "dH", t["b1l_s"][:, :], t["b1lp"][:, :])
                c.dma(g, "dH", t["b2l_s"][:, :], t["b2lp"][:, :])
                c.dma(g, "dH", t["b1h_s"][:, :], t["b1hp"][:, :])
                c.dma(g, "dH", t["b2h_s"][:, :], t["b2hp"][:, :])
                c.dma(g, "dH", t["gl_s"][:, :], t["glp"][:, :])
                c.dma(g, "dH", t["gh_s"][:, :], t["ghp"][:, :])
                g.wait_ge(c.sem["dH"], c.cnt["dH"])
                c.dma(g, "dH", t["hcomp_d"][NSEL:NSEL + 1, :], t["zrow"][0:1, :])
                g.wait_ge(c.sem["dH"], c.cnt["dH"])

        if with_ffn:
            ld_all = c.cnt["dH"]

            @blk.scalar
            def _(s):
                wge(s, c, "dH", ld_all)
                for cc in range(8):
                    sl = t["w1l_s"][:, cc * DFFL:(cc + 1) * DFFL]
                    s.activation(sl, sl, AF.Identity, bias=t["zero128"][:, :],
                                 scale=t["gl_s"][:, cc:cc + 1])
                for cc in range(4):
                    sl = t["w2l_s"][:, cc * D:(cc + 1) * D]
                    s.activation(sl, sl, AF.Identity, bias=t["zero128"][:, :],
                                 scale=0.5)
                s.activation(t["b2l_s"][:, :], t["b2l_s"][:, :], AF.Identity,
                             bias=t["zero1"][:, :], scale=0.5)
                s.activation(t["b2h_s"][:, :], t["b2h_s"][:, :], AF.Identity,
                             bias=t["zero1"][:, :], scale=0.5)

            @blk.vector
            def _(v):
                wge(v, c, "dH", ld_all)
                c.tick("ve", v.reduce_sum(t["st_a"][0:1, 0:1],
                                          t["rbc_s"][0:1, :], axis=AX))

            @blk.tensor
            def _(tn):
                wge(tn, c, "ve", c.cnt["ve"])
                c.tick("tn", tn.matmul(t["pb7"][:, 0:1], t["onesrow_f"][:, :],
                                       t["st_a"][0:1, 0:1], start=True,
                                       stop=True))

            @blk.vector
            def _(v):
                wge(v, c, "tn", c.cnt["tn"])
                c.tick("ve", v.tensor_copy(t["r1bc"][:, :], t["pb7"][:, 0:1]))


def emit_stats_pass2(c, blk, row, mode):
    t = c.t
    r0 = row * NT

    def vp2(v):
        v.tensor_scalar_mul(t["stat_m"][:, r0:r0 + NT],
                            t["stat_sum"][:, r0:r0 + NT], 1.0 / D)
        v.drain()
        v.tensor_mul(t["mtmp"][:, r0:r0 + NT], t["stat_sum"][:, r0:r0 + NT],
                     t["stat_m"][:, r0:r0 + NT])
        v.drain()
        v.tensor_sub(t["stat_ssq"][:, r0:r0 + NT], t["stat_ssq"][:, r0:r0 + NT],
                     t["mtmp"][:, r0:r0 + NT])
        v.drain()
        c.tick("ve", v.tensor_scalar(t["stat_rv"][:, r0:r0 + NT],
                                     t["stat_ssq"][:, r0:r0 + NT],
                                     1.0 / D, EPS_LN, ALU.mult, ALU.add))
    blk.vector(vp2)
    ve_p = c.cnt["ve"]

    def sp2(s, ve_p=ve_p):
        wge(s, c, "ve", ve_p)
        s.activation(t["etile"][:, r0:r0 + NT], t["stat_rv"][:, r0:r0 + NT],
                     AF.Ln, bias=t["zero128"][:, :])
        s.drain()
        last = s.activation(t["stat_rv"][:, r0:r0 + NT],
                            t["etile"][:, r0:r0 + NT], AF.Exp,
                            bias=t["zero128"][:, :], scale=-0.5)
        if mode == "ffn":
            s.activation(t["etile"][:, r0:r0 + NT], t["stat_ssq"][:, r0:r0 + NT],
                         AF.Ln, bias=t["zero128"][:, :])
            s.drain()
            last = s.activation(t["stat_k"][:, r0:r0 + NT],
                                t["etile"][:, r0:r0 + NT], AF.Exp,
                                bias=t["ln32t"][:, :], scale=-0.5)
        c.tick("se", last)
    blk.scalar(sp2)
    se_p = c.cnt["se"]

    def vp3(v, se_p=se_p):
        wge(v, c, "se", se_p)
        kap = t["stat_k"] if mode == "ffn" else t["stat_rv"]
        last = v.scalar_tensor_tensor(t["stat_bb"][:, r0:r0 + NT],
                                      t["stat_m"][:, r0:r0 + NT], -1.0,
                                      kap[:, r0:r0 + NT], ALU.mult, ALU.mult)
        if mode == "ffn":
            v.tensor_scalar(t["mtmp"][:, r0:r0 + NT], t["stat_m"][:, r0:r0 + NT],
                            t["r1bc"][:, :], None, ALU.mult)
            v.drain()
            v.tensor_sub(t["mtmp"][:, r0:r0 + NT], t["xr_t"][:, r0:r0 + NT],
                         t["mtmp"][:, r0:r0 + NT])
            v.drain()
            last = v.tensor_mul(t["sfin"][:, r0:r0 + NT],
                                t["mtmp"][:, r0:r0 + NT],
                                t["stat_rv"][:, r0:r0 + NT])
        c.tick("ve", last)
    blk.vector(vp3)


def emit_apply_y(c, blk, row, src, dst, mode):
    t = c.t
    r0 = row * NT
    kap = t["stat_k"] if mode == "ffn" else t["stat_rv"]
    ld = [0] * NT
    se_yn = [0] * NT
    yn_dma = [0] * NT
    ve_gate = c.cnt["ve"]
    se_gate = c.cnt["se"]
    gp_gate = c.cnt["gp"]
    for tt in range(NT):
        col = r0 + tt
        xb = t["xt%d" % (tt % 3)]
        yb = t["ynt%d" % (tt % 2)]
        rowslice = slice(row * T + tt * 128, row * T + tt * 128 + 128)

        def load(sy, tt=tt, xb=xb, rowslice=rowslice):
            if tt >= 3:
                wge(sy, c, "se", se_yn[tt - 3])
            else:
                wge(sy, c, "ve", ve_gate)
                wge(sy, c, "se", se_gate)
                wge(sy, c, "gp", gp_gate)
            ld[tt] = c.dma(sy, "dX", xb[:, :], src[rowslice, :])
        blk.sync(load)

        def sey(s, tt=tt, col=col, xb=xb, yb=yb):
            wge(s, c, "dX", ld[tt])
            if tt == 0:
                wge(s, c, "ve", ve_gate)
            if tt >= 2:
                wge(s, c, "dY", yn_dma[tt - 2])
            se_yn[tt] = c.tick("se", s.activation(
                yb[:, :], xb[:, :], AF.Identity,
                bias=t["stat_bb"][:, col:col + 1], scale=kap[:, col:col + 1]))
        blk.scalar(sey)

        def gpy(g, tt=tt, yb=yb, rowslice=rowslice):
            wge(g, c, "se", se_yn[tt])
            yn_dma[tt] = c.dma(g, "dY", dst[rowslice, :], yb[:, :])
        blk.gpsimd(gpy)

    def fin(sy):
        wge(sy, c, "dY", c.cnt["dY"])
        wge(sy, c, "dX", c.cnt["dX"])
    blk.sync(fin)


def phase_lnpass(c, row, xsrc, mode):
    t = c.t
    with c.nc.Block() as blk:
        ld = [0] * NT
        ve_done = [0] * NT
        se_acc = [0] * NT
        for tt in range(NT):
            col = row * NT + tt
            xb = t["xt%d" % (tt % 3)]

            def load(sy, tt=tt, xb=xb):
                if tt >= 3:
                    wge(sy, c, "ve", ve_done[tt - 3])
                ld[tt] = c.dma(sy, "dX", xb[:, :],
                               xsrc[(row * T + tt * 128):(row * T + tt * 128 + 128), :])
            blk.sync(load)

            def se1(s, tt=tt, xb=xb):
                wge(s, c, "dX", ld[tt])
                if tt >= 2:
                    wge(s, c, "ve", ve_done[tt - 2])
                se_acc[tt] = c.tick("se", s.activation(
                    t["sq_scr"][:, :], xb[:, :], AF.Square,
                    bias=t["zero128"][:, :],
                    accum_out=t["acc_sq%d" % (tt % 2)][:, :]))
            blk.scalar(se1)

            def ve1(v, tt=tt, col=col, xb=xb):
                wge(v, c, "dX", ld[tt])
                v.reduce_sum(t["stat_sum"][:, col:col + 1], xb[:, :], axis=AX)
                if mode == "ffn":
                    v.scalar_tensor_tensor(t["gp_scr"][:, :], xb[:, :], 1.0,
                                           t["rbc_s"][:, :], ALU.mult, ALU.mult,
                                           accum_out=t["xr_t"][:, col:col + 1])
                wge(v, c, "se", se_acc[tt])
                ve_done[tt] = c.tick("ve", v.tensor_copy(
                    t["stat_ssq"][:, col:col + 1],
                    t["acc_sq%d" % (tt % 2)][:, :]))
            blk.vector(ve1)

        emit_stats_pass2(c, blk, row, mode)
        emit_apply_y(c, blk, row, xsrc, c.t["ynorm_d"], mode)


def phase_coor(c):
    t = c.t
    with c.nc.Block() as blk:
        def init(v):
            c.tick("ve", v.memset(t["sbt"][:, :], 0.0))
        blk.vector(init)

        for it in range(1, 51):
            def se1(s, it=it):
                wge(s, c, "ve", c.cnt["ve"])
                last = None
                for r in range(RPC):
                    last = s.activation(t["etile"][:, r * NT:(r + 1) * NT],
                                        t["sbt"][:, r * NT:(r + 1) * NT], AF.Exp,
                                        bias=t["zero128"][:, :])
                c.tick("se", last)
            blk.scalar(se1)
            se_e = c.cnt["se"]

            def ver(v, se_e=se_e, it=it):
                wge(v, c, "se", se_e)
                if it > 1:
                    wge(v, c, "tn", c.cnt["tn"] - 1)
                last = None
                for r in range(RPC):
                    last = v.reduce_sum(t["rowred"][:, r:r + 1],
                                        t["etile"][:, r * NT:(r + 1) * NT],
                                        axis=AX)
                c.tick("ve", last)
            blk.vector(ver)
            ve_r = c.cnt["ve"]

            def tn1(tn, ve_r=ve_r):
                wge(tn, c, "ve", ve_r)
                c.tick("tn", tn.matmul(t["pb7"][0:1, 0:RPC], t["ones128"][:, :],
                                       t["rowred"][:, 0:RPC], start=True,
                                       stop=True))
            blk.tensor(tn1)
            tn_c = c.cnt["tn"]

            def se2(s, tn_c=tn_c):
                wge(s, c, "tn", tn_c)
                c.tick("se", s.activation(t["na_row"][0:1, 0:RPC],
                                          t["pb7"][0:1, 0:RPC], AF.Ln,
                                          bias=t["zero1"][:, :]))
            blk.scalar(se2)
            se_l = c.cnt["se"]

            def ve1(v, se_l=se_l):
                wge(v, c, "se", se_l)
                c.tick("ve", v.tensor_scalar_add(
                    t["na_row"][0:1, 0:RPC], t["na_row"][0:1, 0:RPC], -LOGK))
            blk.vector(ve1)
            ve_n = c.cnt["ve"]

            def tn2(tn, ve_n=ve_n):
                wge(tn, c, "ve", ve_n)
                c.tick("tn", tn.matmul(t["pb6"][:, 0:RPC], t["onesrow_f"][:, :],
                                       t["na_row"][0:1, 0:RPC], start=True,
                                       stop=True))
            blk.tensor(tn2)
            tn_b = c.cnt["tn"]

            def ve2(v, tn_b=tn_b, it=it):
                wge(v, c, "tn", tn_b)
                v.tensor_copy(t["nab"][:, 0:RPC], t["pb6"][:, 0:RPC])
                v.drain()
                if it < 50:
                    last = None
                    for r in range(RPC):
                        last = v.tensor_scalar_min(
                            t["sbt"][:, r * NT:(r + 1) * NT],
                            t["sfin"][:, r * NT:(r + 1) * NT],
                            t["nab"][:, r:r + 1])
                    c.tick("ve", last)
                else:
                    for r in range(RPC):
                        v.tensor_scalar(
                            t["satf"][:, r * NT:(r + 1) * NT],
                            t["sfin"][:, r * NT:(r + 1) * NT],
                            t["nab"][:, r:r + 1], None, ALU.is_ge)
                    v.drain()
                    c.tick("ve", v.tensor_copy(t["satb"][:, :], t["satf"][:, :]))
            blk.vector(ve2)


def phase_mask(c, row):
    t = c.t
    with c.nc.Block() as blk:
        r0 = row * NT

        def tn1(tn):
            tn.matmul(t["pb0"][:, 0:NT], t["tri_b"][:, :],
                      t["satb"][:, r0:r0 + NT], start=True,
                      stop=False, skip_group_check=True)
            c.tick("tn", tn.matmul(t["pb1"][0:1, 0:NT], t["ones128b"][:, :],
                                   t["satb"][:, r0:r0 + NT], start=True,
                                   stop=True, skip_group_check=True))
        blk.tensor(tn1)
        tn_a = c.cnt["tn"]

        def ve1(v, tn_a=tn_a):
            wge(v, c, "tn", tn_a)
            v.tensor_copy(t["oscr2"][0:1, 0:NT], t["pb1"][0:1, 0:NT])
            v.drain()
            v.tensor_tensor_scan(t["oscr"][0:1, 0:NT], t["oscr2"][0:1, 0:NT],
                                 t["oscr2"][0:1, 0:NT], 0.0, ALU.add,
                                 ALU.bypass)
            v.drain()
            c.tick("ve", v.tensor_sub(t["oscr"][0:1, 0:NT],
                                      t["oscr"][0:1, 0:NT],
                                      t["oscr2"][0:1, 0:NT]))
        blk.vector(ve1)
        ve_o = c.cnt["ve"]

        def tn2(tn, ve_o=ve_o):
            wge(tn, c, "ve", ve_o)
            c.tick("tn", tn.matmul(t["pb0"][:, 0:NT], t["onesrow_f"][:, :],
                                   t["oscr"][0:1, 0:NT], start=False, stop=True,
                                   skip_group_check=True))
        blk.tensor(tn2)
        tn_b = c.cnt["tn"]

        def ve2(v, tn_b=tn_b):
            wge(v, c, "tn", tn_b)
            v.tensor_scalar(t["mtmp"][:, r0:r0 + NT], t["pb0"][:, 0:NT],
                            float(NSEL), None, ALU.is_le)
            v.tensor_scalar_add(t["cumf"][:, r0:r0 + NT], t["pb0"][:, 0:NT],
                                -float(NSEL + 1))
            v.drain()
            v.tensor_mul(t["mtmp"][:, r0:r0 + NT], t["mtmp"][:, r0:r0 + NT],
                         t["satf"][:, r0:r0 + NT])
            v.drain()
            v.tensor_mul(t["cumf"][:, r0:r0 + NT], t["cumf"][:, r0:r0 + NT],
                         t["mtmp"][:, r0:r0 + NT])
            v.drain()
            v.tensor_scalar_add(t["cumf"][:, r0:r0 + NT],
                                t["cumf"][:, r0:r0 + NT], float(NSEL))
            v.drain()
            c.tick("ve", v.tensor_copy(t["cumidx"][:, r0:r0 + NT],
                                       t["cumf"][:, r0:r0 + NT]))
        blk.vector(ve2)


def phase_scatter(c, row):
    t = c.t
    with c.nc.Block() as blk:
        ld = [0] * NT
        sc = [0] * NT
        for tt in range(NT):
            yb = t["ynt%d" % (tt % 2)]

            def load(sy, tt=tt, yb=yb):
                if tt >= 2:
                    wge(sy, c, "dS", sc[tt - 2])
                ld[tt] = c.dma(sy, "dY", yb[:, :], t["ynorm_d"][
                    (row * T + tt * 128):(row * T + tt * 128 + 128), :])
            blk.sync(load)

            def scat(g, tt=tt, yb=yb):
                wge(g, c, "dY", ld[tt])
                sc[tt] = c.idma(g, "dS", t["ycomp_d"][:, :],
                                bass.IndirectOffsetOnAxis(
                                    t["cumidx"][:, row * NT + tt:row * NT + tt + 1], 0),
                                yb[:, :], None)
            blk.gpsimd(scat)

        def fin(sy):
            wge(sy, c, "dS", c.cnt["dS"])
        blk.sync(fin)


def phase_heavy(c, row, sub=99):
    import os
    sub = int(os.environ.get('KHEAVY', '99'))
    t = c.t
    nc = c.nc
    with nc.Block() as blk:
        tl = [[0] * 2 for _ in range(8)]

        def loads(sy):
            for i in range(8):
                for h in range(2):
                    tl[i][h] = c.dma(
                        sy, "dT", t["ycT"][:, i * D + h * 512:i * D + (h + 1) * 512],
                        t["ycomp_d"][h * 512:(h + 1) * 512, i * 128:(i + 1) * 128],
                        transpose=True)
        blk.sync(loads)

        def fold(v):
            for i in range(8):
                for h in range(2):
                    wge(v, c, "dT", tl[i][h])
                    sl = t["ycT"][:, i * D + h * 512:i * D + (h + 1) * 512]
                    last = v.tensor_scalar_mul(sl, sl, t["gh_s"][:, i:i + 1])
            c.tick("ve", last)
        blk.vector(fold)

    if sub < 1:
        return
    with nc.Block() as blk:
        wl = [0] * 32
        mm = [0] * 32
        ev = [0] * 32
        for dd in range(32):
            i = dd % 2
            bk = (dd % 2) * 2

            def load(g, dd=dd, i=i):
                if dd >= 2:
                    wge(g, c, "tn", mm[dd - 2])
                wl[dd] = c.dma(g, "dW", t["wbuf%d" % i][:, :],
                               t["w1hp"][:, dd * D:(dd + 1) * D])
            blk.gpsimd(load)

            def mms(tn, dd=dd, i=i, bk=bk):
                wge(tn, c, "dW", wl[dd])
                if dd >= 2:
                    wge(tn, c, "se", ev[dd - 2])
                last = None
                for h in range(2):
                    for cc in range(8):
                        last = tn.matmul(
                            t["pb%d" % (bk + h)][:, :],
                            t["wbuf%d" % i][:, cc * 128:(cc + 1) * 128],
                            t["ycT"][:, cc * D + h * 512:cc * D + (h + 1) * 512],
                            start=(cc == 0), stop=(cc == 7))
                mm[dd] = c.tick("tn", last)
            blk.tensor(mms)

            def evac(s, dd=dd, bk=bk):
                wge(s, c, "tn", mm[dd])
                last = None
                for h in range(2):
                    last = s.activation(
                        t["hh"][:, dd * D + h * 512:dd * D + (h + 1) * 512],
                        t["pb%d" % (bk + h)][:, :], AF.Gelu,
                        bias=t["b1h_s"][:, dd:dd + 1])
                ev[dd] = c.tick("se", last)
            blk.scalar(evac)

    if sub < 2:
        return
    with nc.Block() as blk:
        mm2 = []
        st2 = {}
        for of in range(2):
            se_prev_evac = c.cnt["se"]
            wl2 = [0] * 32
            for dd in range(32):
                i = dd % 2
                j = of * 32 + dd

                def load(g, dd=dd, i=i, of=of, j=j):
                    if j >= 2:
                        wge(g, c, "tn", mm2[j - 2])
                    wl2[dd] = c.dma(g, "dW", t["wbuf%d" % i][:, 0:512],
                                    t["w2hp"][:, dd * D + of * 512:dd * D + (of + 1) * 512])
                blk.gpsimd(load)

                def mms(tn, dd=dd, i=i, se_prev_evac=se_prev_evac):
                    wge(tn, c, "dW", wl2[dd])
                    if dd == 0:
                        wge(tn, c, "se", se_prev_evac)
                    last = None
                    for ct in range(8):
                        last = tn.matmul(
                            t["pb%d" % ct][:, :],
                            t["hh"][:, dd * D + ct * 128:dd * D + (ct + 1) * 128],
                            t["wbuf%d" % i][:, 0:512],
                            start=(dd == 0), stop=False,
                            skip_group_check=True)
                    mm2.append(c.tick("tn", last))
                blk.tensor(mms)

            def tn1(tn, of=of):
                last = None
                for ct in range(8):
                    last = tn.matmul(t["pb%d" % ct][:, :], t["onesrow_b"][:, :],
                                     t["b2h_s"][0:1, of * 512:(of + 1) * 512],
                                     start=False, stop=True,
                                     skip_group_check=True)
                c.tick("tn", last)
            blk.tensor(tn1)
            tn_d = c.cnt["tn"]

            for ct in range(8):
                def ev(s, ct=ct, tn_d=tn_d, of=of):
                    wge(s, c, "tn", tn_d)
                    if (of, ct - 2) in st2:
                        wge(s, c, "dH", st2[(of, ct - 2)])
                    c.tick("se", s.activation(
                        t["hc_s"][:, (ct % 2) * 512:(ct % 2) * 512 + 512],
                        t["pb%d" % ct][:, :], AF.Identity,
                        bias=t["zero128"][:, :], scale=0.5))
                blk.scalar(ev)
                se_e = c.cnt["se"]

                def store(g, ct=ct, se_e=se_e, of=of):
                    wge(g, c, "se", se_e)
                    st2[(of, ct)] = c.dma(
                        g, "dH",
                        t["hcomp_d"][ct * 128:(ct + 1) * 128,
                                     of * 512:(of + 1) * 512],
                        t["hc_s"][:, (ct % 2) * 512:(ct % 2) * 512 + 512])
                blk.gpsimd(store)

        def fin(sy):
            wge(sy, c, "dH", c.cnt["dH"])
        blk.sync(fin)


def phase_light_m1(c, row):
    t = c.t
    with c.nc.Block() as blk:
        for tb in range(4):
            tl = [0] * 8

            def loads(sy, tb=tb):
                if tb >= 1:
                    wge(sy, c, "tn", c.cnt["tn"])
                for cc in range(8):
                    tl[cc] = c.dma(
                        sy, "dT", t["ycT"][:, cc * 512:(cc + 1) * 512],
                        t["ynorm_d"][(row * T + tb * 512):(row * T + (tb + 1) * 512),
                                     cc * 128:(cc + 1) * 128],
                        transpose=True)
            blk.sync(loads)

            def mms(tn, tb=tb):
                last = None
                for cc in range(8):
                    wge(tn, c, "dT", tl[cc])
                    for d in range(4):
                        last = tn.matmul(
                            t["pb%d" % (4 + d)][:, :],
                            t["w1l_s"][:, cc * DFFL + d * 128:cc * DFFL + (d + 1) * 128],
                            t["ycT"][:, cc * 512:(cc + 1) * 512],
                            start=(cc == 0), stop=(cc == 7),
                            skip_group_check=True)
                c.tick("tn", last)
            blk.tensor(mms)
            tn_m = c.cnt["tn"]

            def evac(v, tb=tb, tn_m=tn_m):
                wge(v, c, "tn", tn_m)
                last = None
                for d in range(4):
                    last = v.tensor_copy(
                        t["hl"][:, d * T + tb * 512:d * T + (tb + 1) * 512],
                        t["pb%d" % (4 + d)][:, :])
                c.tick("ve", last)
            blk.vector(evac)

        def gelu(s):
            wge(s, c, "ve", c.cnt["ve"])
            last = None
            for d in range(4):
                last = s.activation(t["hl"][:, d * T:(d + 1) * T],
                                    t["hl"][:, d * T:(d + 1) * T], AF.Gelu,
                                    bias=t["b1l_s"][:, d:d + 1])
            c.tick("se", last)
        blk.scalar(gelu)


def phase_light2_add(c, row, xsrc, xdst, mode, ydst=None):
    t = c.t
    with c.nc.Block() as blk:
        ld = [0] * NT
        gl = [0] * NT
        mm = [0] * NT
        ve_acc = [0] * NT
        gp_x3 = [0] * NT
        st_x = [0] * NT
        st_done = [0] * NT
        for tt in range(NT):
            col = row * NT + tt
            xb = t["xt%d" % (tt % 3)]
            x3 = t["x3t%d" % (tt % 2)]
            hgb = t["hg%d" % (tt % 2)]
            bk = (tt % 2) * 2
            rowslice = slice(row * T + tt * 128, row * T + tt * 128 + 128)

            def load(sy, tt=tt, xb=xb, rowslice=rowslice):
                if tt >= 3:
                    wge(sy, c, "gp", gp_x3[tt - 3])
                ld[tt] = c.dma(sy, "dX", xb[:, :], xsrc[rowslice, :])
            blk.sync(load)

            def gath(g, tt=tt, hgb=hgb, col=col):
                if tt >= 2:
                    wge(g, c, "ve", ve_acc[tt - 2])
                gl[tt] = c.idma(g, "dG", hgb[:, :], None, t["hcomp_d"][:, :],
                                bass.IndirectOffsetOnAxis(
                                    t["cumidx"][:, col:col + 1], 0))
            blk.gpsimd(gath)

            def mms(tn, tt=tt, bk=bk):
                if tt >= 2:
                    wge(tn, c, "ve", ve_acc[tt - 2])
                last = None
                for of in range(2):
                    for d in range(4):
                        tn.matmul(t["pb%d" % (bk + of)][:, :],
                                  t["hl"][:, d * T + tt * 128:d * T + (tt + 1) * 128],
                                  t["w2l_s"][:, d * D + of * 512:d * D + (of + 1) * 512],
                                  start=(d == 0), stop=False,
                                  skip_group_check=True)
                    last = tn.matmul(t["pb%d" % (bk + of)][:, :],
                                     t["onesrow_b"][:, :],
                                     t["b2l_s"][0:1, of * 512:(of + 1) * 512],
                                     start=False, stop=True,
                                     skip_group_check=True)
                mm[tt] = c.tick("tn", last)
            blk.tensor(mms)

            def veacc(v, tt=tt, bk=bk, hgb=hgb):
                wge(v, c, "tn", mm[tt])
                wge(v, c, "dG", gl[tt])
                if tt >= 1:
                    wge(v, c, "gp", gp_x3[tt - 1])
                v.tensor_add(t["acc"][:, 0:512], t["pb%d" % bk][:, :],
                             hgb[:, 0:512])
                ve_acc[tt] = c.tick("ve", v.tensor_add(
                    t["acc"][:, 512:1024], t["pb%d" % (bk + 1)][:, :],
                    hgb[:, 512:1024]))
            blk.vector(veacc)

            def gpadd(g, tt=tt, xb=xb, x3=x3):
                wge(g, c, "ve", ve_acc[tt])
                wge(g, c, "dX", ld[tt])
                if tt >= 2:
                    wge(g, c, "dO", st_x[tt - 2])
                    if mode != "none":
                        wge(g, c, "ve", st_done[tt - 2])
                gp_x3[tt] = c.tick("gp", g.tensor_add(x3[:, :], xb[:, :],
                                                      t["acc"][:, :]))
            blk.gpsimd(gpadd)

            def store(sy, tt=tt, x3=x3, rowslice=rowslice):
                wge(sy, c, "gp", gp_x3[tt])
                st_x[tt] = c.dma(sy, "dO", xdst[rowslice, :], x3[:, :])
            blk.sync(store)

            if mode != "none":
                def se1(s, tt=tt, x3=x3):
                    wge(s, c, "gp", gp_x3[tt])
                    if tt >= 2:
                        wge(s, c, "ve", st_done[tt - 2])
                    c.tick("se", s.activation(
                        t["sq_scr"][:, :], x3[:, :], AF.Square,
                        bias=t["zero128"][:, :],
                        accum_out=t["acc_sq%d" % (tt % 2)][:, :]))
                blk.scalar(se1)
                se_acc = c.cnt["se"]

                def ve1(v, tt=tt, col=col, x3=x3, se_acc=se_acc):
                    wge(v, c, "gp", gp_x3[tt])
                    v.reduce_sum(t["stat_sum"][:, col:col + 1], x3[:, :], axis=AX)
                    if mode == "ffn":
                        v.scalar_tensor_tensor(t["gp_scr"][:, :], x3[:, :], 1.0,
                                               t["rbc_s"][:, :], ALU.mult,
                                               ALU.mult,
                                               accum_out=t["xr_t"][:, col:col + 1])
                    wge(v, c, "se", se_acc)
                    st_done[tt] = c.tick("ve", v.tensor_copy(
                        t["stat_ssq"][:, col:col + 1],
                        t["acc_sq%d" % (tt % 2)][:, :]))
                blk.vector(ve1)

        if mode != "none":
            emit_stats_pass2(c, blk, row, mode)
            emit_apply_y(c, blk, row, xdst,
                         ydst if ydst is not None else t["ynorm_d"], mode)
        else:
            def fin(sy):
                wge(sy, c, "dO", c.cnt["dO"])
                wge(sy, c, "dG", c.cnt["dG"])
            blk.sync(fin)


def phase_qkv(c, row, y2_d):
    t = c.t
    Q0, K0 = 0, 8 * T
    with c.nc.Block() as blk:
        def vinit(g):
            V3 = t["V_t"][:, :].rearrange("p (a h e) -> p a h e", a=16, h=16, e=65)
            c.tick("gp", g.memset(V3[:, :, :, 64:65], 1.0))
        blk.gpsimd(vinit)
        gp_init = c.cnt["gp"]

        mmg = [0] * 4
        grpw = [0] * 4
        wld = {}
        for tb in range(4):
            tl = [0] * 8

            def loads(sy, tb=tb, tl=tl):
                if tb >= 1:
                    wge(sy, c, "tn", c.cnt["tn"])
                for cc in range(8):
                    tl[cc] = c.dma(
                        sy, "dT", t["ycT"][:, cc * 512:(cc + 1) * 512],
                        y2_d[(row * T + tb * 512):(row * T + (tb + 1) * 512),
                             cc * 128:(cc + 1) * 128],
                        transpose=True)
            blk.sync(loads)

            grp = 0
            for kind in ("q", "k"):
                wsrc = t["wqp"] if kind == "q" else t["wkp"]
                bsb = t["bqp_s"] if kind == "q" else t["bkp_s"]
                base = Q0 if kind == "q" else K0
                for p in range(8):
                    bk = grp % 4
                    wb = t["wbuf%d" % (grp % 2)]
                    grp += 1

                    def wload(g, p=p, wsrc=wsrc, wb=wb, kind=kind, tb=tb,
                              tn_gate=c.cnt["tn"]):
                        wge(g, c, "tn", tn_gate)
                        wld[(kind, p, tb)] = c.dma(
                            g, "dW", wb[:, :], wsrc[:, p * 1024:(p + 1) * 1024])
                    blk.gpsimd(wload)

                    def mms(tn, p=p, bk=bk, wb=wb, kind=kind, tb=tb, tl=tl):
                        wge(tn, c, "dW", wld[(kind, p, tb)])
                        for cc in range(8):
                            wge(tn, c, "dT", tl[cc])
                        wge(tn, c, "se", grpw[bk])
                        last = None
                        for cc in range(8):
                            last = tn.matmul(
                                t["pb%d" % bk][:, :],
                                wb[:, cc * 128:(cc + 1) * 128],
                                t["ycT"][:, cc * 512:(cc + 1) * 512],
                                start=(cc == 0), stop=(cc == 7),
                                skip_group_check=True)
                        mmg[bk] = c.tick("tn", last)
                    blk.tensor(mms)

                    def evac(s, p=p, bk=bk, bsb=bsb, base=base, tb=tb):
                        wge(s, c, "tn", mmg[bk])
                        grpw[bk] = c.tick("se", s.activation(
                            t["hh"][:, base + p * T + tb * 512:
                                    base + p * T + (tb + 1) * 512],
                            t["pb%d" % bk][:, :], AF.Identity,
                            bias=bsb[:, p:p + 1]))
                    blk.scalar(evac)

            for g8 in range(2):
                def vwload(g, g8=g8, tb=tb):
                    wge(g, c, "tn", c.cnt["tn"])
                    wld[("v", g8, tb)] = c.dma(
                        g, "dW", t["vwbuf"][:, :],
                        t["wvp"][:, g8 * 4096:(g8 + 1) * 4096])
                blk.gpsimd(vwload)

                for sub in range(4):
                    tkt = tb * 4 + sub
                    bk = grp % 4
                    grp += 1

                    def mms(tn, sub=sub, g8=g8, bk=bk, tb=tb, tl=tl):
                        wge(tn, c, "dW", wld[("v", g8, tb)])
                        wge(tn, c, "se", grpw[bk])
                        if tb == 0 and sub == 0 and g8 == 0:
                            wge(tn, c, "gp", gp_init)
                        for cc in range(8):
                            tn.matmul(
                                t["pb%d" % bk][:, :],
                                t["ycT"][:, cc * 512 + sub * 128:cc * 512 + (sub + 1) * 128],
                                t["vwbuf"][:, cc * 512:(cc + 1) * 512],
                                start=(cc == 0), stop=False,
                                skip_group_check=True)
                        last = tn.matmul(t["pb%d" % bk][:, :], t["onesrow_b"][:, :],
                                         t["bvp_s"][0:1, g8 * 512:(g8 + 1) * 512],
                                         start=False, stop=True,
                                         skip_group_check=True)
                        mmg[bk] = c.tick("tn", last)
                    blk.tensor(mms)

                    def evac(s, tkt=tkt, g8=g8, bk=bk):
                        wge(s, c, "tn", mmg[bk])
                        V3 = t["V_t"][:, :].rearrange("p (a h e) -> p a h e",
                                                      a=16, h=16, e=65)
                        grpw[bk] = c.tick("se", s.activation(
                            V3[:, tkt, g8 * 8:(g8 + 1) * 8, 0:64],
                            t["pb%d" % bk][:, :], AF.Identity,
                            bias=t["zero128"][:, :]))
                    blk.scalar(evac)

        def fin(sy):
            wge(sy, c, "se", c.cnt["se"])
        blk.sync(fin)


def phase_sav(c, row, oprime):
    t = c.t
    Q0, K0 = 0, 8 * T
    with c.nc.Block() as blk:
        if row == 0:
            def ninit(v):
                c.tick("ve", v.memset(t["normcols"][:, :], 0.0))
            blk.vector(ninit)

        nsq_gate = [c.cnt["ve"]]
        av_gate = [c.cnt["ve"]]
        for p in range(8):
            op_dma_gate = []
            sticks = [0] * 64
            eticks = [0] * 64
            avt = [0] * 64
            for qb in range(4):
                for tkc in range(16):
                    k = qb * 16 + tkc
                    par = k % 2

                    def smm(tn, p=p, qb=qb, tkc=tkc, k=k, par=par):
                        if k >= 2:
                            wge(tn, c, "se", eticks[k - 2])
                        if tkc == 0:
                            wge(tn, c, "ve", nsq_gate[0])
                        tn.matmul(t["qb%d" % par][:, 0:512],
                                  t["hh"][0:64, K0 + p * T + tkc * 128:
                                          K0 + p * T + (tkc + 1) * 128],
                                  t["hh"][0:64, Q0 + p * T + qb * 512:
                                          Q0 + p * T + (qb + 1) * 512],
                                  start=True, stop=True, tile_position=(0, 0),
                                  skip_group_check=True)
                        sticks[k] = c.tick("tn", tn.matmul(
                            t["qb%d" % par][:, 512:1024],
                            t["hh"][64:128, K0 + p * T + tkc * 128:
                                    K0 + p * T + (tkc + 1) * 128],
                            t["hh"][64:128, Q0 + p * T + qb * 512:
                                    Q0 + p * T + (qb + 1) * 512],
                            start=True, stop=True, tile_position=(64, 0),
                            skip_group_check=True))
                    blk.tensor(smm)

                    def sexp(s, k=k, par=par):
                        wge(s, c, "tn", sticks[k])
                        if k >= 2:
                            wge(s, c, "tn", avt[k - 2])
                        eticks[k] = c.tick("se", s.activation(
                            t["hl"][:, par * 1024:(par + 1) * 1024],
                            t["qb%d" % par][:, 0:1024], AF.Exp,
                            scale=0.125, bias=t["negstab"][:, :]))
                    blk.scalar(sexp)

                    def avmm(tn, p=p, tkc=tkc, k=k, par=par):
                        wge(tn, c, "se", eticks[k])
                        if tkc == 0:
                            wge(tn, c, "ve", av_gate[0])
                        hA, hB = 2 * p, 2 * p + 1
                        tn.matmul(t["qb2"][0:65, 0:512],
                                  t["V_t"][:, tkc * 1040 + hA * 65:
                                           tkc * 1040 + hA * 65 + 65],
                                  t["hl"][:, par * 1024:par * 1024 + 512],
                                  start=(tkc == 0), stop=(tkc == 15),
                                  skip_group_check=True)
                        avt[k] = c.tick("tn", tn.matmul(
                            t["qb2"][0:65, 512:1024],
                            t["V_t"][:, tkc * 1040 + hB * 65:
                                     tkc * 1040 + hB * 65 + 65],
                            t["hl"][:, par * 1024 + 512:(par + 1) * 1024],
                            start=(tkc == 0), stop=(tkc == 15),
                            skip_group_check=True))
                    blk.tensor(avmm)

                def ep_ve1(v, qb=qb, k=qb * 16 + 15):
                    wge(v, c, "tn", avt[k])
                    v.reciprocal(t["oscr3"][0:1, 0:512], t["qb2"][64:65, 0:512])
                    c.tick("ve", v.reciprocal(t["oscr3"][0:1, 512:1024],
                                              t["qb2"][64:65, 512:1024]))
                blk.vector(ep_ve1)
                ve_r = c.cnt["ve"]

                def ep_tn(tn, ve_r=ve_r):
                    wge(tn, c, "ve", ve_r)
                    tn.matmul(t["qb3"][0:64, 0:512], t["onesrow_f"][0:1, 0:64],
                              t["oscr3"][0:1, 0:512], start=True, stop=True,
                              skip_group_check=True)
                    c.tick("tn", tn.matmul(t["qb3"][0:64, 512:1024],
                                           t["onesrow_f"][0:1, 0:64],
                                           t["oscr3"][0:1, 512:1024],
                                           start=True, stop=True,
                                           skip_group_check=True))
                blk.tensor(ep_tn)
                tn_b = c.cnt["tn"]

                def ep_se(s, tn_b=tn_b):
                    wge(s, c, "tn", tn_b)
                    s.activation(t["acc"][0:64, 0:512], t["qb3"][0:64, 0:512],
                                 AF.Identity, bias=t["zero128"][0:64, :])
                    c.tick("se", s.activation(t["acc"][0:64, 512:1024],
                                              t["qb3"][0:64, 512:1024],
                                              AF.Identity,
                                              bias=t["zero128"][0:64, :]))
                blk.scalar(ep_se)
                se_c = c.cnt["se"]

                def ep_ve2(v, qb=qb, se_c=se_c, dO_gate=c.cnt["dO"]):
                    wge(v, c, "se", se_c)
                    if qb == 0:
                        wge(v, c, "dO", dO_gate)
                    v.tensor_mul(t["opab"][0:64, qb * 512:(qb + 1) * 512],
                                 t["qb2"][0:64, 0:512], t["acc"][0:64, 0:512])
                    av_gate[0] = c.tick("ve", v.tensor_mul(
                        t["opab"][64:128, qb * 512:(qb + 1) * 512],
                        t["qb2"][0:64, 512:1024], t["acc"][0:64, 512:1024]))
                blk.vector(ep_ve2)
                ve_m = c.cnt["ve"]
                op_dma_gate.append(ve_m)

                def ep_sq(s, qb=qb, ve_m=ve_m):
                    wge(s, c, "ve", ve_m)
                    s.activation(t["ynt0"][0:64, 0:512],
                                 t["opab"][0:64, qb * 512:(qb + 1) * 512],
                                 AF.Square, bias=t["zero128"][0:64, :])
                    c.tick("se", s.activation(
                        t["ynt0"][0:64, 512:1024],
                        t["opab"][64:128, qb * 512:(qb + 1) * 512],
                        AF.Square, bias=t["zero128"][0:64, :]))
                blk.scalar(ep_sq)
                se_q = c.cnt["se"]

                def ep_nsq(tn, se_q=se_q):
                    wge(tn, c, "se", se_q)
                    last = None
                    for sub in range(8):
                        last = tn.matmul(t["qb0"][:, sub:sub + 1],
                                         t["ynt0"][0:64, sub * 128:(sub + 1) * 128],
                                         t["ones128b"][0:64, 0:1],
                                         start=True, stop=True,
                                         skip_group_check=True)
                    c.tick("tn", last)
                blk.tensor(ep_nsq)
                tn_n = c.cnt["tn"]

                def ep_nrm(s, tn_n=tn_n):
                    wge(s, c, "tn", tn_n)
                    s.activation(t["sq_scr"][:, 0:8], t["qb0"][:, 0:8], AF.Ln,
                                 bias=t["zero128"][:, :])
                    s.drain()
                    c.tick("se", s.activation(t["sq_scr"][:, 8:16],
                                              t["sq_scr"][:, 0:8], AF.Exp,
                                              bias=t["zero128"][:, :], scale=0.5))
                blk.scalar(ep_nrm)
                se_n = c.cnt["se"]

                def ep_acc(v, p=p, se_n=se_n):
                    wge(v, c, "se", se_n)
                    v.reduce_sum(t["st_a"][:, :], t["sq_scr"][:, 8:12], axis=AX)
                    v.reduce_sum(t["st_b"][:, :], t["sq_scr"][:, 12:16], axis=AX)
                    v.drain()
                    v.tensor_add(t["normcols"][:, 2 * p:2 * p + 1],
                                 t["normcols"][:, 2 * p:2 * p + 1], t["st_a"][:, :])
                    v.drain()
                    nsq_gate[0] = c.tick("ve", v.tensor_add(
                        t["normcols"][:, 2 * p + 1:2 * p + 2],
                        t["normcols"][:, 2 * p + 1:2 * p + 2], t["st_b"][:, :]))
                blk.vector(ep_acc)

            def opdma(g, p=p, row=row, gates=tuple(op_dma_gate)):
                for gv in gates:
                    wge(g, c, "ve", gv)
                base = (row * 8 + p) * 128
                c.dma(g, "dO", oprime[base:base + 64, :], t["opab"][0:64, :])
                c.dma(g, "dO", oprime[base + 64:base + 128, :],
                      t["opab"][64:128, :])
                wge(g, c, "dO", c.cnt["dO"])
            blk.gpsimd(opdma)

        def fin(sy):
            wge(sy, c, "dO", c.cnt["dO"])
        blk.sync(fin)


def phase_normfin(c, normpart):
    t = c.t
    with c.nc.Block() as blk:
        def tn1(tn):
            c.tick("tn", tn.matmul(t["qb0"][0:16, 0:1], t["normcols"][:, :],
                                   t["ones128"][:, :], start=True, stop=True,
                                   skip_group_check=True))
        blk.tensor(tn1)
        tn_f = c.cnt["tn"]

        def ve1(v, tn_f=tn_f):
            wge(v, c, "tn", tn_f)
            c.tick("ve", v.tensor_copy(t["st_c"][0:16, 0:1], t["qb0"][0:16, 0:1]))
        blk.vector(ve1)
        ve_f = c.cnt["ve"]

        def gp1(g, ve_f=ve_f):
            wge(g, c, "ve", ve_f)
            c.dma(g, "dO", normpart[:, :], t["st_c"][0:16, 0:1])
            wge(g, c, "dO", c.cnt["dO"])
        blk.gpsimd(gp1)


def alloc_l2(c):
    c.sb("an_sb", [8, 16], F32)
    c.sb("gt_sb", [16, 1], F32)
    c.sb("bo_sb", [16, D], F32)
    c.sb("g16", [16, 1], F32)
    c.sb("clip16", [16, 1], F32)
    c.sb("gvec", [128, 8], F32)
    c.sb("gfe", [1, 8], F32)
    c.sb("gfo", [1, 8], F32)
    c.sb("bconst_b", [1, D], BF16)
    c.sb("elo_s", [1, 128], F32)
    c.sb("ehi_s", [1, 128], F32)
    c.dr("opri", [16 * 128, T], BF16, "ExternalInput")
    c.dr("allnorm", [8, 16], F32, "ExternalInput")
    c.dr("wop", [128, 8192], F32, "ExternalInput")
    c.dr("bop", [16, D], F32, "ExternalInput")
    c.dr("gatep", [16, 1], F32, "ExternalInput")
    c.dr("elo", [1, 128], F32, "ExternalInput")
    c.dr("ehi", [1, 128], F32, "ExternalInput")
    c.dr("gd", [16, 1], F32)


def phase_l2prep(c):
    t = c.t
    with c.nc.Block() as blk:
        def g1(g):
            c.dma(g, "dH", t["an_sb"][:, :], t["allnorm"][:, :])
            c.dma(g, "dH", t["gt_sb"][:, :], t["gatep"][:, :])
            c.dma(g, "dH", t["bo_sb"][:, :], t["bop"][:, :])
            c.dma(g, "dH", t["elo_s"][:, :], t["elo"][:, :])
            c.dma(g, "dH", t["ehi_s"][:, :], t["ehi"][:, :])
            for i in range(8):
                c.dma(g, "dH", t["ycT"][:, i * 1024:(i + 1) * 1024],
                      t["wop"][:, i * 1024:(i + 1) * 1024])
            g.wait_ge(c.sem["dH"], c.cnt["dH"])
        blk.gpsimd(g1)
        ld = c.cnt["dH"]

        def tn1(tn, ld=ld):
            wge(tn, c, "dH", ld)
            c.tick("tn", tn.matmul(t["qb0"][0:16, 0:1], t["an_sb"][:, :],
                                   t["ones128"][0:8, 0:1], start=True,
                                   stop=True, skip_group_check=True))
        blk.tensor(tn1)
        tn_s = c.cnt["tn"]

        def ve1(v, tn_s=tn_s, ld=ld):
            wge(v, c, "tn", tn_s)
            wge(v, c, "dH", ld)
            v.tensor_copy(t["st_a"][0:16, 0:1], t["qb0"][0:16, 0:1])
            v.drain()
            v.tensor_scalar(t["st_a"][0:16, 0:1], t["st_a"][0:16, 0:1],
                            1.0 / (B_TOT * T), 1e-5, ALU.mult, ALU.max)
            v.drain()
            v.reciprocal(t["st_b"][0:16, 0:1], t["st_a"][0:16, 0:1])
            v.tensor_scalar(t["clip16"][0:16, 0:1], t["gt_sb"][:, :],
                            0.0, 1.0, ALU.max, ALU.min)
            v.drain()
            v.scalar_tensor_tensor(t["g16"][0:16, 0:1], t["clip16"][0:16, 0:1],
                                   1.0 / 16.0, t["st_b"][0:16, 0:1],
                                   ALU.mult, ALU.mult)
            v.drain()
            c.tick("ve", v.tensor_scalar(t["bo_sb"][:, :], t["bo_sb"][:, :],
                                         t["clip16"][:, :], 1.0 / 16.0,
                                         ALU.mult, ALU.mult))
        blk.vector(ve1)
        ve_g = c.cnt["ve"]

        def g2(g, ve_g=ve_g):
            wge(g, c, "ve", ve_g)
            c.dma(g, "dH", t["gd"][:, :], t["g16"][0:16, 0:1])
            g.wait_ge(c.sem["dH"], c.cnt["dH"])
            g.dma_start(t["gfe"][0:1, :], t["gd"][0:16:2, 0:1],
                        allow_slow_non_contiguous=True).then_inc(c.sem["dH"], 16)
            c.cnt["dH"] += 16
            g.dma_start(t["gfo"][0:1, :], t["gd"][1:16:2, 0:1],
                        allow_slow_non_contiguous=True).then_inc(c.sem["dH"], 16)
            c.cnt["dH"] += 16
            g.wait_ge(c.sem["dH"], c.cnt["dH"])
        blk.gpsimd(g2)
        ld2 = c.cnt["dH"]

        def tn2(tn, ld2=ld2):
            wge(tn, c, "dH", ld2)
            tn.matmul(t["qb1"][:, 0:8], t["elo_s"][:, :], t["gfe"][:, :],
                      start=True, stop=False, skip_group_check=True)
            tn.matmul(t["qb1"][:, 0:8], t["ehi_s"][:, :], t["gfo"][:, :],
                      start=False, stop=True, skip_group_check=True)
            tn.matmul(t["qb2"][0:1, 0:512], t["ones128"][0:16, 0:1],
                      t["bo_sb"][:, 0:512], start=True, stop=True,
                      skip_group_check=True)
            c.tick("tn", tn.matmul(t["qb2"][0:1, 512:1024],
                                   t["ones128"][0:16, 0:1],
                                   t["bo_sb"][:, 512:1024], start=True,
                                   stop=True, skip_group_check=True))
        blk.tensor(tn2)
        tn_g = c.cnt["tn"]

        def ve2(v, tn_g=tn_g):
            wge(v, c, "tn", tn_g)
            v.tensor_copy(t["gvec"][:, :], t["qb1"][:, 0:8])
            c.tick("ve", v.tensor_copy(t["bconst_b"][0:1, :],
                                       t["qb2"][0:1, 0:1024]))
        blk.vector(ve2)
        ve_c = c.cnt["ve"]

        def se1(s, ve_c=ve_c):
            wge(s, c, "ve", ve_c)
            last = None
            for i in range(8):
                sl = t["ycT"][:, i * 1024:(i + 1) * 1024]
                last = s.activation(sl, sl, AF.Identity,
                                    bias=t["zero128"][:, :],
                                    scale=t["gvec"][:, i:i + 1])
            c.tick("se", last)
        blk.scalar(se1)


def phase_oproj_add(c, row, xsrc, xdst, ydst):
    t = c.t
    with c.nc.Block() as blk:
        def opload(g):
            for p in range(8):
                c.dma(g, "dW", t["hh"][:, p * T:(p + 1) * T],
                      t["opri"][(row * 8 + p) * 128:(row * 8 + p) * 128 + 128, :])
            g.wait_ge(c.sem["dW"], c.cnt["dW"])
        blk.gpsimd(opload)
        opld = c.cnt["dW"]

        ld = [0] * NT
        mm = [0] * NT
        ve_x3 = [0] * NT
        st_x = [0] * NT
        st_done = [0] * NT
        for tt in range(NT):
            col = row * NT + tt
            xb = t["xt%d" % (tt % 3)]
            x3 = t["x3t%d" % (tt % 2)]
            bk = (tt % 2) * 2
            rowslice = slice(row * T + tt * 128, row * T + tt * 128 + 128)

            def load(sy, tt=tt, xb=xb, rowslice=rowslice):
                if tt >= 3:
                    wge(sy, c, "ve", ve_x3[tt - 3])
                ld[tt] = c.dma(sy, "dX", xb[:, :], xsrc[rowslice, :])
            blk.sync(load)

            def mms(tn, tt=tt, bk=bk):
                if tt == 0:
                    wge(tn, c, "dW", opld)
                if tt >= 2:
                    wge(tn, c, "ve", ve_x3[tt - 2])
                last = None
                for of in range(2):
                    for p in range(8):
                        tn.matmul(t["pb%d" % (bk + of)][:, :],
                                  t["hh"][:, p * T + tt * 128:p * T + (tt + 1) * 128],
                                  t["ycT"][:, p * 1024 + of * 512:p * 1024 + (of + 1) * 512],
                                  start=(p == 0), stop=False,
                                  skip_group_check=True)
                    last = tn.matmul(t["pb%d" % (bk + of)][:, :],
                                     t["onesrow_b"][:, :],
                                     t["bconst_b"][0:1, of * 512:(of + 1) * 512],
                                     start=False, stop=True,
                                     skip_group_check=True)
                mm[tt] = c.tick("tn", last)
            blk.tensor(mms)

            def vex3(v, tt=tt, bk=bk, xb=xb, x3=x3):
                wge(v, c, "tn", mm[tt])
                wge(v, c, "dX", ld[tt])
                if tt >= 2:
                    wge(v, c, "dO", st_x[tt - 2])
                    wge(v, c, "ve", st_done[tt - 2])
                v.tensor_add(x3[:, 0:512], t["pb%d" % bk][:, :], xb[:, 0:512])
                ve_x3[tt] = c.tick("ve", v.tensor_add(
                    x3[:, 512:1024], t["pb%d" % (bk + 1)][:, :],
                    xb[:, 512:1024]))
            blk.vector(vex3)

            def store(sy, tt=tt, x3=x3, rowslice=rowslice):
                wge(sy, c, "ve", ve_x3[tt])
                st_x[tt] = c.dma(sy, "dO", xdst[rowslice, :], x3[:, :])
            blk.sync(store)

            def se1(s, tt=tt, x3=x3):
                wge(s, c, "ve", ve_x3[tt])
                if tt >= 2:
                    wge(s, c, "ve", st_done[tt - 2])
                c.tick("se", s.activation(
                    t["sq_scr"][:, :], x3[:, :], AF.Square,
                    bias=t["zero128"][:, :],
                    accum_out=t["acc_sq%d" % (tt % 2)][:, :]))
            blk.scalar(se1)
            se_acc = c.cnt["se"]

            def ve1(v, tt=tt, col=col, x3=x3, se_acc=se_acc):
                v.reduce_sum(t["stat_sum"][:, col:col + 1], x3[:, :], axis=AX)
                v.scalar_tensor_tensor(t["gp_scr"][:, :], x3[:, :], 1.0,
                                       t["rbc_s"][:, :], ALU.mult, ALU.mult,
                                       accum_out=t["xr_t"][:, col:col + 1])
                wge(v, c, "se", se_acc)
                st_done[tt] = c.tick("ve", v.tensor_copy(
                    t["stat_ssq"][:, col:col + 1],
                    t["acc_sq%d" % (tt % 2)][:, :]))
            blk.vector(ve1)

        emit_stats_pass2(c, blk, row, "ffn")
        emit_apply_y(c, blk, row, xdst, ydst, "ffn")



def build_l1():
    nc = bass.Bass("TRN2", target_bir_lowering=False)
    c = Ctx(nc)
    xin = c.dr("xin", [RPC * T, D], F32, "ExternalInput")
    x2o = c.dr("x2o", [RPC * T, D], F32, "ExternalOutput")
    oprime = c.dr("oprime", [16 * 128, T], BF16, "ExternalOutput")
    normpart = c.dr("normpart", [16, 1], F32, "ExternalOutput")
    y2_d = c.dr("y2_d", [RPC * T, D], BF16)
    alloc_common(c)
    alloc_attn(c)
    phase_consts(c)
    phase_attn_consts(c)
    for r in range(RPC):
        phase_lnpass(c, r, xin, "ffn")
    phase_coor(c)
    for r in range(RPC):
        phase_mask(c, r)
    for r in range(RPC):
        phase_scatter(c, r)
        phase_heavy(c, r)
        phase_light_m1(c, r)
        phase_light2_add(c, r, xin, x2o, "attn", ydst=y2_d)
    for r in range(RPC):
        phase_qkv(c, r, y2_d)
        phase_sav(c, r, oprime)
    phase_normfin(c, normpart)
    return nc


def build_l2():
    nc = bass.Bass("TRN2", target_bir_lowering=False)
    c = Ctx(nc)
    x2i = c.dr("x2i", [RPC * T, D], F32, "ExternalInput")
    xout = c.dr("xout", [RPC * T, D], F32, "ExternalOutput")
    alloc_common(c)
    alloc_l2(c)
    phase_consts(c)
    phase_l2prep(c)
    for r in range(RPC):
        phase_oproj_add(c, r, x2i, xout, c.t["ynorm_d"])
    phase_coor(c)
    for r in range(RPC):
        phase_mask(c, r)
    for r in range(RPC):
        phase_scatter(c, r)
        phase_heavy(c, r)
        phase_light_m1(c, r)
        phase_light2_add(c, r, xout, xout, "none")
    return nc



def ffn_host_inputs(inputs):
    w1l = np.asarray(inputs["L0_ffn_w1_l"], np.float32)
    w2l = np.asarray(inputs["L0_ffn_w2_l"], np.float32)
    w1h = np.asarray(inputs["L0_ffn_w1_h"], np.float32)
    w2h = np.asarray(inputs["L0_ffn_w2_h"], np.float32)
    return {
        "w1lp": np.ascontiguousarray(
            w1l.T.reshape(8, 128, DFFL).transpose(1, 0, 2).reshape(128, 8 * DFFL)),
        "w2lp": np.ascontiguousarray(
            w2l.T.reshape(4, 128, D).transpose(1, 0, 2).reshape(128, 4 * D)),
        "b1lp": np.ascontiguousarray(
            np.asarray(inputs["L0_ffn_b1_l"], np.float32).reshape(4, 128).T),
        "b2lp": np.asarray(inputs["L0_ffn_b2_l"], np.float32).reshape(1, D).copy(),
        "w1hp": np.ascontiguousarray(
            w1h.T.reshape(8, 128, 32, 128).transpose(1, 2, 0, 3).reshape(128, 32 * D)),
        "w2hp": np.ascontiguousarray(
            w2h.T.reshape(32, 128, 2, 512).transpose(1, 0, 2, 3).reshape(128, 32 * D)),
        "b1hp": np.ascontiguousarray(
            np.asarray(inputs["L0_ffn_b1_h"], np.float32).reshape(32, 128).T),
        "b2hp": np.asarray(inputs["L0_ffn_b2_h"], np.float32).reshape(1, D).copy(),
        "glp": np.ascontiguousarray(
            np.asarray(inputs["L0_ffn_gamma_l"], np.float32).reshape(8, 128).T),
        "ghp": np.ascontiguousarray(
            np.asarray(inputs["L0_ffn_gamma_h"], np.float32).reshape(8, 128).T),
        "rbc": np.ascontiguousarray(np.broadcast_to(
            np.asarray(inputs["L0_ffn_routing_token"], np.float32), (128, D))),
        "trif": np.triu(np.ones((128, 128), np.float32)),
    }


def attn_host_inputs(inputs):
    Wq = np.asarray(inputs["L0_attn_Wq"], np.float32)
    Wk = np.asarray(inputs["L0_attn_Wk"], np.float32)
    Wv = np.asarray(inputs["L0_attn_Wv"], np.float32)
    bq = np.asarray(inputs["L0_attn_bq"], np.float32)
    bk = np.asarray(inputs["L0_attn_bk"], np.float32)
    bv = np.asarray(inputs["L0_attn_bv"], np.float32)
    qk = lambda W: np.ascontiguousarray(
        W.reshape(8, 2, 8, 128, 64).transpose(3, 0, 2, 1, 4).reshape(128, 8192))
    return {
        "wqp": qk(Wq), "wkp": qk(Wk),
        "wvp": np.ascontiguousarray(
            Wv.reshape(2, 8, 8, 128, 64).transpose(3, 0, 2, 1, 4).reshape(128, 8192)),
        "bqp": np.ascontiguousarray(bq.reshape(8, 2, 64).transpose(1, 2, 0).reshape(128, 8)),
        "bkp": np.ascontiguousarray(bk.reshape(8, 2, 64).transpose(1, 2, 0).reshape(128, 8)),
        "bvp": np.ascontiguousarray(bv.reshape(1, 1024)),
    }


def l2_host_inputs(inputs):
    Wo = np.asarray(inputs["L0_attn_Wo"], np.float32)
    elo = np.zeros((1, 128), np.float32)
    elo[0, 0:64] = 1.0
    ehi = np.zeros((1, 128), np.float32)
    ehi[0, 64:128] = 1.0
    return {
        "wop": np.ascontiguousarray(
            Wo.reshape(8, 2, 64, D).transpose(1, 2, 0, 3).reshape(128, 8192)),
        "bop": np.asarray(inputs["L0_attn_bo"], np.float32).copy(),
        "gatep": np.asarray(inputs["L0_attn_gate"], np.float32).reshape(16, 1).copy(),
        "elo": elo, "ehi": ehi,
    }


def kernel(**inputs):
    x = np.asarray(inputs["L0_attn_hidden_states"], np.float32)
    ffn_in = ffn_host_inputs(inputs)
    attn_in = attn_host_inputs(inputs)
    l2_in = l2_host_inputs(inputs)

    nc1 = build_l1()
    maps1 = []
    for core in range(8):
        m = dict(ffn_in)
        m.update(attn_in)
        m["xin"] = np.ascontiguousarray(
            x[2 * core:2 * core + 2].reshape(RPC * T, D))
        maps1.append(m)
    res1 = run_bass_kernel_spmd(nc1, maps1, core_ids=list(range(8)))

    allnorm = np.ascontiguousarray(
        np.stack([res1.results[cc]["normpart"][:, 0] for cc in range(8)]))
    nc2 = build_l2()
    maps2 = []
    for core in range(8):
        m = dict(ffn_in)
        m.update(l2_in)
        m["x2i"] = res1.results[core]["x2o"]
        m["opri"] = res1.results[core]["oprime"]
        m["allnorm"] = allnorm
        maps2.append(m)
    res2 = run_bass_kernel_spmd(nc2, maps2, core_ids=list(range(8)))
    out = np.concatenate(
        [r["xout"].reshape(RPC, T, D) for r in res2.results], axis=0)
    return out


# version 03
# speedup vs baseline: 1.7281x; 1.7281x over previous
import numpy as np
import ml_dtypes
import concourse.bass as bass
import concourse.mybir as mybir
from concourse.bass_utils import run_bass_kernel_spmd

F32 = mybir.dt.float32
BF16 = mybir.dt.bfloat16
I32 = mybir.dt.int32
AF = mybir.ActivationFunctionType
ALU = mybir.AluOpType
AX = mybir.AxisListType.X

T = 2048
D = 1024
RPC = 2
B_TOT = 16
NT = T // 128
NSEL = 1024
DFFL = 512
LOGK = float(np.log(np.float32(1152.0)))
LN32 = float(np.log(np.float32(32.0)))
EPS_LN = 1e-5
SOFT_STAB = 2.0

STAGE = "A"
LAST_RES = None


class Ctx:
    def __init__(self, nc):
        self.nc = nc
        self.sem = {}
        self.cnt = {}
        self.t = {}

    def sems(self, *names):
        for n in names:
            self.sem[n] = self.nc.semaphore(n).__enter__()
            self.cnt[n] = 0

    def dma(self, eng, name, out, in_, **kw):
        eng.dma_start(out, in_, **kw).then_inc(self.sem[name], 16)
        self.cnt[name] += 16
        return self.cnt[name]

    def idma(self, eng, name, out, out_off, in_, in_off):
        eng.indirect_dma_start(out, out_off, in_, in_off).then_inc(
            self.sem[name], 16)
        self.cnt[name] += 16
        return self.cnt[name]

    def tick(self, name, ins):
        ins.then_inc(self.sem[name], 1)
        self.cnt[name] += 1
        return self.cnt[name]

    def sb(self, name, shape, dtype):
        h = self.nc.sbuf_tensor(name, shape, dtype).__enter__()
        self.t[name] = h
        return h

    def ps(self, name):
        h = self.nc.psum_tensor(name, [128, 1024], F32).__enter__()
        self.t[name] = h
        return h

    def dr(self, name, shape, dtype, kind=None):
        h = (self.nc.dram_tensor(name, shape, dtype, kind=kind) if kind
             else self.nc.dram_tensor(name, shape, dtype))
        self.t[name] = h
        return h



_ACT_SET_CACHE = {}


def _act_set_id(nc, name):
    key = (id(nc.m), name)
    if key not in _ACT_SET_CACHE:
        from concourse.hw_specs import get_activation_tables
        names = list(get_activation_tables(nc.m.arch).keys())
        _ACT_SET_CACHE[key] = names.index(name)
    return _ACT_SET_CACHE[key]


def load_act_set(c, s, name):
    s.add_instruction(mybir.InstLoadActFuncSet(
        name=c.nc.get_next_instruction_name(),
        act_func_set_id=_act_set_id(c.nc, name), ins=[], outs=[]))


def wge(eng, c, name, val):
    eng.wait_ge(c.sem[name], val)



def alloc_common(c, with_ffn=True):
    nc = c.nc
    if with_ffn:
        c.dr("w1lp", [128, 8 * DFFL], F32, "ExternalInput")
        c.dr("w2lp", [128, 4 * D], F32, "ExternalInput")
        c.dr("b1lp", [128, 4], F32, "ExternalInput")
        c.dr("b2lp", [1, D], F32, "ExternalInput")
        c.dr("w1hp", [128, 32 * D], F32, "ExternalInput")
        c.dr("w2hp", [128, 32 * D], F32, "ExternalInput")
        c.dr("b1hp", [128, 32], F32, "ExternalInput")
        c.dr("b2hp", [1, D], F32, "ExternalInput")
        c.dr("glp", [128, 8], F32, "ExternalInput")
        c.dr("ghp", [128, 8], F32, "ExternalInput")
        c.dr("rbc", [128, D], F32, "ExternalInput")
        c.dr("trif", [128, 128], F32, "ExternalInput")
        c.dr("ynorm_d", [RPC * T, D], BF16)
        c.dr("ycomp_d", [NSEL + 1, D], BF16)
        c.dr("hcomp_d", [NSEL + 1, D], F32)

    c.sb("ones128", [128, 1], F32)
    c.sb("ones128b", [128, 1], BF16)
    c.sb("zero128", [128, 1], F32)
    c.sb("ln32t", [128, 1], F32)
    c.sb("zero1", [1, 1], F32)
    c.sb("negstab", [128, 1], F32)
    c.sb("onesrow_f", [1, 128], F32)
    c.sb("onesrow_b", [1, 128], BF16)
    c.sb("r1bc", [128, 1], F32)
    if with_ffn:
        c.sb("tri_b", [128, 128], BF16)
        c.sb("rbc_s", [128, D], F32)
        c.sb("w1l_s", [128, 8 * DFFL], BF16)
        c.sb("w2l_s", [128, 4 * D], BF16)
        c.sb("b1l_s", [128, 4], F32)
        c.sb("b2l_s", [1, D], BF16)
        c.sb("b1h_s", [128, 32], F32)
        c.sb("b2h_s", [1, D], BF16)
        c.sb("gl_s", [128, 8], F32)
        c.sb("gh_s", [128, 8], F32)
    for i in range(2):
        c.sb("wbuf%d" % i, [128, D], BF16)
    for i in range(3):
        c.sb("xt%d" % i, [128, D], F32)
    for i in range(2):
        c.sb("ynt%d" % i, [128, D], BF16)
        c.sb("x3t%d" % i, [128, D], F32)
        c.sb("hg%d" % i, [128, D], F32)
    c.sb("acc", [128, D], F32)
    c.sb("sq_scr", [128, D], F32)
    c.t["zrow"] = c.t["sq_scr"][0:1, 0:D]
    c.t["gp_scr"] = c.t["sq_scr"]
    for n in ["stat_sum", "stat_ssq", "stat_m", "stat_k", "stat_bb", "stat_rv",
              "xr_t", "sfin", "sbt", "satf", "mtmp", "cumf"]:
        c.sb(n, [128, RPC * NT], F32)
    c.sb("satb", [128, RPC * NT], BF16)
    c.sb("cumidx", [128, RPC * NT], I32)
    c.sb("etile", [128, RPC * NT], F32)
    c.sb("rowred", [128, RPC], F32)
    c.sb("na_row", [1, RPC], F32)
    c.sb("nab", [128, RPC], F32)
    c.sb("oscr", [1, NT], F32)
    c.sb("oscr2", [1, NT], F32)
    c.sb("st_a", [128, 1], F32)
    c.sb("acc_sq0", [128, 1], F32)
    c.sb("acc_sq1", [128, 1], F32)
    c.sb("st_b", [128, 1], F32)
    c.sb("st_c", [128, 1], F32)
    c.sb("ycT", [128, 8 * D], BF16)
    c.sb("hl", [128, 4 * T], BF16)
    c.sb("hh", [128, 32 * NSEL], BF16)
    c.t["hc_s"] = c.t["acc"]
    for i in range(4):
        c.ps("qb%d" % i)
    for i in range(8):
        c.t["pb%d" % i] = c.t["qb%d" % (i // 2)][:, (i % 2) * 512:(i % 2) * 512 + 512]
    c.sems("dX", "dY", "dW", "dS", "dG", "dH", "dO", "dT", "tn", "ve", "se", "gp")


def alloc_attn(c):
    c.sb("V_t", [128, 16 * 16 * 65], BF16)
    c.sb("opab", [128, T], BF16)
    c.sb("normcols", [128, 16], F32)
    c.t["oscr3"] = c.t["x3t0"]
    c.t["wqkbuf0"] = c.t["hl"][:, 0:2048]
    c.t["wqkbuf1"] = c.t["hl"][:, 2048:4096]
    c.t["vwbuf0"] = c.t["ycT"][:, 4096:8192]
    c.t["vwbuf1"] = c.t["hl"][:, 4096:8192]
    c.sb("bqp_s", [128, 8], F32)
    c.sb("bkp_s", [128, 8], F32)
    c.sb("bvp_s", [1, 1024], BF16)
    c.dr("wqkp", [128, 16384], F32, "ExternalInput")
    c.dr("wvp", [128, 8192], F32, "ExternalInput")
    c.dr("bqp", [128, 8], F32, "ExternalInput")
    c.dr("bkp", [128, 8], F32, "ExternalInput")
    c.dr("bvp", [1, 1024], F32, "ExternalInput")


def phase_attn_consts(c):
    t = c.t
    with c.nc.Block() as blk:
        def g1(g):
            c.dma(g, "dH", t["bqp_s"][:, :], t["bqp"][:, :])
            c.dma(g, "dH", t["bkp_s"][:, :], t["bkp"][:, :])
            c.dma(g, "dH", t["bvp_s"][:, :], t["bvp"][:, :])
            g.wait_ge(c.sem["dH"], c.cnt["dH"])
        blk.gpsimd(g1)



def phase_consts(c, with_ffn=True):
    t = c.t
    with c.nc.Block() as blk:
        @blk.gpsimd
        def _(g):
            g.memset(t["ones128"][:, :], 1.0)
            g.memset(t["ones128b"][:, :], 1.0)
            g.memset(t["zero128"][:, :], 0.0)
            g.memset(t["ln32t"][:, :], LN32)
            g.memset(t["zero1"][:, :], 0.0)
            g.memset(t["negstab"][:, :], -SOFT_STAB)
            g.memset(t["onesrow_f"][:, :], 1.0)
            g.memset(t["onesrow_b"][:, :], 1.0)
            g.memset(t["zrow"][:, :], 0.0)
            if with_ffn:
                c.dma(g, "dH", t["rbc_s"][:, :], t["rbc"][:, :])
                c.dma(g, "dH", t["tri_b"][:, :], t["trif"][:, :])
                c.dma(g, "dH", t["w1l_s"][:, :], t["w1lp"][:, :])
                c.dma(g, "dH", t["w2l_s"][:, :], t["w2lp"][:, :])
                c.dma(g, "dH", t["b1l_s"][:, :], t["b1lp"][:, :])
                c.dma(g, "dH", t["b2l_s"][:, :], t["b2lp"][:, :])
                c.dma(g, "dH", t["b1h_s"][:, :], t["b1hp"][:, :])
                c.dma(g, "dH", t["b2h_s"][:, :], t["b2hp"][:, :])
                c.dma(g, "dH", t["gl_s"][:, :], t["glp"][:, :])
                c.dma(g, "dH", t["gh_s"][:, :], t["ghp"][:, :])
                g.wait_ge(c.sem["dH"], c.cnt["dH"])
                c.dma(g, "dH", t["hcomp_d"][NSEL:NSEL + 1, :], t["zrow"][0:1, :])
                g.wait_ge(c.sem["dH"], c.cnt["dH"])

        if with_ffn:
            ld_all = c.cnt["dH"]

            @blk.scalar
            def _(s):
                wge(s, c, "dH", ld_all)
                for cc in range(8):
                    sl = t["w1l_s"][:, cc * DFFL:(cc + 1) * DFFL]
                    s.activation(sl, sl, AF.Identity, bias=t["zero128"][:, :],
                                 scale=t["gl_s"][:, cc:cc + 1])
                for cc in range(4):
                    sl = t["w2l_s"][:, cc * D:(cc + 1) * D]
                    s.activation(sl, sl, AF.Identity, bias=t["zero128"][:, :],
                                 scale=0.5)
                s.activation(t["b2l_s"][:, :], t["b2l_s"][:, :], AF.Identity,
                             bias=t["zero1"][:, :], scale=0.5)
                s.activation(t["b2h_s"][:, :], t["b2h_s"][:, :], AF.Identity,
                             bias=t["zero1"][:, :], scale=0.5)
                load_act_set(c, s, "natural_log_exp_and_others")

            @blk.vector
            def _(v):
                wge(v, c, "dH", ld_all)
                c.tick("ve", v.reduce_sum(t["st_a"][0:1, 0:1],
                                          t["rbc_s"][0:1, :], axis=AX))

            @blk.tensor
            def _(tn):
                wge(tn, c, "ve", c.cnt["ve"])
                c.tick("tn", tn.matmul(t["pb7"][:, 0:1], t["onesrow_f"][:, :],
                                       t["st_a"][0:1, 0:1], start=True,
                                       stop=True))

            @blk.vector
            def _(v):
                wge(v, c, "tn", c.cnt["tn"])
                c.tick("ve", v.tensor_copy(t["r1bc"][:, :], t["pb7"][:, 0:1]))


def emit_stats_pass2(c, blk, row, mode):
    t = c.t
    r0 = row * NT

    def vp2(v):
        v.tensor_scalar_mul(t["stat_m"][:, r0:r0 + NT],
                            t["stat_sum"][:, r0:r0 + NT], 1.0 / D)
        v.drain()
        v.tensor_mul(t["mtmp"][:, r0:r0 + NT], t["stat_sum"][:, r0:r0 + NT],
                     t["stat_m"][:, r0:r0 + NT])
        v.drain()
        v.tensor_sub(t["stat_ssq"][:, r0:r0 + NT], t["stat_ssq"][:, r0:r0 + NT],
                     t["mtmp"][:, r0:r0 + NT])
        v.drain()
        c.tick("ve", v.tensor_scalar(t["stat_rv"][:, r0:r0 + NT],
                                     t["stat_ssq"][:, r0:r0 + NT],
                                     1.0 / D, EPS_LN, ALU.mult, ALU.add))
    blk.vector(vp2)
    ve_p = c.cnt["ve"]

    def sp2(s, ve_p=ve_p):
        wge(s, c, "ve", ve_p)
        s.activation(t["etile"][:, r0:r0 + NT], t["stat_rv"][:, r0:r0 + NT],
                     AF.Ln, bias=t["zero128"][:, :])
        s.drain()
        last = s.activation(t["stat_rv"][:, r0:r0 + NT],
                            t["etile"][:, r0:r0 + NT], AF.Exp,
                            bias=t["zero128"][:, :], scale=-0.5)
        if mode == "ffn":
            s.activation(t["etile"][:, r0:r0 + NT], t["stat_ssq"][:, r0:r0 + NT],
                         AF.Ln, bias=t["zero128"][:, :])
            s.drain()
            last = s.activation(t["stat_k"][:, r0:r0 + NT],
                                t["etile"][:, r0:r0 + NT], AF.Exp,
                                bias=t["ln32t"][:, :], scale=-0.5)
        c.tick("se", last)
    blk.scalar(sp2)
    se_p = c.cnt["se"]

    def vp3(v, se_p=se_p):
        wge(v, c, "se", se_p)
        kap = t["stat_k"] if mode == "ffn" else t["stat_rv"]
        last = v.scalar_tensor_tensor(t["stat_bb"][:, r0:r0 + NT],
                                      t["stat_m"][:, r0:r0 + NT], -1.0,
                                      kap[:, r0:r0 + NT], ALU.mult, ALU.mult)
        if mode == "ffn":
            v.tensor_scalar(t["mtmp"][:, r0:r0 + NT], t["stat_m"][:, r0:r0 + NT],
                            t["r1bc"][:, :], None, ALU.mult)
            v.drain()
            v.tensor_sub(t["mtmp"][:, r0:r0 + NT], t["xr_t"][:, r0:r0 + NT],
                         t["mtmp"][:, r0:r0 + NT])
            v.drain()
            last = v.tensor_mul(t["sfin"][:, r0:r0 + NT],
                                t["mtmp"][:, r0:r0 + NT],
                                t["stat_rv"][:, r0:r0 + NT])
        c.tick("ve", last)
    blk.vector(vp3)


def emit_apply_y(c, blk, row, src, dst, mode):
    t = c.t
    r0 = row * NT
    kap = t["stat_k"] if mode == "ffn" else t["stat_rv"]
    ld = [0] * NT
    se_yn = [0] * NT
    yn_dma = [0] * NT
    ve_gate = c.cnt["ve"]
    se_gate = c.cnt["se"]
    gp_gate = c.cnt["gp"]
    for tt in range(NT):
        col = r0 + tt
        xb = t["xt%d" % (tt % 3)]
        yb = t["ynt%d" % (tt % 2)]
        rowslice = slice(row * T + tt * 128, row * T + tt * 128 + 128)

        def load(sy, tt=tt, xb=xb, rowslice=rowslice):
            if tt >= 3:
                wge(sy, c, "se", se_yn[tt - 3])
            else:
                wge(sy, c, "ve", ve_gate)
                wge(sy, c, "se", se_gate)
                wge(sy, c, "gp", gp_gate)
            ld[tt] = c.dma(sy, "dX", xb[:, :], src[rowslice, :])
        blk.sync(load)

        def sey(s, tt=tt, col=col, xb=xb, yb=yb):
            wge(s, c, "dX", ld[tt])
            if tt == 0:
                wge(s, c, "ve", ve_gate)
            if tt >= 2:
                wge(s, c, "dY", yn_dma[tt - 2])
            se_yn[tt] = c.tick("se", s.activation(
                yb[:, :], xb[:, :], AF.Identity,
                bias=t["stat_bb"][:, col:col + 1], scale=kap[:, col:col + 1]))
        blk.scalar(sey)

        def gpy(g, tt=tt, yb=yb, rowslice=rowslice):
            wge(g, c, "se", se_yn[tt])
            yn_dma[tt] = c.dma(g, "dY", dst[rowslice, :], yb[:, :])
        blk.gpsimd(gpy)

    def fin(sy):
        wge(sy, c, "dY", c.cnt["dY"])
        wge(sy, c, "dX", c.cnt["dX"])
    blk.sync(fin)


def phase_lnpass(c, row, xsrc, mode):
    t = c.t
    with c.nc.Block() as blk:
        ld = [0] * NT
        ve_done = [0] * NT
        se_acc = [0] * NT
        for tt in range(NT):
            col = row * NT + tt
            xb = t["xt%d" % (tt % 3)]

            def load(sy, tt=tt, xb=xb):
                if tt >= 3:
                    wge(sy, c, "ve", ve_done[tt - 3])
                ld[tt] = c.dma(sy, "dX", xb[:, :],
                               xsrc[(row * T + tt * 128):(row * T + tt * 128 + 128), :])
            blk.sync(load)

            def se1(s, tt=tt, xb=xb):
                wge(s, c, "dX", ld[tt])
                if tt >= 2:
                    wge(s, c, "ve", ve_done[tt - 2])
                se_acc[tt] = c.tick("se", s.activation(
                    t["sq_scr"][:, :], xb[:, :], AF.Square,
                    bias=t["zero128"][:, :],
                    accum_out=t["acc_sq%d" % (tt % 2)][:, :]))
            blk.scalar(se1)

            def ve1(v, tt=tt, col=col, xb=xb):
                wge(v, c, "dX", ld[tt])
                v.reduce_sum(t["stat_sum"][:, col:col + 1], xb[:, :], axis=AX)
                if mode == "ffn":
                    v.scalar_tensor_tensor(t["gp_scr"][:, :], xb[:, :], 1.0,
                                           t["rbc_s"][:, :], ALU.mult, ALU.mult,
                                           accum_out=t["xr_t"][:, col:col + 1])
                wge(v, c, "se", se_acc[tt])
                ve_done[tt] = c.tick("ve", v.tensor_copy(
                    t["stat_ssq"][:, col:col + 1],
                    t["acc_sq%d" % (tt % 2)][:, :]))
            blk.vector(ve1)

        emit_stats_pass2(c, blk, row, mode)
        emit_apply_y(c, blk, row, xsrc, c.t["ynorm_d"], mode)


def phase_coor(c):
    t = c.t
    with c.nc.Block() as blk:
        def init(v):
            c.tick("ve", v.memset(t["sbt"][:, :], 0.0))
        blk.vector(init)

        for it in range(1, 51):
            def se1(s, it=it):
                wge(s, c, "ve", c.cnt["ve"])
                last = None
                for r in range(RPC):
                    last = s.activation(t["etile"][:, r * NT:(r + 1) * NT],
                                        t["sbt"][:, r * NT:(r + 1) * NT], AF.Exp,
                                        bias=t["zero128"][:, :])
                c.tick("se", last)
            blk.scalar(se1)
            se_e = c.cnt["se"]

            def ver(v, se_e=se_e, it=it):
                wge(v, c, "se", se_e)
                if it > 1:
                    wge(v, c, "tn", c.cnt["tn"] - 1)
                last = None
                for r in range(RPC):
                    last = v.reduce_sum(t["rowred"][:, r:r + 1],
                                        t["etile"][:, r * NT:(r + 1) * NT],
                                        axis=AX)
                c.tick("ve", last)
            blk.vector(ver)
            ve_r = c.cnt["ve"]

            def tn1(tn, ve_r=ve_r):
                wge(tn, c, "ve", ve_r)
                c.tick("tn", tn.matmul(t["pb7"][0:1, 0:RPC], t["ones128"][:, :],
                                       t["rowred"][:, 0:RPC], start=True,
                                       stop=True))
            blk.tensor(tn1)
            tn_c = c.cnt["tn"]

            def se2(s, tn_c=tn_c):
                wge(s, c, "tn", tn_c)
                c.tick("se", s.activation(t["na_row"][0:1, 0:RPC],
                                          t["pb7"][0:1, 0:RPC], AF.Ln,
                                          bias=t["zero1"][:, :]))
            blk.scalar(se2)
            se_l = c.cnt["se"]

            def ve1(v, se_l=se_l):
                wge(v, c, "se", se_l)
                c.tick("ve", v.tensor_scalar_add(
                    t["na_row"][0:1, 0:RPC], t["na_row"][0:1, 0:RPC], -LOGK))
            blk.vector(ve1)
            ve_n = c.cnt["ve"]

            def tn2(tn, ve_n=ve_n):
                wge(tn, c, "ve", ve_n)
                c.tick("tn", tn.matmul(t["pb6"][:, 0:RPC], t["onesrow_f"][:, :],
                                       t["na_row"][0:1, 0:RPC], start=True,
                                       stop=True))
            blk.tensor(tn2)
            tn_b = c.cnt["tn"]

            def ve2(v, tn_b=tn_b, it=it):
                wge(v, c, "tn", tn_b)
                v.tensor_copy(t["nab"][:, 0:RPC], t["pb6"][:, 0:RPC])
                v.drain()
                if it < 50:
                    last = None
                    for r in range(RPC):
                        last = v.tensor_scalar_min(
                            t["sbt"][:, r * NT:(r + 1) * NT],
                            t["sfin"][:, r * NT:(r + 1) * NT],
                            t["nab"][:, r:r + 1])
                    c.tick("ve", last)
                else:
                    for r in range(RPC):
                        v.tensor_scalar(
                            t["satf"][:, r * NT:(r + 1) * NT],
                            t["sfin"][:, r * NT:(r + 1) * NT],
                            t["nab"][:, r:r + 1], None, ALU.is_ge)
                    v.drain()
                    c.tick("ve", v.tensor_copy(t["satb"][:, :], t["satf"][:, :]))
            blk.vector(ve2)


def phase_mask(c, row):
    t = c.t
    with c.nc.Block() as blk:
        r0 = row * NT

        def tn1(tn):
            tn.matmul(t["pb0"][:, 0:NT], t["tri_b"][:, :],
                      t["satb"][:, r0:r0 + NT], start=True,
                      stop=False, skip_group_check=True)
            c.tick("tn", tn.matmul(t["pb1"][0:1, 0:NT], t["ones128b"][:, :],
                                   t["satb"][:, r0:r0 + NT], start=True,
                                   stop=True, skip_group_check=True))
        blk.tensor(tn1)
        tn_a = c.cnt["tn"]

        def ve1(v, tn_a=tn_a):
            wge(v, c, "tn", tn_a)
            v.tensor_copy(t["oscr2"][0:1, 0:NT], t["pb1"][0:1, 0:NT])
            v.drain()
            v.tensor_tensor_scan(t["oscr"][0:1, 0:NT], t["oscr2"][0:1, 0:NT],
                                 t["oscr2"][0:1, 0:NT], 0.0, ALU.add,
                                 ALU.bypass)
            v.drain()
            c.tick("ve", v.tensor_sub(t["oscr"][0:1, 0:NT],
                                      t["oscr"][0:1, 0:NT],
                                      t["oscr2"][0:1, 0:NT]))
        blk.vector(ve1)
        ve_o = c.cnt["ve"]

        def tn2(tn, ve_o=ve_o):
            wge(tn, c, "ve", ve_o)
            c.tick("tn", tn.matmul(t["pb0"][:, 0:NT], t["onesrow_f"][:, :],
                                   t["oscr"][0:1, 0:NT], start=False, stop=True,
                                   skip_group_check=True))
        blk.tensor(tn2)
        tn_b = c.cnt["tn"]

        def ve2(v, tn_b=tn_b):
            wge(v, c, "tn", tn_b)
            v.tensor_scalar(t["mtmp"][:, r0:r0 + NT], t["pb0"][:, 0:NT],
                            float(NSEL), None, ALU.is_le)
            v.tensor_scalar_add(t["cumf"][:, r0:r0 + NT], t["pb0"][:, 0:NT],
                                -float(NSEL + 1))
            v.drain()
            v.tensor_mul(t["mtmp"][:, r0:r0 + NT], t["mtmp"][:, r0:r0 + NT],
                         t["satf"][:, r0:r0 + NT])
            v.drain()
            v.tensor_mul(t["cumf"][:, r0:r0 + NT], t["cumf"][:, r0:r0 + NT],
                         t["mtmp"][:, r0:r0 + NT])
            v.drain()
            v.tensor_scalar_add(t["cumf"][:, r0:r0 + NT],
                                t["cumf"][:, r0:r0 + NT], float(NSEL))
            v.drain()
            c.tick("ve", v.tensor_copy(t["cumidx"][:, r0:r0 + NT],
                                       t["cumf"][:, r0:r0 + NT]))
        blk.vector(ve2)


def phase_scatter(c, row):
    t = c.t
    with c.nc.Block() as blk:
        ld = [0] * NT
        sc = [0] * NT
        for tt in range(NT):
            yb = t["ynt%d" % (tt % 2)]

            def load(sy, tt=tt, yb=yb):
                if tt >= 2:
                    wge(sy, c, "dS", sc[tt - 2])
                ld[tt] = c.dma(sy, "dY", yb[:, :], t["ynorm_d"][
                    (row * T + tt * 128):(row * T + tt * 128 + 128), :])
            blk.sync(load)

            def scat(g, tt=tt, yb=yb):
                wge(g, c, "dY", ld[tt])
                sc[tt] = c.idma(g, "dS", t["ycomp_d"][:, :],
                                bass.IndirectOffsetOnAxis(
                                    t["cumidx"][:, row * NT + tt:row * NT + tt + 1], 0),
                                yb[:, :], None)
            blk.gpsimd(scat)

        def fin(sy):
            wge(sy, c, "dS", c.cnt["dS"])
        blk.sync(fin)


def phase_heavy(c, row, sub=99):
    import os
    sub = int(os.environ.get('KHEAVY', '99'))
    t = c.t
    nc = c.nc
    with nc.Block() as blk:
        tl = [[0] * 2 for _ in range(8)]

        def loads(sy):
            for i in range(8):
                for h in range(2):
                    tl[i][h] = c.dma(
                        sy, "dT", t["ycT"][:, i * D + h * 512:i * D + (h + 1) * 512],
                        t["ycomp_d"][h * 512:(h + 1) * 512, i * 128:(i + 1) * 128],
                        transpose=True)
        blk.sync(loads)

        def fold(v):
            for i in range(8):
                for h in range(2):
                    wge(v, c, "dT", tl[i][h])
                    sl = t["ycT"][:, i * D + h * 512:i * D + (h + 1) * 512]
                    last = v.tensor_scalar_mul(sl, sl, t["gh_s"][:, i:i + 1])
            c.tick("ve", last)
        blk.vector(fold)

    if sub < 1:
        return
    with nc.Block() as blk:
        wl = [0] * 32
        mm = [0] * 32
        ev = [0] * 32
        for dd in range(32):
            i = dd % 2
            bk = (dd % 2) * 2

            def load(g, dd=dd, i=i):
                if dd >= 2:
                    wge(g, c, "tn", mm[dd - 2])
                wl[dd] = c.dma(g, "dW", t["wbuf%d" % i][:, :],
                               t["w1hp"][:, dd * D:(dd + 1) * D])
            blk.gpsimd(load)

            def mms(tn, dd=dd, i=i, bk=bk):
                wge(tn, c, "dW", wl[dd])
                if dd >= 2:
                    wge(tn, c, "se", ev[dd - 2])
                last = None
                for h in range(2):
                    for cc in range(8):
                        last = tn.matmul(
                            t["pb%d" % (bk + h)][:, :],
                            t["wbuf%d" % i][:, cc * 128:(cc + 1) * 128],
                            t["ycT"][:, cc * D + h * 512:cc * D + (h + 1) * 512],
                            start=(cc == 0), stop=(cc == 7))
                mm[dd] = c.tick("tn", last)
            blk.tensor(mms)

            def evac(s, dd=dd, bk=bk):
                if dd == 0:
                    load_act_set(c, s, "gelu_and_others")
                wge(s, c, "tn", mm[dd])
                last = None
                for h in range(2):
                    last = s.activation(
                        t["hh"][:, dd * D + h * 512:dd * D + (h + 1) * 512],
                        t["pb%d" % (bk + h)][:, :], AF.Gelu,
                        bias=t["b1h_s"][:, dd:dd + 1])
                ev[dd] = c.tick("se", last)
            blk.scalar(evac)

    if sub < 2:
        return
    with nc.Block() as blk:
        mm2 = []
        st2 = {}
        for of in range(2):
            se_prev_evac = c.cnt["se"]
            wl2 = [0] * 32
            for dd in range(32):
                i = dd % 2
                j = of * 32 + dd

                def load(g, dd=dd, i=i, of=of, j=j):
                    if j >= 2:
                        wge(g, c, "tn", mm2[j - 2])
                    wl2[dd] = c.dma(g, "dW", t["wbuf%d" % i][:, 0:512],
                                    t["w2hp"][:, dd * D + of * 512:dd * D + (of + 1) * 512])
                blk.gpsimd(load)

                def mms(tn, dd=dd, i=i, se_prev_evac=se_prev_evac):
                    wge(tn, c, "dW", wl2[dd])
                    if dd == 0:
                        wge(tn, c, "se", se_prev_evac)
                    last = None
                    for ct in range(8):
                        last = tn.matmul(
                            t["pb%d" % ct][:, :],
                            t["hh"][:, dd * D + ct * 128:dd * D + (ct + 1) * 128],
                            t["wbuf%d" % i][:, 0:512],
                            start=(dd == 0), stop=False,
                            skip_group_check=True)
                    mm2.append(c.tick("tn", last))
                blk.tensor(mms)

            def tn1(tn, of=of):
                last = None
                for ct in range(8):
                    last = tn.matmul(t["pb%d" % ct][:, :], t["onesrow_b"][:, :],
                                     t["b2h_s"][0:1, of * 512:(of + 1) * 512],
                                     start=False, stop=True,
                                     skip_group_check=True)
                c.tick("tn", last)
            blk.tensor(tn1)
            tn_d = c.cnt["tn"]

            for ct in range(8):
                def ev(s, ct=ct, tn_d=tn_d, of=of):
                    wge(s, c, "tn", tn_d)
                    if (of, ct - 2) in st2:
                        wge(s, c, "dH", st2[(of, ct - 2)])
                    c.tick("se", s.activation(
                        t["hc_s"][:, (ct % 2) * 512:(ct % 2) * 512 + 512],
                        t["pb%d" % ct][:, :], AF.Identity,
                        bias=t["zero128"][:, :], scale=0.5))
                blk.scalar(ev)
                se_e = c.cnt["se"]

                def store(g, ct=ct, se_e=se_e, of=of):
                    wge(g, c, "se", se_e)
                    st2[(of, ct)] = c.dma(
                        g, "dH",
                        t["hcomp_d"][ct * 128:(ct + 1) * 128,
                                     of * 512:(of + 1) * 512],
                        t["hc_s"][:, (ct % 2) * 512:(ct % 2) * 512 + 512])
                blk.gpsimd(store)

        def fin(sy):
            wge(sy, c, "dH", c.cnt["dH"])
        blk.sync(fin)


def phase_light_m1(c, row):
    t = c.t
    with c.nc.Block() as blk:
        for tb in range(4):
            tl = [0] * 8

            def loads(sy, tb=tb):
                if tb >= 1:
                    wge(sy, c, "tn", c.cnt["tn"])
                for cc in range(8):
                    tl[cc] = c.dma(
                        sy, "dT", t["ycT"][:, cc * 512:(cc + 1) * 512],
                        t["ynorm_d"][(row * T + tb * 512):(row * T + (tb + 1) * 512),
                                     cc * 128:(cc + 1) * 128],
                        transpose=True)
            blk.sync(loads)

            def mms(tn, tb=tb):
                last = None
                for cc in range(8):
                    wge(tn, c, "dT", tl[cc])
                    for d in range(4):
                        last = tn.matmul(
                            t["pb%d" % (4 + d)][:, :],
                            t["w1l_s"][:, cc * DFFL + d * 128:cc * DFFL + (d + 1) * 128],
                            t["ycT"][:, cc * 512:(cc + 1) * 512],
                            start=(cc == 0), stop=(cc == 7),
                            skip_group_check=True)
                c.tick("tn", last)
            blk.tensor(mms)
            tn_m = c.cnt["tn"]

            def evac(v, tb=tb, tn_m=tn_m):
                wge(v, c, "tn", tn_m)
                last = None
                for d in range(4):
                    last = v.tensor_copy(
                        t["hl"][:, d * T + tb * 512:d * T + (tb + 1) * 512],
                        t["pb%d" % (4 + d)][:, :])
                c.tick("ve", last)
            blk.vector(evac)

        def gelu(s):
            wge(s, c, "ve", c.cnt["ve"])
            last = None
            for d in range(4):
                last = s.activation(t["hl"][:, d * T:(d + 1) * T],
                                    t["hl"][:, d * T:(d + 1) * T], AF.Gelu,
                                    bias=t["b1l_s"][:, d:d + 1])
            c.tick("se", last)
        blk.scalar(gelu)


def phase_light2_add(c, row, xsrc, xdst, mode, ydst=None):
    t = c.t
    with c.nc.Block() as blk:
        ld = [0] * NT
        gl = [0] * NT
        mm = [0] * NT
        ve_acc = [0] * NT
        gp_x3 = [0] * NT
        st_x = [0] * NT
        st_done = [0] * NT
        for tt in range(NT):
            col = row * NT + tt
            xb = t["xt%d" % (tt % 3)]
            x3 = t["x3t%d" % (tt % 2)]
            hgb = t["hg%d" % (tt % 2)]
            bk = (tt % 2) * 2
            rowslice = slice(row * T + tt * 128, row * T + tt * 128 + 128)

            def load(sy, tt=tt, xb=xb, rowslice=rowslice):
                if tt >= 3:
                    wge(sy, c, "gp", gp_x3[tt - 3])
                ld[tt] = c.dma(sy, "dX", xb[:, :], xsrc[rowslice, :])
            blk.sync(load)

            def gath(g, tt=tt, hgb=hgb, col=col):
                if tt >= 2:
                    wge(g, c, "ve", ve_acc[tt - 2])
                gl[tt] = c.idma(g, "dG", hgb[:, :], None, t["hcomp_d"][:, :],
                                bass.IndirectOffsetOnAxis(
                                    t["cumidx"][:, col:col + 1], 0))
            blk.gpsimd(gath)

            def mms(tn, tt=tt, bk=bk):
                if tt >= 2:
                    wge(tn, c, "ve", ve_acc[tt - 2])
                last = None
                for of in range(2):
                    for d in range(4):
                        tn.matmul(t["pb%d" % (bk + of)][:, :],
                                  t["hl"][:, d * T + tt * 128:d * T + (tt + 1) * 128],
                                  t["w2l_s"][:, d * D + of * 512:d * D + (of + 1) * 512],
                                  start=(d == 0), stop=False,
                                  skip_group_check=True)
                    last = tn.matmul(t["pb%d" % (bk + of)][:, :],
                                     t["onesrow_b"][:, :],
                                     t["b2l_s"][0:1, of * 512:(of + 1) * 512],
                                     start=False, stop=True,
                                     skip_group_check=True)
                mm[tt] = c.tick("tn", last)
            blk.tensor(mms)

            def veacc(v, tt=tt, bk=bk, hgb=hgb):
                wge(v, c, "tn", mm[tt])
                wge(v, c, "dG", gl[tt])
                if tt >= 1:
                    wge(v, c, "gp", gp_x3[tt - 1])
                v.tensor_add(t["acc"][:, 0:512], t["pb%d" % bk][:, :],
                             hgb[:, 0:512])
                ve_acc[tt] = c.tick("ve", v.tensor_add(
                    t["acc"][:, 512:1024], t["pb%d" % (bk + 1)][:, :],
                    hgb[:, 512:1024]))
            blk.vector(veacc)

            def gpadd(g, tt=tt, xb=xb, x3=x3):
                wge(g, c, "ve", ve_acc[tt])
                wge(g, c, "dX", ld[tt])
                if tt >= 2:
                    wge(g, c, "dO", st_x[tt - 2])
                    if mode != "none":
                        wge(g, c, "ve", st_done[tt - 2])
                gp_x3[tt] = c.tick("gp", g.tensor_add(x3[:, :], xb[:, :],
                                                      t["acc"][:, :]))
            blk.gpsimd(gpadd)

            def store(sy, tt=tt, x3=x3, rowslice=rowslice):
                wge(sy, c, "gp", gp_x3[tt])
                st_x[tt] = c.dma(sy, "dO", xdst[rowslice, :], x3[:, :])
            blk.sync(store)

            if mode != "none":
                def se1(s, tt=tt, x3=x3):
                    if tt == 0:
                        load_act_set(c, s, "natural_log_exp_and_others")
                    wge(s, c, "gp", gp_x3[tt])
                    if tt >= 2:
                        wge(s, c, "ve", st_done[tt - 2])
                    c.tick("se", s.activation(
                        t["sq_scr"][:, :], x3[:, :], AF.Square,
                        bias=t["zero128"][:, :],
                        accum_out=t["acc_sq%d" % (tt % 2)][:, :]))
                blk.scalar(se1)
                se_acc = c.cnt["se"]

                def ve1(v, tt=tt, col=col, x3=x3, se_acc=se_acc):
                    wge(v, c, "gp", gp_x3[tt])
                    v.reduce_sum(t["stat_sum"][:, col:col + 1], x3[:, :], axis=AX)
                    if mode == "ffn":
                        v.scalar_tensor_tensor(t["gp_scr"][:, :], x3[:, :], 1.0,
                                               t["rbc_s"][:, :], ALU.mult,
                                               ALU.mult,
                                               accum_out=t["xr_t"][:, col:col + 1])
                    wge(v, c, "se", se_acc)
                    st_done[tt] = c.tick("ve", v.tensor_copy(
                        t["stat_ssq"][:, col:col + 1],
                        t["acc_sq%d" % (tt % 2)][:, :]))
                blk.vector(ve1)

        if mode != "none":
            emit_stats_pass2(c, blk, row, mode)
            emit_apply_y(c, blk, row, xdst,
                         ydst if ydst is not None else t["ynorm_d"], mode)
        else:
            def fin(sy):
                wge(sy, c, "dO", c.cnt["dO"])
                wge(sy, c, "dG", c.cnt["dG"])
            blk.sync(fin)


def phase_qkv(c, row, y2_d):
    t = c.t
    Q0, K0 = 0, 8 * T
    with c.nc.Block() as blk:
        def vinit(g):
            V3 = t["V_t"][:, :].rearrange("p (a h e) -> p a h e", a=16, h=16, e=65)
            c.tick("gp", g.memset(V3[:, :, :, 64:65], 1.0))
        blk.gpsimd(vinit)
        gp_init = c.cnt["gp"]

        mmg = [0] * 4
        grpw = [0] * 4
        qk_mm = {}
        for tb in range(4):
            tl = [0] * 8

            def loads(sy, tb=tb, tl=tl, tn_gate=c.cnt["tn"]):
                if tb >= 1:
                    wge(sy, c, "tn", tn_gate)
                for cc in range(8):
                    tl[cc] = c.dma(
                        sy, "dT", t["ycT"][:, cc * 512:(cc + 1) * 512],
                        y2_d[(row * T + tb * 512):(row * T + (tb + 1) * 512),
                             cc * 128:(cc + 1) * 128],
                        transpose=True)
            blk.sync(loads)

            wld = [0] * 10
            for p in range(10):
                def wload(g, p=p, tb=tb, wld=wld):
                    gi = tb * 10 + p
                    if gi >= 2:
                        wge(g, c, "tn", qk_mm.get(gi - 2, 0))
                    if p < 8:
                        wld[p] = c.dma(g, "dW",
                                       t["wqkbuf%d" % (p % 2)][:, :],
                                       t["wqkp"][:, p * 2048:(p + 1) * 2048])
                    else:
                        g8 = p - 8
                        wld[p] = c.dma(g, "dW", t["vwbuf%d" % (g8 % 2)][:, :],
                                       t["wvp"][:, g8 * 4096:(g8 + 1) * 4096])
                blk.gpsimd(wload)

            grp = 0
            for p in range(8):
                wb = t["wqkbuf%d" % (p % 2)]
                for ki, kind in enumerate(("q", "k")):
                    bsb = t["bqp_s"] if kind == "q" else t["bkp_s"]
                    base = Q0 if kind == "q" else K0
                    bk = grp % 4
                    grp += 1

                    def mms(tn, p=p, ki=ki, bk=bk, wb=wb, tb=tb, tl=tl):
                        wge(tn, c, "dW", wld[p])
                        for cc in range(8):
                            wge(tn, c, "dT", tl[cc])
                        wge(tn, c, "se", grpw[bk])
                        last = None
                        for cc in range(8):
                            last = tn.matmul(
                                t["pb%d" % bk][:, :],
                                wb[:, ki * 1024 + cc * 128:ki * 1024 + (cc + 1) * 128],
                                t["ycT"][:, cc * 512:(cc + 1) * 512],
                                start=(cc == 0), stop=(cc == 7),
                                skip_group_check=True)
                        mmg[bk] = c.tick("tn", last)
                        if ki == 1:
                            qk_mm[tb * 10 + p] = mmg[bk]
                    blk.tensor(mms)

                    def evac(s, p=p, bk=bk, bsb=bsb, base=base, tb=tb):
                        wge(s, c, "tn", mmg[bk])
                        grpw[bk] = c.tick("se", s.activation(
                            t["hh"][:, base + p * T + tb * 512:
                                    base + p * T + (tb + 1) * 512],
                            t["pb%d" % bk][:, :], AF.Identity,
                            bias=bsb[:, p:p + 1]))
                    blk.scalar(evac)

            for g8 in range(2):
                for sub in range(4):
                    tkt = tb * 4 + sub
                    bk = grp % 4
                    grp += 1

                    def mms(tn, sub=sub, g8=g8, bk=bk, tb=tb, tl=tl):
                        wge(tn, c, "dW", wld[8 + g8])
                        wge(tn, c, "se", grpw[bk])
                        if tb == 0 and sub == 0 and g8 == 0:
                            wge(tn, c, "gp", gp_init)
                        for cc in range(8):
                            tn.matmul(
                                t["pb%d" % bk][:, :],
                                t["ycT"][:, cc * 512 + sub * 128:cc * 512 + (sub + 1) * 128],
                                t["vwbuf%d" % (g8 % 2)][:, cc * 512:(cc + 1) * 512],
                                start=(cc == 0), stop=False,
                                skip_group_check=True)
                        last = tn.matmul(t["pb%d" % bk][:, :], t["onesrow_b"][:, :],
                                         t["bvp_s"][0:1, g8 * 512:(g8 + 1) * 512],
                                         start=False, stop=True,
                                         skip_group_check=True)
                        mmg[bk] = c.tick("tn", last)
                        if sub == 3:
                            qk_mm[tb * 10 + 8 + g8] = mmg[bk]
                    blk.tensor(mms)

                    def evac(s, tkt=tkt, g8=g8, bk=bk):
                        wge(s, c, "tn", mmg[bk])
                        V3 = t["V_t"][:, :].rearrange("p (a h e) -> p a h e",
                                                      a=16, h=16, e=65)
                        grpw[bk] = c.tick("se", s.activation(
                            V3[:, tkt, g8 * 8:(g8 + 1) * 8, 0:64],
                            t["pb%d" % bk][:, :], AF.Identity,
                            bias=t["zero128"][:, :]))
                    blk.scalar(evac)

        def fin(sy):
            wge(sy, c, "se", c.cnt["se"])
            wge(sy, c, "dW", c.cnt["dW"])
            wge(sy, c, "dT", c.cnt["dT"])
        blk.sync(fin)


def phase_sav(c, row, oprime):
    t = c.t
    Q0, K0 = 0, 8 * T
    with c.nc.Block() as blk:
        if row == 0:
            def ninit(v):
                c.tick("ve", v.memset(t["normcols"][:, :], 0.0))
            blk.vector(ninit)

        av_gate = [c.cnt["ve"]]
        bc_gate = [c.cnt["se"]]
        for p in range(8):
            op_dma_gate = []
            sticks = [0] * 64
            eticks = [0] * 64
            avt = [0] * 64

            def emit_smm(k):
                qb, tkc = k // 16, k % 16
                par = k % 2

                def smm(tn, p=p, qb=qb, tkc=tkc, k=k, par=par):
                    if k >= 2:
                        wge(tn, c, "se", eticks[k - 2])
                    tn.matmul(t["qb%d" % par][:, 0:512],
                              t["hh"][0:64, K0 + p * T + tkc * 128:
                                      K0 + p * T + (tkc + 1) * 128],
                              t["hh"][0:64, Q0 + p * T + qb * 512:
                                      Q0 + p * T + (qb + 1) * 512],
                              start=True, stop=True, tile_position=(0, 0),
                              skip_group_check=True)
                    sticks[k] = c.tick("tn", tn.matmul(
                        t["qb%d" % par][:, 512:1024],
                        t["hh"][64:128, K0 + p * T + tkc * 128:
                                K0 + p * T + (tkc + 1) * 128],
                        t["hh"][64:128, Q0 + p * T + qb * 512:
                                Q0 + p * T + (qb + 1) * 512],
                        start=True, stop=True, tile_position=(64, 0),
                        skip_group_check=True))
                blk.tensor(smm)

                def sexp(s, k=k, par=par):
                    wge(s, c, "tn", sticks[k])
                    if k >= 2:
                        wge(s, c, "tn", avt[k - 2])
                    eticks[k] = c.tick("se", s.activation(
                        t["hl"][:, par * 1024:(par + 1) * 1024],
                        t["qb%d" % par][:, 0:1024], AF.Exp,
                        scale=0.125, bias=t["negstab"][:, :]))
                blk.scalar(sexp)

            def emit_avmm(k):
                qb, tkc = k // 16, k % 16
                par = k % 2

                def avmm(tn, p=p, tkc=tkc, k=k, par=par):
                    wge(tn, c, "se", eticks[k])
                    if tkc == 0:
                        wge(tn, c, "ve", av_gate[0])
                    hA, hB = 2 * p, 2 * p + 1
                    tn.matmul(t["qb2"][0:65, 0:512],
                              t["V_t"][:, tkc * 1040 + hA * 65:
                                       tkc * 1040 + hA * 65 + 65],
                              t["hl"][:, par * 1024:par * 1024 + 512],
                              start=(tkc == 0), stop=(tkc == 15),
                              skip_group_check=True)
                    avt[k] = c.tick("tn", tn.matmul(
                        t["qb2"][0:65, 512:1024],
                        t["V_t"][:, tkc * 1040 + hB * 65:
                                 tkc * 1040 + hB * 65 + 65],
                        t["hl"][:, par * 1024 + 512:(par + 1) * 1024],
                        start=(tkc == 0), stop=(tkc == 15),
                        skip_group_check=True))
                blk.tensor(avmm)

            def emit_epilogue(qb):
                k15 = qb * 16 + 15

                def ep_ln(s, qb=qb, k15=k15):
                    wge(s, c, "tn", avt[k15])
                    c.tick("se", s.activation(t["oscr3"][0:1, 0:1024],
                                              t["qb2"][64:65, 0:1024], AF.Ln,
                                              bias=t["zero1"][:, :]))
                blk.scalar(ep_ln)
                se_l = c.cnt["se"]

                def ep_tn(tn, se_l=se_l, bcg=bc_gate[0]):
                    wge(tn, c, "se", se_l)
                    wge(tn, c, "se", bcg)
                    tn.matmul(t["qb3"][0:64, 0:512], t["onesrow_f"][0:1, 0:64],
                              t["oscr3"][0:1, 0:512], start=True, stop=True,
                              skip_group_check=True)
                    c.tick("tn", tn.matmul(t["qb3"][0:64, 512:1024],
                                           t["onesrow_f"][0:1, 0:64],
                                           t["oscr3"][0:1, 512:1024],
                                           start=True, stop=True,
                                           skip_group_check=True))
                blk.tensor(ep_tn)
                tn_b = c.cnt["tn"]

                def ep_exp(s, tn_b=tn_b):
                    wge(s, c, "tn", tn_b)
                    c.tick("se", s.activation(t["acc"][0:64, 0:1024],
                                              t["qb3"][0:64, 0:1024], AF.Exp,
                                              bias=t["zero128"][0:64, :],
                                              scale=-1.0))
                blk.scalar(ep_exp)
                se_c = c.cnt["se"]

                def ep_ve2(v, qb=qb, se_c=se_c, dO_gate=c.cnt["dO"]):
                    wge(v, c, "se", se_c)
                    if qb == 0:
                        wge(v, c, "dO", dO_gate)
                    v.tensor_mul(t["opab"][0:64, qb * 512:(qb + 1) * 512],
                                 t["qb2"][0:64, 0:512], t["acc"][0:64, 0:512])
                    av_gate[0] = c.tick("ve", v.tensor_mul(
                        t["opab"][64:128, qb * 512:(qb + 1) * 512],
                        t["qb2"][0:64, 512:1024], t["acc"][0:64, 512:1024]))
                blk.vector(ep_ve2)
                ve_m = c.cnt["ve"]
                op_dma_gate.append(ve_m)

                def ep_sq(s, qb=qb, ve_m=ve_m):
                    wge(s, c, "ve", ve_m)
                    s.activation(t["ynt0"][0:64, 0:512],
                                 t["opab"][0:64, qb * 512:(qb + 1) * 512],
                                 AF.Square, bias=t["zero128"][0:64, :])
                    c.tick("se", s.activation(
                        t["ynt0"][0:64, 512:1024],
                        t["opab"][64:128, qb * 512:(qb + 1) * 512],
                        AF.Square, bias=t["zero128"][0:64, :]))
                blk.scalar(ep_sq)
                se_q = c.cnt["se"]

                def ep_nsq(tn, se_q=se_q):
                    wge(tn, c, "se", se_q)
                    last = None
                    for sub in range(8):
                        last = tn.matmul(t["qb3"][:, 1016 + sub:1017 + sub],
                                         t["ynt0"][0:64, sub * 128:(sub + 1) * 128],
                                         t["ones128b"][0:64, 0:1],
                                         start=True, stop=True,
                                         skip_group_check=True)
                    c.tick("tn", last)
                blk.tensor(ep_nsq)
                tn_n = c.cnt["tn"]

                def ep_nrm(s, tn_n=tn_n):
                    wge(s, c, "tn", tn_n)
                    s.activation(t["sq_scr"][:, 0:8], t["qb3"][:, 1016:1024],
                                 AF.Ln, bias=t["zero128"][:, :])
                    s.drain()
                    bc_gate[0] = c.tick("se", s.activation(
                        t["sq_scr"][:, 8:16], t["sq_scr"][:, 0:8], AF.Exp,
                        bias=t["zero128"][:, :], scale=0.5))
                blk.scalar(ep_nrm)
                se_n = c.cnt["se"]

                def ep_acc(v, p=p, se_n=se_n):
                    wge(v, c, "se", se_n)
                    v.reduce_sum(t["st_a"][:, :], t["sq_scr"][:, 8:12], axis=AX)
                    v.reduce_sum(t["st_b"][:, :], t["sq_scr"][:, 12:16], axis=AX)
                    v.drain()
                    v.tensor_add(t["normcols"][:, 2 * p:2 * p + 1],
                                 t["normcols"][:, 2 * p:2 * p + 1], t["st_a"][:, :])
                    v.drain()
                    c.tick("ve", v.tensor_add(
                        t["normcols"][:, 2 * p + 1:2 * p + 2],
                        t["normcols"][:, 2 * p + 1:2 * p + 2], t["st_b"][:, :]))
                blk.vector(ep_acc)

            emit_smm(0)
            for k in range(1, 64):
                emit_smm(k)
                emit_avmm(k - 1)
                if (k - 1) % 16 == 15:
                    emit_epilogue((k - 1) // 16)
            emit_avmm(63)
            emit_epilogue(3)

            def opdma(g, p=p, row=row, gates=tuple(op_dma_gate)):
                for gv in gates:
                    wge(g, c, "ve", gv)
                base = (row * 8 + p) * 128
                c.dma(g, "dO", oprime[base:base + 64, :], t["opab"][0:64, :])
                c.dma(g, "dO", oprime[base + 64:base + 128, :],
                      t["opab"][64:128, :])
            blk.gpsimd(opdma)

        def fin(sy):
            wge(sy, c, "dO", c.cnt["dO"])
        blk.sync(fin)


def phase_normfin(c, normpart):
    t = c.t
    with c.nc.Block() as blk:
        def tn1(tn):
            c.tick("tn", tn.matmul(t["qb0"][0:16, 0:1], t["normcols"][:, :],
                                   t["ones128"][:, :], start=True, stop=True,
                                   skip_group_check=True))
        blk.tensor(tn1)
        tn_f = c.cnt["tn"]

        def ve1(v, tn_f=tn_f):
            wge(v, c, "tn", tn_f)
            c.tick("ve", v.tensor_copy(t["st_c"][0:16, 0:1], t["qb0"][0:16, 0:1]))
        blk.vector(ve1)
        ve_f = c.cnt["ve"]

        def gp1(g, ve_f=ve_f):
            wge(g, c, "ve", ve_f)
            c.dma(g, "dO", normpart[:, :], t["st_c"][0:16, 0:1])
            wge(g, c, "dO", c.cnt["dO"])
        blk.gpsimd(gp1)


def alloc_l2(c):
    c.sb("an_sb", [8, 16], F32)
    c.sb("gt_sb", [16, 1], F32)
    c.sb("bo_sb", [16, D], F32)
    c.sb("g16", [16, 1], F32)
    c.sb("clip16", [16, 1], F32)
    c.sb("gvec", [128, 8], F32)
    c.sb("gfe", [1, 8], F32)
    c.sb("gfo", [1, 8], F32)
    c.sb("bconst_b", [1, D], BF16)
    c.sb("elo_s", [1, 128], F32)
    c.sb("ehi_s", [1, 128], F32)
    c.dr("opri", [16 * 128, T], BF16, "ExternalInput")
    c.dr("allnorm", [8, 16], F32, "ExternalInput")
    c.dr("wop", [128, 8192], F32, "ExternalInput")
    c.dr("bop", [16, D], F32, "ExternalInput")
    c.dr("gatep", [16, 1], F32, "ExternalInput")
    c.dr("elo", [1, 128], F32, "ExternalInput")
    c.dr("ehi", [1, 128], F32, "ExternalInput")
    c.dr("gd", [16, 1], F32)


def phase_l2prep(c):
    t = c.t
    with c.nc.Block() as blk:
        def g1(g):
            c.dma(g, "dH", t["an_sb"][:, :], t["allnorm"][:, :])
            c.dma(g, "dH", t["gt_sb"][:, :], t["gatep"][:, :])
            c.dma(g, "dH", t["bo_sb"][:, :], t["bop"][:, :])
            c.dma(g, "dH", t["elo_s"][:, :], t["elo"][:, :])
            c.dma(g, "dH", t["ehi_s"][:, :], t["ehi"][:, :])
            for i in range(8):
                c.dma(g, "dH", t["ycT"][:, i * 1024:(i + 1) * 1024],
                      t["wop"][:, i * 1024:(i + 1) * 1024])
            g.wait_ge(c.sem["dH"], c.cnt["dH"])
        blk.gpsimd(g1)
        ld = c.cnt["dH"]

        def tn1(tn, ld=ld):
            wge(tn, c, "dH", ld)
            c.tick("tn", tn.matmul(t["qb0"][0:16, 0:1], t["an_sb"][:, :],
                                   t["ones128"][0:8, 0:1], start=True,
                                   stop=True, skip_group_check=True))
        blk.tensor(tn1)
        tn_s = c.cnt["tn"]

        def ve1(v, tn_s=tn_s, ld=ld):
            wge(v, c, "tn", tn_s)
            wge(v, c, "dH", ld)
            v.tensor_copy(t["st_a"][0:16, 0:1], t["qb0"][0:16, 0:1])
            v.drain()
            v.tensor_scalar(t["st_a"][0:16, 0:1], t["st_a"][0:16, 0:1],
                            1.0 / (B_TOT * T), 1e-5, ALU.mult, ALU.max)
            v.drain()
            v.reciprocal(t["st_b"][0:16, 0:1], t["st_a"][0:16, 0:1])
            v.tensor_scalar(t["clip16"][0:16, 0:1], t["gt_sb"][:, :],
                            0.0, 1.0, ALU.max, ALU.min)
            v.drain()
            v.scalar_tensor_tensor(t["g16"][0:16, 0:1], t["clip16"][0:16, 0:1],
                                   1.0 / 16.0, t["st_b"][0:16, 0:1],
                                   ALU.mult, ALU.mult)
            v.drain()
            c.tick("ve", v.tensor_scalar(t["bo_sb"][:, :], t["bo_sb"][:, :],
                                         t["clip16"][:, :], 1.0 / 16.0,
                                         ALU.mult, ALU.mult))
        blk.vector(ve1)
        ve_g = c.cnt["ve"]

        def g2(g, ve_g=ve_g):
            wge(g, c, "ve", ve_g)
            c.dma(g, "dH", t["gd"][:, :], t["g16"][0:16, 0:1])
            g.wait_ge(c.sem["dH"], c.cnt["dH"])
            g.dma_start(t["gfe"][0:1, :], t["gd"][0:16:2, 0:1],
                        allow_slow_non_contiguous=True).then_inc(c.sem["dH"], 16)
            c.cnt["dH"] += 16
            g.dma_start(t["gfo"][0:1, :], t["gd"][1:16:2, 0:1],
                        allow_slow_non_contiguous=True).then_inc(c.sem["dH"], 16)
            c.cnt["dH"] += 16
            g.wait_ge(c.sem["dH"], c.cnt["dH"])
        blk.gpsimd(g2)
        ld2 = c.cnt["dH"]

        def tn2(tn, ld2=ld2):
            wge(tn, c, "dH", ld2)
            tn.matmul(t["qb1"][:, 0:8], t["elo_s"][:, :], t["gfe"][:, :],
                      start=True, stop=False, skip_group_check=True)
            tn.matmul(t["qb1"][:, 0:8], t["ehi_s"][:, :], t["gfo"][:, :],
                      start=False, stop=True, skip_group_check=True)
            tn.matmul(t["qb2"][0:1, 0:512], t["ones128"][0:16, 0:1],
                      t["bo_sb"][:, 0:512], start=True, stop=True,
                      skip_group_check=True)
            c.tick("tn", tn.matmul(t["qb2"][0:1, 512:1024],
                                   t["ones128"][0:16, 0:1],
                                   t["bo_sb"][:, 512:1024], start=True,
                                   stop=True, skip_group_check=True))
        blk.tensor(tn2)
        tn_g = c.cnt["tn"]

        def ve2(v, tn_g=tn_g):
            wge(v, c, "tn", tn_g)
            v.tensor_copy(t["gvec"][:, :], t["qb1"][:, 0:8])
            c.tick("ve", v.tensor_copy(t["bconst_b"][0:1, :],
                                       t["qb2"][0:1, 0:1024]))
        blk.vector(ve2)
        ve_c = c.cnt["ve"]

        def se1(s, ve_c=ve_c):
            wge(s, c, "ve", ve_c)
            last = None
            for i in range(8):
                sl = t["ycT"][:, i * 1024:(i + 1) * 1024]
                last = s.activation(sl, sl, AF.Identity,
                                    bias=t["zero128"][:, :],
                                    scale=t["gvec"][:, i:i + 1])
            c.tick("se", last)
        blk.scalar(se1)


def phase_oproj_add(c, row, xsrc, xdst, ydst):
    t = c.t
    with c.nc.Block() as blk:
        def opload(g):
            for p in range(8):
                c.dma(g, "dW", t["hh"][:, p * T:(p + 1) * T],
                      t["opri"][(row * 8 + p) * 128:(row * 8 + p) * 128 + 128, :])
            g.wait_ge(c.sem["dW"], c.cnt["dW"])
        blk.gpsimd(opload)
        opld = c.cnt["dW"]

        ld = [0] * NT
        mm = [0] * NT
        ve_x3 = [0] * NT
        st_x = [0] * NT
        st_done = [0] * NT
        for tt in range(NT):
            col = row * NT + tt
            xb = t["xt%d" % (tt % 3)]
            x3 = t["x3t%d" % (tt % 2)]
            bk = (tt % 2) * 2
            rowslice = slice(row * T + tt * 128, row * T + tt * 128 + 128)

            def load(sy, tt=tt, xb=xb, rowslice=rowslice):
                if tt >= 3:
                    wge(sy, c, "ve", ve_x3[tt - 3])
                ld[tt] = c.dma(sy, "dX", xb[:, :], xsrc[rowslice, :])
            blk.sync(load)

            def mms(tn, tt=tt, bk=bk):
                if tt == 0:
                    wge(tn, c, "dW", opld)
                if tt >= 2:
                    wge(tn, c, "ve", ve_x3[tt - 2])
                last = None
                for of in range(2):
                    for p in range(8):
                        tn.matmul(t["pb%d" % (bk + of)][:, :],
                                  t["hh"][:, p * T + tt * 128:p * T + (tt + 1) * 128],
                                  t["ycT"][:, p * 1024 + of * 512:p * 1024 + (of + 1) * 512],
                                  start=(p == 0), stop=False,
                                  skip_group_check=True)
                    last = tn.matmul(t["pb%d" % (bk + of)][:, :],
                                     t["onesrow_b"][:, :],
                                     t["bconst_b"][0:1, of * 512:(of + 1) * 512],
                                     start=False, stop=True,
                                     skip_group_check=True)
                mm[tt] = c.tick("tn", last)
            blk.tensor(mms)

            def vex3(v, tt=tt, bk=bk, xb=xb, x3=x3):
                wge(v, c, "tn", mm[tt])
                wge(v, c, "dX", ld[tt])
                if tt >= 2:
                    wge(v, c, "dO", st_x[tt - 2])
                    wge(v, c, "ve", st_done[tt - 2])
                v.tensor_add(x3[:, 0:512], t["pb%d" % bk][:, :], xb[:, 0:512])
                ve_x3[tt] = c.tick("ve", v.tensor_add(
                    x3[:, 512:1024], t["pb%d" % (bk + 1)][:, :],
                    xb[:, 512:1024]))
            blk.vector(vex3)

            def store(sy, tt=tt, x3=x3, rowslice=rowslice):
                wge(sy, c, "ve", ve_x3[tt])
                st_x[tt] = c.dma(sy, "dO", xdst[rowslice, :], x3[:, :])
            blk.sync(store)

            def se1(s, tt=tt, x3=x3):
                wge(s, c, "ve", ve_x3[tt])
                if tt >= 2:
                    wge(s, c, "ve", st_done[tt - 2])
                c.tick("se", s.activation(
                    t["sq_scr"][:, :], x3[:, :], AF.Square,
                    bias=t["zero128"][:, :],
                    accum_out=t["acc_sq%d" % (tt % 2)][:, :]))
            blk.scalar(se1)
            se_acc = c.cnt["se"]

            def ve1(v, tt=tt, col=col, x3=x3, se_acc=se_acc):
                v.reduce_sum(t["stat_sum"][:, col:col + 1], x3[:, :], axis=AX)
                v.scalar_tensor_tensor(t["gp_scr"][:, :], x3[:, :], 1.0,
                                       t["rbc_s"][:, :], ALU.mult, ALU.mult,
                                       accum_out=t["xr_t"][:, col:col + 1])
                wge(v, c, "se", se_acc)
                st_done[tt] = c.tick("ve", v.tensor_copy(
                    t["stat_ssq"][:, col:col + 1],
                    t["acc_sq%d" % (tt % 2)][:, :]))
            blk.vector(ve1)

        emit_stats_pass2(c, blk, row, "ffn")
        emit_apply_y(c, blk, row, xdst, ydst, "ffn")



def build_l1():
    nc = bass.Bass("TRN2", target_bir_lowering=False)
    c = Ctx(nc)
    xin = c.dr("xin", [RPC * T, D], F32, "ExternalInput")
    x2o = c.dr("x2o", [RPC * T, D], F32, "ExternalOutput")
    oprime = c.dr("oprime", [16 * 128, T], BF16, "ExternalOutput")
    normpart = c.dr("normpart", [16, 1], F32, "ExternalOutput")
    y2_d = c.dr("y2_d", [RPC * T, D], BF16)
    alloc_common(c)
    alloc_attn(c)
    phase_consts(c)
    phase_attn_consts(c)
    for r in range(RPC):
        phase_lnpass(c, r, xin, "ffn")
    phase_coor(c)
    for r in range(RPC):
        phase_mask(c, r)
    for r in range(RPC):
        phase_scatter(c, r)
        phase_heavy(c, r)
        phase_light_m1(c, r)
        phase_light2_add(c, r, xin, x2o, "attn", ydst=y2_d)
    for r in range(RPC):
        phase_qkv(c, r, y2_d)
        phase_sav(c, r, oprime)
    phase_normfin(c, normpart)
    return nc


def build_l2():
    nc = bass.Bass("TRN2", target_bir_lowering=False)
    c = Ctx(nc)
    x2i = c.dr("x2i", [RPC * T, D], F32, "ExternalInput")
    xout = c.dr("xout", [RPC * T, D], F32, "ExternalOutput")
    alloc_common(c)
    alloc_l2(c)
    phase_consts(c)
    phase_l2prep(c)
    for r in range(RPC):
        phase_oproj_add(c, r, x2i, xout, c.t["ynorm_d"])
    phase_coor(c)
    for r in range(RPC):
        phase_mask(c, r)
    for r in range(RPC):
        phase_scatter(c, r)
        phase_heavy(c, r)
        phase_light_m1(c, r)
        phase_light2_add(c, r, xout, xout, "none")
    return nc



def ffn_host_inputs(inputs):
    w1l = np.asarray(inputs["L0_ffn_w1_l"], np.float32)
    w2l = np.asarray(inputs["L0_ffn_w2_l"], np.float32)
    w1h = np.asarray(inputs["L0_ffn_w1_h"], np.float32)
    w2h = np.asarray(inputs["L0_ffn_w2_h"], np.float32)
    return {
        "w1lp": np.ascontiguousarray(
            w1l.T.reshape(8, 128, DFFL).transpose(1, 0, 2).reshape(128, 8 * DFFL)),
        "w2lp": np.ascontiguousarray(
            w2l.T.reshape(4, 128, D).transpose(1, 0, 2).reshape(128, 4 * D)),
        "b1lp": np.ascontiguousarray(
            np.asarray(inputs["L0_ffn_b1_l"], np.float32).reshape(4, 128).T),
        "b2lp": np.asarray(inputs["L0_ffn_b2_l"], np.float32).reshape(1, D).copy(),
        "w1hp": np.ascontiguousarray(
            w1h.T.reshape(8, 128, 32, 128).transpose(1, 2, 0, 3).reshape(128, 32 * D)),
        "w2hp": np.ascontiguousarray(
            w2h.T.reshape(32, 128, 2, 512).transpose(1, 0, 2, 3).reshape(128, 32 * D)),
        "b1hp": np.ascontiguousarray(
            np.asarray(inputs["L0_ffn_b1_h"], np.float32).reshape(32, 128).T),
        "b2hp": np.asarray(inputs["L0_ffn_b2_h"], np.float32).reshape(1, D).copy(),
        "glp": np.ascontiguousarray(
            np.asarray(inputs["L0_ffn_gamma_l"], np.float32).reshape(8, 128).T),
        "ghp": np.ascontiguousarray(
            np.asarray(inputs["L0_ffn_gamma_h"], np.float32).reshape(8, 128).T),
        "rbc": np.ascontiguousarray(np.broadcast_to(
            np.asarray(inputs["L0_ffn_routing_token"], np.float32), (128, D))),
        "trif": np.triu(np.ones((128, 128), np.float32)),
    }


def attn_host_inputs(inputs):
    Wq = np.asarray(inputs["L0_attn_Wq"], np.float32)
    Wk = np.asarray(inputs["L0_attn_Wk"], np.float32)
    Wv = np.asarray(inputs["L0_attn_Wv"], np.float32)
    bq = np.asarray(inputs["L0_attn_bq"], np.float32)
    bk = np.asarray(inputs["L0_attn_bk"], np.float32)
    bv = np.asarray(inputs["L0_attn_bv"], np.float32)
    qk = lambda W: W.reshape(8, 2, 8, 128, 64).transpose(3, 0, 2, 1, 4).reshape(128, 8, 1024)
    wqk = np.concatenate([qk(Wq), qk(Wk)], axis=2).reshape(128, 16384)
    return {
        "wqkp": np.ascontiguousarray(wqk),
        "wvp": np.ascontiguousarray(
            Wv.reshape(2, 8, 8, 128, 64).transpose(3, 0, 2, 1, 4).reshape(128, 8192)),
        "bqp": np.ascontiguousarray(bq.reshape(8, 2, 64).transpose(1, 2, 0).reshape(128, 8)),
        "bkp": np.ascontiguousarray(bk.reshape(8, 2, 64).transpose(1, 2, 0).reshape(128, 8)),
        "bvp": np.ascontiguousarray(bv.reshape(1, 1024)),
    }


def l2_host_inputs(inputs):
    Wo = np.asarray(inputs["L0_attn_Wo"], np.float32)
    elo = np.zeros((1, 128), np.float32)
    elo[0, 0:64] = 1.0
    ehi = np.zeros((1, 128), np.float32)
    ehi[0, 64:128] = 1.0
    return {
        "wop": np.ascontiguousarray(
            Wo.reshape(8, 2, 64, D).transpose(1, 2, 0, 3).reshape(128, 8192)),
        "bop": np.asarray(inputs["L0_attn_bo"], np.float32).copy(),
        "gatep": np.asarray(inputs["L0_attn_gate"], np.float32).reshape(16, 1).copy(),
        "elo": elo, "ehi": ehi,
    }


def kernel(**inputs):
    x = np.asarray(inputs["L0_attn_hidden_states"], np.float32)
    ffn_in = ffn_host_inputs(inputs)
    attn_in = attn_host_inputs(inputs)
    l2_in = l2_host_inputs(inputs)

    nc1 = build_l1()
    maps1 = []
    for core in range(8):
        m = dict(ffn_in)
        m.update(attn_in)
        m["xin"] = np.ascontiguousarray(
            x[2 * core:2 * core + 2].reshape(RPC * T, D))
        maps1.append(m)
    res1 = run_bass_kernel_spmd(nc1, maps1, core_ids=list(range(8)))

    allnorm = np.ascontiguousarray(
        np.stack([res1.results[cc]["normpart"][:, 0] for cc in range(8)]))
    nc2 = build_l2()
    maps2 = []
    for core in range(8):
        m = dict(ffn_in)
        m.update(l2_in)
        m["x2i"] = res1.results[core]["x2o"]
        m["opri"] = res1.results[core]["oprime"]
        m["allnorm"] = allnorm
        maps2.append(m)
    res2 = run_bass_kernel_spmd(nc2, maps2, core_ids=list(range(8)))
    out = np.concatenate(
        [r["xout"].reshape(RPC, T, D) for r in res2.results], axis=0)
    return out


# version 04
# speedup vs baseline: 1.7978x; 1.0403x over previous
import numpy as np
import ml_dtypes
import concourse.bass as bass
import concourse.mybir as mybir
from concourse.bass_utils import run_bass_kernel_spmd

F32 = mybir.dt.float32
BF16 = mybir.dt.bfloat16
I32 = mybir.dt.int32
AF = mybir.ActivationFunctionType
ALU = mybir.AluOpType
AX = mybir.AxisListType.X

T = 2048
D = 1024
RPC = 2
B_TOT = 16
NT = T // 128
NSEL = 1024
DFFL = 512
LOGK = float(np.log(np.float32(1152.0)))
LN32 = float(np.log(np.float32(32.0)))
EPS_LN = 1e-5
SOFT_STAB = 2.0

STAGE = "A"
LAST_RES = None


class Ctx:
    def __init__(self, nc):
        self.nc = nc
        self.sem = {}
        self.cnt = {}
        self.t = {}

    def sems(self, *names):
        for n in names:
            self.sem[n] = self.nc.semaphore(n).__enter__()
            self.cnt[n] = 0

    def dma(self, eng, name, out, in_, **kw):
        eng.dma_start(out, in_, **kw).then_inc(self.sem[name], 16)
        self.cnt[name] += 16
        return self.cnt[name]

    def idma(self, eng, name, out, out_off, in_, in_off):
        eng.indirect_dma_start(out, out_off, in_, in_off).then_inc(
            self.sem[name], 16)
        self.cnt[name] += 16
        return self.cnt[name]

    def tick(self, name, ins):
        ins.then_inc(self.sem[name], 1)
        self.cnt[name] += 1
        return self.cnt[name]

    def sb(self, name, shape, dtype):
        h = self.nc.sbuf_tensor(name, shape, dtype).__enter__()
        self.t[name] = h
        return h

    def ps(self, name):
        h = self.nc.psum_tensor(name, [128, 1024], F32).__enter__()
        self.t[name] = h
        return h

    def dr(self, name, shape, dtype, kind=None):
        h = (self.nc.dram_tensor(name, shape, dtype, kind=kind) if kind
             else self.nc.dram_tensor(name, shape, dtype))
        self.t[name] = h
        return h



_ACT_SET_CACHE = {}


def _act_set_id(nc, name):
    key = (id(nc.m), name)
    if key not in _ACT_SET_CACHE:
        from concourse.hw_specs import get_activation_tables
        names = list(get_activation_tables(nc.m.arch).keys())
        _ACT_SET_CACHE[key] = names.index(name)
    return _ACT_SET_CACHE[key]


def load_act_set(c, s, name):
    s.add_instruction(mybir.InstLoadActFuncSet(
        name=c.nc.get_next_instruction_name(),
        act_func_set_id=_act_set_id(c.nc, name), ins=[], outs=[]))


def wge(eng, c, name, val):
    eng.wait_ge(c.sem[name], val)



def alloc_common(c, with_ffn=True):
    nc = c.nc
    if with_ffn:
        c.dr("w1lp", [128, 8 * DFFL], F32, "ExternalInput")
        c.dr("w2lp", [128, 4 * D], F32, "ExternalInput")
        c.dr("b1lp", [128, 4], F32, "ExternalInput")
        c.dr("b2lp", [1, D], F32, "ExternalInput")
        c.dr("w1hp", [128, 32 * D], F32, "ExternalInput")
        c.dr("w2hp", [128, 32 * D], F32, "ExternalInput")
        c.dr("b1hp", [128, 32], F32, "ExternalInput")
        c.dr("b2hp", [1, D], F32, "ExternalInput")
        c.dr("glp", [128, 8], F32, "ExternalInput")
        c.dr("ghp", [128, 8], F32, "ExternalInput")
        c.dr("rbc", [128, D], F32, "ExternalInput")
        c.dr("trif", [128, 128], F32, "ExternalInput")
        c.dr("ynorm_d", [RPC * T, D], BF16)
        c.dr("ycomp_d", [NSEL + 1, D], BF16)
        c.dr("hcomp_d", [NSEL + 1, D], F32)
        c.dr("w1h_bf", [128, 32 * D], BF16)
        c.dr("w2h_bf", [128, 32 * D], BF16)

    c.sb("ones128", [128, 1], F32)
    c.sb("ones128b", [128, 1], BF16)
    c.sb("zero128", [128, 1], F32)
    c.sb("ln32t", [128, 1], F32)
    c.sb("zero1", [1, 1], F32)
    c.sb("negstab", [128, 1], F32)
    c.sb("onesrow_f", [1, 128], F32)
    c.sb("onesrow_b", [1, 128], BF16)
    c.sb("r1bc", [128, 1], F32)
    if with_ffn:
        c.sb("tri_b", [128, 128], BF16)
        c.sb("rbc_s", [128, D], F32)
        c.sb("w1l_s", [128, 8 * DFFL], BF16)
        c.sb("w2l_s", [128, 4 * D], BF16)
        c.sb("b1l_s", [128, 4], F32)
        c.sb("b2l_s", [1, D], BF16)
        c.sb("b1h_s", [128, 32], F32)
        c.sb("b2h_s", [1, D], BF16)
        c.sb("gl_s", [128, 8], F32)
        c.sb("gh_s", [128, 8], F32)
    for i in range(2):
        c.sb("wbuf%d" % i, [128, D], BF16)
    for i in range(3):
        c.sb("xt%d" % i, [128, D], F32)
    for i in range(2):
        c.sb("ynt%d" % i, [128, D], BF16)
        c.sb("x3t%d" % i, [128, D], F32)
        c.sb("hg%d" % i, [128, D], F32)
    c.sb("acc", [128, D], F32)
    c.sb("sq_scr", [128, D], F32)
    c.t["zrow"] = c.t["sq_scr"][0:1, 0:D]
    c.t["gp_scr"] = c.t["sq_scr"]
    for n in ["stat_sum", "stat_ssq", "stat_m", "stat_k", "stat_bb", "stat_rv",
              "xr_t", "sfin", "sbt", "satf", "mtmp", "cumf"]:
        c.sb(n, [128, RPC * NT], F32)
    c.sb("satb", [128, RPC * NT], BF16)
    c.sb("cumidx", [128, RPC * NT], I32)
    c.sb("etile", [128, RPC * NT], F32)
    c.sb("rowred", [128, RPC], F32)
    c.sb("na_row", [1, RPC], F32)
    c.sb("nab", [128, RPC], F32)
    c.sb("oscr", [1, NT], F32)
    c.sb("oscr2", [1, NT], F32)
    c.sb("st_a", [128, 1], F32)
    c.sb("acc_sq0", [128, 1], F32)
    c.sb("acc_sq1", [128, 1], F32)
    c.sb("st_b", [128, 1], F32)
    c.sb("st_c", [128, 1], F32)
    c.sb("ycT", [128, 8 * D], BF16)
    c.sb("hl", [128, 4 * T], BF16)
    c.sb("hh", [128, 32 * NSEL], BF16)
    c.t["hc_s"] = c.t["acc"]
    for i in range(4):
        c.ps("qb%d" % i)
    for i in range(8):
        c.t["pb%d" % i] = c.t["qb%d" % (i // 2)][:, (i % 2) * 512:(i % 2) * 512 + 512]
    c.sems("dX", "dY", "dW", "dS", "dG", "dH", "dO", "dT", "dP", "tn", "ve", "se", "gp")


def alloc_attn(c):
    c.sb("V_t", [128, 16 * 16 * 65], BF16)
    c.sb("opab", [128, T], BF16)
    c.sb("normcols", [128, 16], F32)
    c.t["oscr3"] = c.t["x3t0"]
    c.t["wqkbuf0"] = c.t["hl"][:, 0:2048]
    c.t["wqkbuf1"] = c.t["hl"][:, 2048:4096]
    c.t["vwbuf0"] = c.t["ycT"][:, 4096:8192]
    c.t["vwbuf1"] = c.t["hl"][:, 4096:8192]
    c.sb("bqp_s", [128, 8], F32)
    c.sb("bkp_s", [128, 8], F32)
    c.sb("bvp_s", [1, 1024], BF16)
    c.dr("wqkp", [128, 16384], F32, "ExternalInput")
    c.dr("wvp", [128, 8192], F32, "ExternalInput")
    c.dr("wqk_bf", [128, 16384], BF16)
    c.dr("wv_bf", [128, 8192], BF16)
    c.dr("bqp", [128, 8], F32, "ExternalInput")
    c.dr("bkp", [128, 8], F32, "ExternalInput")
    c.dr("bvp", [1, 1024], F32, "ExternalInput")


def phase_attn_consts(c):
    t = c.t
    with c.nc.Block() as blk:
        def g1(g):
            c.dma(g, "dH", t["bqp_s"][:, :], t["bqp"][:, :])
            c.dma(g, "dH", t["bkp_s"][:, :], t["bkp"][:, :])
            c.dma(g, "dH", t["bvp_s"][:, :], t["bvp"][:, :])
            g.wait_ge(c.sem["dH"], c.cnt["dH"])
            c.t["wqkcast"] = c.dma(g, "dP", t["wqk_bf"][:, :], t["wqkp"][:, :])
            c.t["wvcast"] = c.dma(g, "dP", t["wv_bf"][:, :], t["wvp"][:, :])
        blk.gpsimd(g1)



def phase_consts(c, with_ffn=True):
    t = c.t
    with c.nc.Block() as blk:
        @blk.gpsimd
        def _(g):
            g.memset(t["ones128"][:, :], 1.0)
            g.memset(t["ones128b"][:, :], 1.0)
            g.memset(t["zero128"][:, :], 0.0)
            g.memset(t["ln32t"][:, :], LN32)
            g.memset(t["zero1"][:, :], 0.0)
            g.memset(t["negstab"][:, :], -SOFT_STAB)
            g.memset(t["onesrow_f"][:, :], 1.0)
            g.memset(t["onesrow_b"][:, :], 1.0)
            g.memset(t["zrow"][:, :], 0.0)
            if with_ffn:
                c.dma(g, "dH", t["rbc_s"][:, :], t["rbc"][:, :])
                c.dma(g, "dH", t["tri_b"][:, :], t["trif"][:, :])
                c.dma(g, "dH", t["w1l_s"][:, :], t["w1lp"][:, :])
                c.dma(g, "dH", t["w2l_s"][:, :], t["w2lp"][:, :])
                c.dma(g, "dH", t["b1l_s"][:, :], t["b1lp"][:, :])
                c.dma(g, "dH", t["b2l_s"][:, :], t["b2lp"][:, :])
                c.dma(g, "dH", t["b1h_s"][:, :], t["b1hp"][:, :])
                c.dma(g, "dH", t["b2h_s"][:, :], t["b2hp"][:, :])
                c.dma(g, "dH", t["gl_s"][:, :], t["glp"][:, :])
                c.dma(g, "dH", t["gh_s"][:, :], t["ghp"][:, :])
                g.wait_ge(c.sem["dH"], c.cnt["dH"])
                c.dma(g, "dH", t["hcomp_d"][NSEL:NSEL + 1, :], t["zrow"][0:1, :])
                g.wait_ge(c.sem["dH"], c.cnt["dH"])
                c.t["w1cast"] = c.dma(g, "dP", t["w1h_bf"][:, :], t["w1hp"][:, :])
                c.t["w2cast"] = c.dma(g, "dP", t["w2h_bf"][:, :], t["w2hp"][:, :])

        if with_ffn:
            ld_all = c.cnt["dH"]

            @blk.scalar
            def _(s):
                wge(s, c, "dH", ld_all)
                for cc in range(8):
                    sl = t["w1l_s"][:, cc * DFFL:(cc + 1) * DFFL]
                    s.activation(sl, sl, AF.Identity, bias=t["zero128"][:, :],
                                 scale=t["gl_s"][:, cc:cc + 1])
                for cc in range(4):
                    sl = t["w2l_s"][:, cc * D:(cc + 1) * D]
                    s.activation(sl, sl, AF.Identity, bias=t["zero128"][:, :],
                                 scale=0.5)
                s.activation(t["b2l_s"][:, :], t["b2l_s"][:, :], AF.Identity,
                             bias=t["zero1"][:, :], scale=0.5)
                s.activation(t["b2h_s"][:, :], t["b2h_s"][:, :], AF.Identity,
                             bias=t["zero1"][:, :], scale=0.5)
                load_act_set(c, s, "natural_log_exp_and_others")

            @blk.vector
            def _(v):
                wge(v, c, "dH", ld_all)
                c.tick("ve", v.reduce_sum(t["st_a"][0:1, 0:1],
                                          t["rbc_s"][0:1, :], axis=AX))

            @blk.tensor
            def _(tn):
                wge(tn, c, "ve", c.cnt["ve"])
                c.tick("tn", tn.matmul(t["pb7"][:, 0:1], t["onesrow_f"][:, :],
                                       t["st_a"][0:1, 0:1], start=True,
                                       stop=True))

            @blk.vector
            def _(v):
                wge(v, c, "tn", c.cnt["tn"])
                c.tick("ve", v.tensor_copy(t["r1bc"][:, :], t["pb7"][:, 0:1]))


def emit_stats_pass2(c, blk, row, mode):
    t = c.t
    r0 = row * NT

    def vp2(v):
        v.tensor_scalar_mul(t["stat_m"][:, r0:r0 + NT],
                            t["stat_sum"][:, r0:r0 + NT], 1.0 / D)
        v.drain()
        v.tensor_mul(t["mtmp"][:, r0:r0 + NT], t["stat_sum"][:, r0:r0 + NT],
                     t["stat_m"][:, r0:r0 + NT])
        v.drain()
        v.tensor_sub(t["stat_ssq"][:, r0:r0 + NT], t["stat_ssq"][:, r0:r0 + NT],
                     t["mtmp"][:, r0:r0 + NT])
        v.drain()
        c.tick("ve", v.tensor_scalar(t["stat_rv"][:, r0:r0 + NT],
                                     t["stat_ssq"][:, r0:r0 + NT],
                                     1.0 / D, EPS_LN, ALU.mult, ALU.add))
    blk.vector(vp2)
    ve_p = c.cnt["ve"]

    def sp2(s, ve_p=ve_p):
        wge(s, c, "ve", ve_p)
        s.activation(t["etile"][:, r0:r0 + NT], t["stat_rv"][:, r0:r0 + NT],
                     AF.Ln, bias=t["zero128"][:, :])
        s.drain()
        last = s.activation(t["stat_rv"][:, r0:r0 + NT],
                            t["etile"][:, r0:r0 + NT], AF.Exp,
                            bias=t["zero128"][:, :], scale=-0.5)
        if mode == "ffn":
            s.activation(t["etile"][:, r0:r0 + NT], t["stat_ssq"][:, r0:r0 + NT],
                         AF.Ln, bias=t["zero128"][:, :])
            s.drain()
            last = s.activation(t["stat_k"][:, r0:r0 + NT],
                                t["etile"][:, r0:r0 + NT], AF.Exp,
                                bias=t["ln32t"][:, :], scale=-0.5)
        c.tick("se", last)
    blk.scalar(sp2)
    se_p = c.cnt["se"]

    def vp3(v, se_p=se_p):
        wge(v, c, "se", se_p)
        kap = t["stat_k"] if mode == "ffn" else t["stat_rv"]
        last = v.scalar_tensor_tensor(t["stat_bb"][:, r0:r0 + NT],
                                      t["stat_m"][:, r0:r0 + NT], -1.0,
                                      kap[:, r0:r0 + NT], ALU.mult, ALU.mult)
        if mode == "ffn":
            v.tensor_scalar(t["mtmp"][:, r0:r0 + NT], t["stat_m"][:, r0:r0 + NT],
                            t["r1bc"][:, :], None, ALU.mult)
            v.drain()
            v.tensor_sub(t["mtmp"][:, r0:r0 + NT], t["xr_t"][:, r0:r0 + NT],
                         t["mtmp"][:, r0:r0 + NT])
            v.drain()
            last = v.tensor_mul(t["sfin"][:, r0:r0 + NT],
                                t["mtmp"][:, r0:r0 + NT],
                                t["stat_rv"][:, r0:r0 + NT])
        c.tick("ve", last)
    blk.vector(vp3)


def emit_apply_y(c, blk, row, src, dst, mode):
    t = c.t
    r0 = row * NT
    kap = t["stat_k"] if mode == "ffn" else t["stat_rv"]
    ld = [0] * NT
    se_yn = [0] * NT
    yn_dma = [0] * NT
    ve_gate = c.cnt["ve"]
    se_gate = c.cnt["se"]
    gp_gate = c.cnt["gp"]
    for tt in range(NT):
        col = r0 + tt
        xb = t["xt%d" % (tt % 3)]
        yb = t["ynt%d" % (tt % 2)]
        rowslice = slice(row * T + tt * 128, row * T + tt * 128 + 128)

        def load(sy, tt=tt, xb=xb, rowslice=rowslice):
            if tt >= 3:
                wge(sy, c, "se", se_yn[tt - 3])
            else:
                wge(sy, c, "ve", ve_gate)
                wge(sy, c, "se", se_gate)
                wge(sy, c, "gp", gp_gate)
            ld[tt] = c.dma(sy, "dX", xb[:, :], src[rowslice, :])
        blk.sync(load)

        def sey(s, tt=tt, col=col, xb=xb, yb=yb):
            wge(s, c, "dX", ld[tt])
            if tt == 0:
                wge(s, c, "ve", ve_gate)
            if tt >= 2:
                wge(s, c, "dY", yn_dma[tt - 2])
            se_yn[tt] = c.tick("se", s.activation(
                yb[:, :], xb[:, :], AF.Identity,
                bias=t["stat_bb"][:, col:col + 1], scale=kap[:, col:col + 1]))
        blk.scalar(sey)

        def gpy(g, tt=tt, yb=yb, rowslice=rowslice):
            wge(g, c, "se", se_yn[tt])
            yn_dma[tt] = c.dma(g, "dY", dst[rowslice, :], yb[:, :])
        blk.gpsimd(gpy)

    def fin(sy):
        wge(sy, c, "dY", c.cnt["dY"])
        wge(sy, c, "dX", c.cnt["dX"])
    blk.sync(fin)


def phase_lnpass(c, row, xsrc, mode):
    t = c.t
    with c.nc.Block() as blk:
        ld = [0] * NT
        ve_done = [0] * NT
        se_acc = [0] * NT
        for tt in range(NT):
            col = row * NT + tt
            xb = t["xt%d" % (tt % 3)]

            def load(sy, tt=tt, xb=xb):
                if tt >= 3:
                    wge(sy, c, "ve", ve_done[tt - 3])
                ld[tt] = c.dma(sy, "dX", xb[:, :],
                               xsrc[(row * T + tt * 128):(row * T + tt * 128 + 128), :])
            blk.sync(load)

            def se1(s, tt=tt, xb=xb):
                wge(s, c, "dX", ld[tt])
                if tt >= 2:
                    wge(s, c, "ve", ve_done[tt - 2])
                se_acc[tt] = c.tick("se", s.activation(
                    t["sq_scr"][:, :], xb[:, :], AF.Square,
                    bias=t["zero128"][:, :],
                    accum_out=t["acc_sq%d" % (tt % 2)][:, :]))
            blk.scalar(se1)

            def ve1(v, tt=tt, col=col, xb=xb):
                wge(v, c, "dX", ld[tt])
                v.reduce_sum(t["stat_sum"][:, col:col + 1], xb[:, :], axis=AX)
                if mode == "ffn":
                    v.scalar_tensor_tensor(t["gp_scr"][:, :], xb[:, :], 1.0,
                                           t["rbc_s"][:, :], ALU.mult, ALU.mult,
                                           accum_out=t["xr_t"][:, col:col + 1])
                wge(v, c, "se", se_acc[tt])
                ve_done[tt] = c.tick("ve", v.tensor_copy(
                    t["stat_ssq"][:, col:col + 1],
                    t["acc_sq%d" % (tt % 2)][:, :]))
            blk.vector(ve1)

        emit_stats_pass2(c, blk, row, mode)
        emit_apply_y(c, blk, row, xsrc, c.t["ynorm_d"], mode)


def phase_coor(c):
    t = c.t
    with c.nc.Block() as blk:
        def init(v):
            c.tick("ve", v.memset(t["sbt"][:, :], 0.0))
        blk.vector(init)

        for it in range(1, 51):
            def se1(s, it=it):
                wge(s, c, "ve", c.cnt["ve"])
                last = None
                for r in range(RPC):
                    last = s.activation(t["etile"][:, r * NT:(r + 1) * NT],
                                        t["sbt"][:, r * NT:(r + 1) * NT], AF.Exp,
                                        bias=t["zero128"][:, :])
                c.tick("se", last)
            blk.scalar(se1)
            se_e = c.cnt["se"]

            def ver(v, se_e=se_e, it=it):
                wge(v, c, "se", se_e)
                if it > 1:
                    wge(v, c, "tn", c.cnt["tn"] - 1)
                last = None
                for r in range(RPC):
                    last = v.reduce_sum(t["rowred"][:, r:r + 1],
                                        t["etile"][:, r * NT:(r + 1) * NT],
                                        axis=AX)
                c.tick("ve", last)
            blk.vector(ver)
            ve_r = c.cnt["ve"]

            def tn1(tn, ve_r=ve_r):
                wge(tn, c, "ve", ve_r)
                c.tick("tn", tn.matmul(t["pb7"][0:1, 0:RPC], t["ones128"][:, :],
                                       t["rowred"][:, 0:RPC], start=True,
                                       stop=True))
            blk.tensor(tn1)
            tn_c = c.cnt["tn"]

            def se2(s, tn_c=tn_c):
                wge(s, c, "tn", tn_c)
                c.tick("se", s.activation(t["na_row"][0:1, 0:RPC],
                                          t["pb7"][0:1, 0:RPC], AF.Ln,
                                          bias=t["zero1"][:, :]))
            blk.scalar(se2)
            se_l = c.cnt["se"]

            def ve1(v, se_l=se_l):
                wge(v, c, "se", se_l)
                c.tick("ve", v.tensor_scalar_add(
                    t["na_row"][0:1, 0:RPC], t["na_row"][0:1, 0:RPC], -LOGK))
            blk.vector(ve1)
            ve_n = c.cnt["ve"]

            def tn2(tn, ve_n=ve_n):
                wge(tn, c, "ve", ve_n)
                c.tick("tn", tn.matmul(t["pb6"][:, 0:RPC], t["onesrow_f"][:, :],
                                       t["na_row"][0:1, 0:RPC], start=True,
                                       stop=True))
            blk.tensor(tn2)
            tn_b = c.cnt["tn"]

            def ve2(v, tn_b=tn_b, it=it):
                wge(v, c, "tn", tn_b)
                v.tensor_copy(t["nab"][:, 0:RPC], t["pb6"][:, 0:RPC])
                v.drain()
                if it < 50:
                    last = None
                    for r in range(RPC):
                        last = v.tensor_scalar_min(
                            t["sbt"][:, r * NT:(r + 1) * NT],
                            t["sfin"][:, r * NT:(r + 1) * NT],
                            t["nab"][:, r:r + 1])
                    c.tick("ve", last)
                else:
                    for r in range(RPC):
                        v.tensor_scalar(
                            t["satf"][:, r * NT:(r + 1) * NT],
                            t["sfin"][:, r * NT:(r + 1) * NT],
                            t["nab"][:, r:r + 1], None, ALU.is_ge)
                    v.drain()
                    c.tick("ve", v.tensor_copy(t["satb"][:, :], t["satf"][:, :]))
            blk.vector(ve2)


def phase_mask(c, row):
    t = c.t
    with c.nc.Block() as blk:
        r0 = row * NT

        def tn1(tn):
            tn.matmul(t["pb0"][:, 0:NT], t["tri_b"][:, :],
                      t["satb"][:, r0:r0 + NT], start=True,
                      stop=False, skip_group_check=True)
            c.tick("tn", tn.matmul(t["pb1"][0:1, 0:NT], t["ones128b"][:, :],
                                   t["satb"][:, r0:r0 + NT], start=True,
                                   stop=True, skip_group_check=True))
        blk.tensor(tn1)
        tn_a = c.cnt["tn"]

        def ve1(v, tn_a=tn_a):
            wge(v, c, "tn", tn_a)
            v.tensor_copy(t["oscr2"][0:1, 0:NT], t["pb1"][0:1, 0:NT])
            v.drain()
            v.tensor_tensor_scan(t["oscr"][0:1, 0:NT], t["oscr2"][0:1, 0:NT],
                                 t["oscr2"][0:1, 0:NT], 0.0, ALU.add,
                                 ALU.bypass)
            v.drain()
            c.tick("ve", v.tensor_sub(t["oscr"][0:1, 0:NT],
                                      t["oscr"][0:1, 0:NT],
                                      t["oscr2"][0:1, 0:NT]))
        blk.vector(ve1)
        ve_o = c.cnt["ve"]

        def tn2(tn, ve_o=ve_o):
            wge(tn, c, "ve", ve_o)
            c.tick("tn", tn.matmul(t["pb0"][:, 0:NT], t["onesrow_f"][:, :],
                                   t["oscr"][0:1, 0:NT], start=False, stop=True,
                                   skip_group_check=True))
        blk.tensor(tn2)
        tn_b = c.cnt["tn"]

        def ve2(v, tn_b=tn_b):
            wge(v, c, "tn", tn_b)
            v.tensor_scalar(t["mtmp"][:, r0:r0 + NT], t["pb0"][:, 0:NT],
                            float(NSEL), None, ALU.is_le)
            v.tensor_scalar_add(t["cumf"][:, r0:r0 + NT], t["pb0"][:, 0:NT],
                                -float(NSEL + 1))
            v.drain()
            v.tensor_mul(t["mtmp"][:, r0:r0 + NT], t["mtmp"][:, r0:r0 + NT],
                         t["satf"][:, r0:r0 + NT])
            v.drain()
            v.tensor_mul(t["cumf"][:, r0:r0 + NT], t["cumf"][:, r0:r0 + NT],
                         t["mtmp"][:, r0:r0 + NT])
            v.drain()
            v.tensor_scalar_add(t["cumf"][:, r0:r0 + NT],
                                t["cumf"][:, r0:r0 + NT], float(NSEL))
            v.drain()
            c.tick("ve", v.tensor_copy(t["cumidx"][:, r0:r0 + NT],
                                       t["cumf"][:, r0:r0 + NT]))
        blk.vector(ve2)


def phase_scatter(c, row):
    t = c.t
    with c.nc.Block() as blk:
        ld = [0] * NT
        sc = [0] * NT
        for tt in range(NT):
            yb = t["ynt%d" % (tt % 2)]

            def load(sy, tt=tt, yb=yb):
                if tt >= 2:
                    wge(sy, c, "dS", sc[tt - 2])
                ld[tt] = c.dma(sy, "dY", yb[:, :], t["ynorm_d"][
                    (row * T + tt * 128):(row * T + tt * 128 + 128), :])
            blk.sync(load)

            def scat(g, tt=tt, yb=yb):
                wge(g, c, "dY", ld[tt])
                sc[tt] = c.idma(g, "dS", t["ycomp_d"][:, :],
                                bass.IndirectOffsetOnAxis(
                                    t["cumidx"][:, row * NT + tt:row * NT + tt + 1], 0),
                                yb[:, :], None)
            blk.gpsimd(scat)

        def fin(sy):
            wge(sy, c, "dS", c.cnt["dS"])
        blk.sync(fin)


def phase_heavy(c, row, sub=99):
    import os
    sub = int(os.environ.get('KHEAVY', '99'))
    t = c.t
    nc = c.nc
    with nc.Block() as blk:
        tl = [[0] * 2 for _ in range(8)]

        def loads(sy):
            for i in range(8):
                for h in range(2):
                    tl[i][h] = c.dma(
                        sy, "dT", t["ycT"][:, i * D + h * 512:i * D + (h + 1) * 512],
                        t["ycomp_d"][h * 512:(h + 1) * 512, i * 128:(i + 1) * 128],
                        transpose=True)
        blk.sync(loads)

        def fold(v):
            for i in range(8):
                for h in range(2):
                    wge(v, c, "dT", tl[i][h])
                    sl = t["ycT"][:, i * D + h * 512:i * D + (h + 1) * 512]
                    last = v.tensor_scalar_mul(sl, sl, t["gh_s"][:, i:i + 1])
            c.tick("ve", last)
        blk.vector(fold)

    if sub < 1:
        return
    with nc.Block() as blk:
        wl = [0] * 32
        mm = [0] * 32
        ev = [0] * 32
        for dd in range(32):
            i = dd % 2
            bk = (dd % 2) * 2

            def load(sy, dd=dd, i=i):
                if dd == 0:
                    wge(sy, c, "dP", t["w1cast"])
                if dd >= 2:
                    wge(sy, c, "tn", mm[dd - 2])
                wl[dd] = c.dma(sy, "dW", t["wbuf%d" % i][:, :],
                               t["w1h_bf"][:, dd * D:(dd + 1) * D])
            blk.sync(load)

            def mms(tn, dd=dd, i=i, bk=bk):
                wge(tn, c, "dW", wl[dd])
                if dd >= 2:
                    wge(tn, c, "se", ev[dd - 2])
                last = None
                for h in range(2):
                    for cc in range(8):
                        last = tn.matmul(
                            t["pb%d" % (bk + h)][:, :],
                            t["wbuf%d" % i][:, cc * 128:(cc + 1) * 128],
                            t["ycT"][:, cc * D + h * 512:cc * D + (h + 1) * 512],
                            start=(cc == 0), stop=(cc == 7))
                mm[dd] = c.tick("tn", last)
            blk.tensor(mms)

            def evac(s, dd=dd, bk=bk):
                if dd == 0:
                    load_act_set(c, s, "gelu_and_others")
                wge(s, c, "tn", mm[dd])
                last = None
                for h in range(2):
                    last = s.activation(
                        t["hh"][:, dd * D + h * 512:dd * D + (h + 1) * 512],
                        t["pb%d" % (bk + h)][:, :], AF.Gelu,
                        bias=t["b1h_s"][:, dd:dd + 1])
                ev[dd] = c.tick("se", last)
            blk.scalar(evac)

    if sub < 2:
        return
    with nc.Block() as blk:
        mm2 = []
        st2 = {}
        for of in range(2):
            se_prev_evac = c.cnt["se"]
            wl2 = [0] * 32
            for dd in range(32):
                i = dd % 2
                j = of * 32 + dd

                def load(sy, dd=dd, i=i, of=of, j=j):
                    if j == 0:
                        wge(sy, c, "dP", t["w2cast"])
                    if j >= 2:
                        wge(sy, c, "tn", mm2[j - 2])
                    wl2[dd] = c.dma(sy, "dW", t["wbuf%d" % i][:, 0:512],
                                    t["w2h_bf"][:, dd * D + of * 512:dd * D + (of + 1) * 512])
                blk.sync(load)

                def mms(tn, dd=dd, i=i, se_prev_evac=se_prev_evac):
                    wge(tn, c, "dW", wl2[dd])
                    if dd == 0:
                        wge(tn, c, "se", se_prev_evac)
                    last = None
                    for ct in range(8):
                        last = tn.matmul(
                            t["pb%d" % ct][:, :],
                            t["hh"][:, dd * D + ct * 128:dd * D + (ct + 1) * 128],
                            t["wbuf%d" % i][:, 0:512],
                            start=(dd == 0), stop=False,
                            skip_group_check=True)
                    mm2.append(c.tick("tn", last))
                blk.tensor(mms)

            def tn1(tn, of=of):
                last = None
                for ct in range(8):
                    last = tn.matmul(t["pb%d" % ct][:, :], t["onesrow_b"][:, :],
                                     t["b2h_s"][0:1, of * 512:(of + 1) * 512],
                                     start=False, stop=True,
                                     skip_group_check=True)
                c.tick("tn", last)
            blk.tensor(tn1)
            tn_d = c.cnt["tn"]

            for ct in range(8):
                def ev(s, ct=ct, tn_d=tn_d, of=of):
                    wge(s, c, "tn", tn_d)
                    if (of, ct - 2) in st2:
                        wge(s, c, "dH", st2[(of, ct - 2)])
                    c.tick("se", s.activation(
                        t["hc_s"][:, (ct % 2) * 512:(ct % 2) * 512 + 512],
                        t["pb%d" % ct][:, :], AF.Identity,
                        bias=t["zero128"][:, :], scale=0.5))
                blk.scalar(ev)
                se_e = c.cnt["se"]

                def store(g, ct=ct, se_e=se_e, of=of):
                    wge(g, c, "se", se_e)
                    st2[(of, ct)] = c.dma(
                        g, "dH",
                        t["hcomp_d"][ct * 128:(ct + 1) * 128,
                                     of * 512:(of + 1) * 512],
                        t["hc_s"][:, (ct % 2) * 512:(ct % 2) * 512 + 512])
                blk.gpsimd(store)

        def fin(sy):
            wge(sy, c, "dH", c.cnt["dH"])
        blk.sync(fin)


def phase_light_m1(c, row):
    t = c.t
    with c.nc.Block() as blk:
        for tb in range(4):
            tl = [0] * 8

            def loads(sy, tb=tb):
                if tb >= 1:
                    wge(sy, c, "tn", c.cnt["tn"])
                for cc in range(8):
                    tl[cc] = c.dma(
                        sy, "dT", t["ycT"][:, cc * 512:(cc + 1) * 512],
                        t["ynorm_d"][(row * T + tb * 512):(row * T + (tb + 1) * 512),
                                     cc * 128:(cc + 1) * 128],
                        transpose=True)
            blk.sync(loads)

            def mms(tn, tb=tb):
                last = None
                for cc in range(8):
                    wge(tn, c, "dT", tl[cc])
                    for d in range(4):
                        last = tn.matmul(
                            t["pb%d" % (4 + d)][:, :],
                            t["w1l_s"][:, cc * DFFL + d * 128:cc * DFFL + (d + 1) * 128],
                            t["ycT"][:, cc * 512:(cc + 1) * 512],
                            start=(cc == 0), stop=(cc == 7),
                            skip_group_check=True)
                c.tick("tn", last)
            blk.tensor(mms)
            tn_m = c.cnt["tn"]

            def evac(v, tb=tb, tn_m=tn_m):
                wge(v, c, "tn", tn_m)
                last = None
                for d in range(4):
                    last = v.tensor_copy(
                        t["hl"][:, d * T + tb * 512:d * T + (tb + 1) * 512],
                        t["pb%d" % (4 + d)][:, :])
                c.tick("ve", last)
            blk.vector(evac)

        def gelu(s):
            wge(s, c, "ve", c.cnt["ve"])
            last = None
            for d in range(4):
                last = s.activation(t["hl"][:, d * T:(d + 1) * T],
                                    t["hl"][:, d * T:(d + 1) * T], AF.Gelu,
                                    bias=t["b1l_s"][:, d:d + 1])
            c.tick("se", last)
        blk.scalar(gelu)


def phase_light2_add(c, row, xsrc, xdst, mode, ydst=None):
    t = c.t
    with c.nc.Block() as blk:
        ld = [0] * NT
        gl = [0] * NT
        mm = [0] * NT
        ve_acc = [0] * NT
        gp_x3 = [0] * NT
        st_x = [0] * NT
        st_done = [0] * NT
        for tt in range(NT):
            col = row * NT + tt
            xb = t["xt%d" % (tt % 3)]
            x3 = t["x3t%d" % (tt % 2)]
            hgb = t["hg%d" % (tt % 2)]
            bk = (tt % 2) * 2
            rowslice = slice(row * T + tt * 128, row * T + tt * 128 + 128)

            def load(sy, tt=tt, xb=xb, rowslice=rowslice):
                if tt >= 3:
                    wge(sy, c, "gp", gp_x3[tt - 3])
                ld[tt] = c.dma(sy, "dX", xb[:, :], xsrc[rowslice, :])
            blk.sync(load)

            def gath(g, tt=tt, hgb=hgb, col=col):
                if tt >= 2:
                    wge(g, c, "ve", ve_acc[tt - 2])
                gl[tt] = c.idma(g, "dG", hgb[:, :], None, t["hcomp_d"][:, :],
                                bass.IndirectOffsetOnAxis(
                                    t["cumidx"][:, col:col + 1], 0))
            blk.gpsimd(gath)

            def mms(tn, tt=tt, bk=bk):
                if tt >= 2:
                    wge(tn, c, "ve", ve_acc[tt - 2])
                last = None
                for of in range(2):
                    for d in range(4):
                        tn.matmul(t["pb%d" % (bk + of)][:, :],
                                  t["hl"][:, d * T + tt * 128:d * T + (tt + 1) * 128],
                                  t["w2l_s"][:, d * D + of * 512:d * D + (of + 1) * 512],
                                  start=(d == 0), stop=False,
                                  skip_group_check=True)
                    last = tn.matmul(t["pb%d" % (bk + of)][:, :],
                                     t["onesrow_b"][:, :],
                                     t["b2l_s"][0:1, of * 512:(of + 1) * 512],
                                     start=False, stop=True,
                                     skip_group_check=True)
                mm[tt] = c.tick("tn", last)
            blk.tensor(mms)

            def veacc(v, tt=tt, bk=bk, hgb=hgb):
                wge(v, c, "tn", mm[tt])
                wge(v, c, "dG", gl[tt])
                if tt >= 1:
                    wge(v, c, "gp", gp_x3[tt - 1])
                v.tensor_add(t["acc"][:, 0:512], t["pb%d" % bk][:, :],
                             hgb[:, 0:512])
                ve_acc[tt] = c.tick("ve", v.tensor_add(
                    t["acc"][:, 512:1024], t["pb%d" % (bk + 1)][:, :],
                    hgb[:, 512:1024]))
            blk.vector(veacc)

            def gpadd(g, tt=tt, xb=xb, x3=x3):
                wge(g, c, "ve", ve_acc[tt])
                wge(g, c, "dX", ld[tt])
                if tt >= 2:
                    wge(g, c, "dO", st_x[tt - 2])
                    if mode != "none":
                        wge(g, c, "ve", st_done[tt - 2])
                gp_x3[tt] = c.tick("gp", g.tensor_add(x3[:, :], xb[:, :],
                                                      t["acc"][:, :]))
            blk.gpsimd(gpadd)

            def store(sy, tt=tt, x3=x3, rowslice=rowslice):
                wge(sy, c, "gp", gp_x3[tt])
                st_x[tt] = c.dma(sy, "dO", xdst[rowslice, :], x3[:, :])
            blk.sync(store)

            if mode != "none":
                def se1(s, tt=tt, x3=x3):
                    if tt == 0:
                        load_act_set(c, s, "natural_log_exp_and_others")
                    wge(s, c, "gp", gp_x3[tt])
                    if tt >= 2:
                        wge(s, c, "ve", st_done[tt - 2])
                    c.tick("se", s.activation(
                        t["sq_scr"][:, :], x3[:, :], AF.Square,
                        bias=t["zero128"][:, :],
                        accum_out=t["acc_sq%d" % (tt % 2)][:, :]))
                blk.scalar(se1)
                se_acc = c.cnt["se"]

                def ve1(v, tt=tt, col=col, x3=x3, se_acc=se_acc):
                    wge(v, c, "gp", gp_x3[tt])
                    v.reduce_sum(t["stat_sum"][:, col:col + 1], x3[:, :], axis=AX)
                    if mode == "ffn":
                        v.scalar_tensor_tensor(t["gp_scr"][:, :], x3[:, :], 1.0,
                                               t["rbc_s"][:, :], ALU.mult,
                                               ALU.mult,
                                               accum_out=t["xr_t"][:, col:col + 1])
                    wge(v, c, "se", se_acc)
                    st_done[tt] = c.tick("ve", v.tensor_copy(
                        t["stat_ssq"][:, col:col + 1],
                        t["acc_sq%d" % (tt % 2)][:, :]))
                blk.vector(ve1)

        if mode != "none":
            emit_stats_pass2(c, blk, row, mode)
            emit_apply_y(c, blk, row, xdst,
                         ydst if ydst is not None else t["ynorm_d"], mode)
        else:
            def fin(sy):
                wge(sy, c, "dO", c.cnt["dO"])
                wge(sy, c, "dG", c.cnt["dG"])
            blk.sync(fin)


def phase_qkv(c, row, y2_d):
    t = c.t
    Q0, K0 = 0, 8 * T
    with c.nc.Block() as blk:
        def vinit(g):
            V3 = t["V_t"][:, :].rearrange("p (a h e) -> p a h e", a=16, h=16, e=65)
            c.tick("gp", g.memset(V3[:, :, :, 64:65], 1.0))
        blk.gpsimd(vinit)
        gp_init = c.cnt["gp"]

        mmg = [0] * 4
        grpw = [0] * 4
        qk_mm = {}
        for tb in range(4):
            tl = [0] * 8

            def loads(sy, tb=tb, tl=tl, tn_gate=c.cnt["tn"]):
                if tb >= 1:
                    wge(sy, c, "tn", tn_gate)
                for cc in range(8):
                    tl[cc] = c.dma(
                        sy, "dT", t["ycT"][:, cc * 512:(cc + 1) * 512],
                        y2_d[(row * T + tb * 512):(row * T + (tb + 1) * 512),
                             cc * 128:(cc + 1) * 128],
                        transpose=True)
            blk.sync(loads)

            wld = [0] * 10
            for p in range(10):
                def wload(sy, p=p, tb=tb, wld=wld):
                    gi = tb * 10 + p
                    if gi == 0:
                        wge(sy, c, "dP", max(t["wqkcast"], t["wvcast"]))
                    if gi >= 2:
                        wge(sy, c, "tn", qk_mm.get(gi - 2, 0))
                    if p < 8:
                        wld[p] = c.dma(sy, "dW",
                                       t["wqkbuf%d" % (p % 2)][:, :],
                                       t["wqk_bf"][:, p * 2048:(p + 1) * 2048])
                    else:
                        g8 = p - 8
                        wld[p] = c.dma(sy, "dW", t["vwbuf%d" % (g8 % 2)][:, :],
                                       t["wv_bf"][:, g8 * 4096:(g8 + 1) * 4096])
                blk.sync(wload)

            grp = 0
            for p in range(8):
                wb = t["wqkbuf%d" % (p % 2)]
                for ki, kind in enumerate(("q", "k")):
                    bsb = t["bqp_s"] if kind == "q" else t["bkp_s"]
                    base = Q0 if kind == "q" else K0
                    bk = grp % 4
                    grp += 1

                    def mms(tn, p=p, ki=ki, bk=bk, wb=wb, tb=tb, tl=tl):
                        wge(tn, c, "dW", wld[p])
                        for cc in range(8):
                            wge(tn, c, "dT", tl[cc])
                        wge(tn, c, "se", grpw[bk])
                        last = None
                        for cc in range(8):
                            last = tn.matmul(
                                t["pb%d" % bk][:, :],
                                wb[:, ki * 1024 + cc * 128:ki * 1024 + (cc + 1) * 128],
                                t["ycT"][:, cc * 512:(cc + 1) * 512],
                                start=(cc == 0), stop=(cc == 7),
                                skip_group_check=True)
                        mmg[bk] = c.tick("tn", last)
                        if ki == 1:
                            qk_mm[tb * 10 + p] = mmg[bk]
                    blk.tensor(mms)

                    def evac(s, p=p, bk=bk, bsb=bsb, base=base, tb=tb):
                        wge(s, c, "tn", mmg[bk])
                        grpw[bk] = c.tick("se", s.activation(
                            t["hh"][:, base + p * T + tb * 512:
                                    base + p * T + (tb + 1) * 512],
                            t["pb%d" % bk][:, :], AF.Identity,
                            bias=bsb[:, p:p + 1]))
                    blk.scalar(evac)

            for g8 in range(2):
                for sub in range(4):
                    tkt = tb * 4 + sub
                    bk = grp % 4
                    grp += 1

                    def mms(tn, sub=sub, g8=g8, bk=bk, tb=tb, tl=tl):
                        wge(tn, c, "dW", wld[8 + g8])
                        wge(tn, c, "se", grpw[bk])
                        if tb == 0 and sub == 0 and g8 == 0:
                            wge(tn, c, "gp", gp_init)
                        for cc in range(8):
                            tn.matmul(
                                t["pb%d" % bk][:, :],
                                t["ycT"][:, cc * 512 + sub * 128:cc * 512 + (sub + 1) * 128],
                                t["vwbuf%d" % (g8 % 2)][:, cc * 512:(cc + 1) * 512],
                                start=(cc == 0), stop=False,
                                skip_group_check=True)
                        last = tn.matmul(t["pb%d" % bk][:, :], t["onesrow_b"][:, :],
                                         t["bvp_s"][0:1, g8 * 512:(g8 + 1) * 512],
                                         start=False, stop=True,
                                         skip_group_check=True)
                        mmg[bk] = c.tick("tn", last)
                        if sub == 3:
                            qk_mm[tb * 10 + 8 + g8] = mmg[bk]
                    blk.tensor(mms)

                    def evac(s, tkt=tkt, g8=g8, bk=bk):
                        wge(s, c, "tn", mmg[bk])
                        V3 = t["V_t"][:, :].rearrange("p (a h e) -> p a h e",
                                                      a=16, h=16, e=65)
                        grpw[bk] = c.tick("se", s.activation(
                            V3[:, tkt, g8 * 8:(g8 + 1) * 8, 0:64],
                            t["pb%d" % bk][:, :], AF.Identity,
                            bias=t["zero128"][:, :]))
                    blk.scalar(evac)

        def fin(sy):
            wge(sy, c, "se", c.cnt["se"])
            wge(sy, c, "dW", c.cnt["dW"])
            wge(sy, c, "dT", c.cnt["dT"])
        blk.sync(fin)


def phase_sav(c, row, oprime):
    t = c.t
    Q0, K0 = 0, 8 * T
    with c.nc.Block() as blk:
        if row == 0:
            def ninit(v):
                c.tick("ve", v.memset(t["normcols"][:, :], 0.0))
            blk.vector(ninit)

        av_gate = [c.cnt["ve"]]
        bc_gate = [c.cnt["se"]]
        for p in range(8):
            op_dma_gate = []
            sticks = [0] * 64
            eticks = [0] * 64
            avt = [0] * 64

            def emit_smm(k):
                qb, tkc = k // 16, k % 16
                par = k % 2

                def smm(tn, p=p, qb=qb, tkc=tkc, k=k, par=par):
                    if k >= 2:
                        wge(tn, c, "se", eticks[k - 2])
                    tn.matmul(t["qb%d" % par][:, 0:512],
                              t["hh"][0:64, K0 + p * T + tkc * 128:
                                      K0 + p * T + (tkc + 1) * 128],
                              t["hh"][0:64, Q0 + p * T + qb * 512:
                                      Q0 + p * T + (qb + 1) * 512],
                              start=True, stop=True, tile_position=(0, 0),
                              skip_group_check=True)
                    sticks[k] = c.tick("tn", tn.matmul(
                        t["qb%d" % par][:, 512:1024],
                        t["hh"][64:128, K0 + p * T + tkc * 128:
                                K0 + p * T + (tkc + 1) * 128],
                        t["hh"][64:128, Q0 + p * T + qb * 512:
                                Q0 + p * T + (qb + 1) * 512],
                        start=True, stop=True, tile_position=(64, 0),
                        skip_group_check=True))
                blk.tensor(smm)

                def sexp(s, k=k, par=par):
                    wge(s, c, "tn", sticks[k])
                    if k >= 2:
                        wge(s, c, "tn", avt[k - 2])
                    eticks[k] = c.tick("se", s.activation(
                        t["hl"][:, par * 1024:(par + 1) * 1024],
                        t["qb%d" % par][:, 0:1024], AF.Exp,
                        scale=0.125, bias=t["negstab"][:, :]))
                blk.scalar(sexp)

            def emit_avmm(k):
                qb, tkc = k // 16, k % 16
                par = k % 2

                def avmm(tn, p=p, tkc=tkc, k=k, par=par):
                    wge(tn, c, "se", eticks[k])
                    if tkc == 0:
                        wge(tn, c, "ve", av_gate[0])
                    hA, hB = 2 * p, 2 * p + 1
                    tn.matmul(t["qb2"][0:65, 0:512],
                              t["V_t"][:, tkc * 1040 + hA * 65:
                                       tkc * 1040 + hA * 65 + 65],
                              t["hl"][:, par * 1024:par * 1024 + 512],
                              start=(tkc == 0), stop=(tkc == 15),
                              skip_group_check=True)
                    avt[k] = c.tick("tn", tn.matmul(
                        t["qb2"][0:65, 512:1024],
                        t["V_t"][:, tkc * 1040 + hB * 65:
                                 tkc * 1040 + hB * 65 + 65],
                        t["hl"][:, par * 1024 + 512:(par + 1) * 1024],
                        start=(tkc == 0), stop=(tkc == 15),
                        skip_group_check=True))
                blk.tensor(avmm)

            def emit_epilogue(qb):
                k15 = qb * 16 + 15

                def ep_ln(s, qb=qb, k15=k15):
                    wge(s, c, "tn", avt[k15])
                    c.tick("se", s.activation(t["oscr3"][0:1, 0:1024],
                                              t["qb2"][64:65, 0:1024], AF.Ln,
                                              bias=t["zero1"][:, :]))
                blk.scalar(ep_ln)
                se_l = c.cnt["se"]

                def ep_tn(tn, se_l=se_l, bcg=bc_gate[0]):
                    wge(tn, c, "se", se_l)
                    wge(tn, c, "se", bcg)
                    tn.matmul(t["qb3"][0:64, 0:512], t["onesrow_f"][0:1, 0:64],
                              t["oscr3"][0:1, 0:512], start=True, stop=True,
                              skip_group_check=True)
                    c.tick("tn", tn.matmul(t["qb3"][0:64, 512:1024],
                                           t["onesrow_f"][0:1, 0:64],
                                           t["oscr3"][0:1, 512:1024],
                                           start=True, stop=True,
                                           skip_group_check=True))
                blk.tensor(ep_tn)
                tn_b = c.cnt["tn"]

                def ep_exp(s, tn_b=tn_b):
                    wge(s, c, "tn", tn_b)
                    c.tick("se", s.activation(t["acc"][0:64, 0:1024],
                                              t["qb3"][0:64, 0:1024], AF.Exp,
                                              bias=t["zero128"][0:64, :],
                                              scale=-1.0))
                blk.scalar(ep_exp)
                se_c = c.cnt["se"]

                def ep_ve2(v, qb=qb, se_c=se_c, dO_gate=c.cnt["dO"]):
                    wge(v, c, "se", se_c)
                    if qb == 0:
                        wge(v, c, "dO", dO_gate)
                    v.tensor_mul(t["opab"][0:64, qb * 512:(qb + 1) * 512],
                                 t["qb2"][0:64, 0:512], t["acc"][0:64, 0:512])
                    av_gate[0] = c.tick("ve", v.tensor_mul(
                        t["opab"][64:128, qb * 512:(qb + 1) * 512],
                        t["qb2"][0:64, 512:1024], t["acc"][0:64, 512:1024]))
                blk.vector(ep_ve2)
                ve_m = c.cnt["ve"]
                op_dma_gate.append(ve_m)

                def ep_sq(s, qb=qb, ve_m=ve_m):
                    wge(s, c, "ve", ve_m)
                    s.activation(t["ynt0"][0:64, 0:512],
                                 t["opab"][0:64, qb * 512:(qb + 1) * 512],
                                 AF.Square, bias=t["zero128"][0:64, :])
                    c.tick("se", s.activation(
                        t["ynt0"][0:64, 512:1024],
                        t["opab"][64:128, qb * 512:(qb + 1) * 512],
                        AF.Square, bias=t["zero128"][0:64, :]))
                blk.scalar(ep_sq)
                se_q = c.cnt["se"]

                def ep_nsq(tn, se_q=se_q):
                    wge(tn, c, "se", se_q)
                    last = None
                    for sub in range(8):
                        last = tn.matmul(t["qb3"][:, 1016 + sub:1017 + sub],
                                         t["ynt0"][0:64, sub * 128:(sub + 1) * 128],
                                         t["ones128b"][0:64, 0:1],
                                         start=True, stop=True,
                                         skip_group_check=True)
                    c.tick("tn", last)
                blk.tensor(ep_nsq)
                tn_n = c.cnt["tn"]

                def ep_nrm(s, tn_n=tn_n):
                    wge(s, c, "tn", tn_n)
                    s.activation(t["sq_scr"][:, 0:8], t["qb3"][:, 1016:1024],
                                 AF.Ln, bias=t["zero128"][:, :])
                    s.drain()
                    bc_gate[0] = c.tick("se", s.activation(
                        t["sq_scr"][:, 8:16], t["sq_scr"][:, 0:8], AF.Exp,
                        bias=t["zero128"][:, :], scale=0.5))
                blk.scalar(ep_nrm)
                se_n = c.cnt["se"]

                def ep_acc(v, p=p, se_n=se_n):
                    wge(v, c, "se", se_n)
                    v.reduce_sum(t["st_a"][:, :], t["sq_scr"][:, 8:12], axis=AX)
                    v.reduce_sum(t["st_b"][:, :], t["sq_scr"][:, 12:16], axis=AX)
                    v.drain()
                    v.tensor_add(t["normcols"][:, 2 * p:2 * p + 1],
                                 t["normcols"][:, 2 * p:2 * p + 1], t["st_a"][:, :])
                    v.drain()
                    c.tick("ve", v.tensor_add(
                        t["normcols"][:, 2 * p + 1:2 * p + 2],
                        t["normcols"][:, 2 * p + 1:2 * p + 2], t["st_b"][:, :]))
                blk.vector(ep_acc)

            emit_smm(0)
            for k in range(1, 64):
                emit_smm(k)
                emit_avmm(k - 1)
                if (k - 1) % 16 == 15:
                    emit_epilogue((k - 1) // 16)
            emit_avmm(63)
            emit_epilogue(3)

            def opdma(g, p=p, row=row, gates=tuple(op_dma_gate)):
                for gv in gates:
                    wge(g, c, "ve", gv)
                base = (row * 8 + p) * 128
                c.dma(g, "dO", oprime[base:base + 64, :], t["opab"][0:64, :])
                c.dma(g, "dO", oprime[base + 64:base + 128, :],
                      t["opab"][64:128, :])
            blk.gpsimd(opdma)

        def fin(sy):
            wge(sy, c, "dO", c.cnt["dO"])
        blk.sync(fin)


def phase_normfin(c, normpart):
    t = c.t
    with c.nc.Block() as blk:
        def tn1(tn):
            c.tick("tn", tn.matmul(t["qb0"][0:16, 0:1], t["normcols"][:, :],
                                   t["ones128"][:, :], start=True, stop=True,
                                   skip_group_check=True))
        blk.tensor(tn1)
        tn_f = c.cnt["tn"]

        def ve1(v, tn_f=tn_f):
            wge(v, c, "tn", tn_f)
            c.tick("ve", v.tensor_copy(t["st_c"][0:16, 0:1], t["qb0"][0:16, 0:1]))
        blk.vector(ve1)
        ve_f = c.cnt["ve"]

        def gp1(g, ve_f=ve_f):
            wge(g, c, "ve", ve_f)
            c.dma(g, "dO", normpart[:, :], t["st_c"][0:16, 0:1])
            wge(g, c, "dO", c.cnt["dO"])
        blk.gpsimd(gp1)


def alloc_l2(c):
    c.sb("an_sb", [8, 16], F32)
    c.sb("gt_sb", [16, 1], F32)
    c.sb("bo_sb", [16, D], F32)
    c.sb("g16", [16, 1], F32)
    c.sb("clip16", [16, 1], F32)
    c.sb("gvec", [128, 8], F32)
    c.sb("gfe", [1, 8], F32)
    c.sb("gfo", [1, 8], F32)
    c.sb("bconst_b", [1, D], BF16)
    c.sb("elo_s", [1, 128], F32)
    c.sb("ehi_s", [1, 128], F32)
    c.dr("opri", [16 * 128, T], BF16, "ExternalInput")
    c.dr("allnorm", [8, 16], F32, "ExternalInput")
    c.dr("wop", [128, 8192], F32, "ExternalInput")
    c.dr("bop", [16, D], F32, "ExternalInput")
    c.dr("gatep", [16, 1], F32, "ExternalInput")
    c.dr("elo", [1, 128], F32, "ExternalInput")
    c.dr("ehi", [1, 128], F32, "ExternalInput")
    c.dr("gd", [16, 1], F32)


def phase_l2prep(c):
    t = c.t
    with c.nc.Block() as blk:
        def g1(g):
            c.dma(g, "dH", t["an_sb"][:, :], t["allnorm"][:, :])
            c.dma(g, "dH", t["gt_sb"][:, :], t["gatep"][:, :])
            c.dma(g, "dH", t["bo_sb"][:, :], t["bop"][:, :])
            c.dma(g, "dH", t["elo_s"][:, :], t["elo"][:, :])
            c.dma(g, "dH", t["ehi_s"][:, :], t["ehi"][:, :])
            for i in range(8):
                c.dma(g, "dH", t["ycT"][:, i * 1024:(i + 1) * 1024],
                      t["wop"][:, i * 1024:(i + 1) * 1024])
            g.wait_ge(c.sem["dH"], c.cnt["dH"])
        blk.gpsimd(g1)
        ld = c.cnt["dH"]

        def tn1(tn, ld=ld):
            wge(tn, c, "dH", ld)
            c.tick("tn", tn.matmul(t["qb0"][0:16, 0:1], t["an_sb"][:, :],
                                   t["ones128"][0:8, 0:1], start=True,
                                   stop=True, skip_group_check=True))
        blk.tensor(tn1)
        tn_s = c.cnt["tn"]

        def ve1(v, tn_s=tn_s, ld=ld):
            wge(v, c, "tn", tn_s)
            wge(v, c, "dH", ld)
            v.tensor_copy(t["st_a"][0:16, 0:1], t["qb0"][0:16, 0:1])
            v.drain()
            v.tensor_scalar(t["st_a"][0:16, 0:1], t["st_a"][0:16, 0:1],
                            1.0 / (B_TOT * T), 1e-5, ALU.mult, ALU.max)
            v.drain()
            v.reciprocal(t["st_b"][0:16, 0:1], t["st_a"][0:16, 0:1])
            v.tensor_scalar(t["clip16"][0:16, 0:1], t["gt_sb"][:, :],
                            0.0, 1.0, ALU.max, ALU.min)
            v.drain()
            v.scalar_tensor_tensor(t["g16"][0:16, 0:1], t["clip16"][0:16, 0:1],
                                   1.0 / 16.0, t["st_b"][0:16, 0:1],
                                   ALU.mult, ALU.mult)
            v.drain()
            c.tick("ve", v.tensor_scalar(t["bo_sb"][:, :], t["bo_sb"][:, :],
                                         t["clip16"][:, :], 1.0 / 16.0,
                                         ALU.mult, ALU.mult))
        blk.vector(ve1)
        ve_g = c.cnt["ve"]

        def g2(g, ve_g=ve_g):
            wge(g, c, "ve", ve_g)
            c.dma(g, "dH", t["gd"][:, :], t["g16"][0:16, 0:1])
            g.wait_ge(c.sem["dH"], c.cnt["dH"])
            g.dma_start(t["gfe"][0:1, :], t["gd"][0:16:2, 0:1],
                        allow_slow_non_contiguous=True).then_inc(c.sem["dH"], 16)
            c.cnt["dH"] += 16
            g.dma_start(t["gfo"][0:1, :], t["gd"][1:16:2, 0:1],
                        allow_slow_non_contiguous=True).then_inc(c.sem["dH"], 16)
            c.cnt["dH"] += 16
            g.wait_ge(c.sem["dH"], c.cnt["dH"])
        blk.gpsimd(g2)
        ld2 = c.cnt["dH"]

        def tn2(tn, ld2=ld2):
            wge(tn, c, "dH", ld2)
            tn.matmul(t["qb1"][:, 0:8], t["elo_s"][:, :], t["gfe"][:, :],
                      start=True, stop=False, skip_group_check=True)
            tn.matmul(t["qb1"][:, 0:8], t["ehi_s"][:, :], t["gfo"][:, :],
                      start=False, stop=True, skip_group_check=True)
            tn.matmul(t["qb2"][0:1, 0:512], t["ones128"][0:16, 0:1],
                      t["bo_sb"][:, 0:512], start=True, stop=True,
                      skip_group_check=True)
            c.tick("tn", tn.matmul(t["qb2"][0:1, 512:1024],
                                   t["ones128"][0:16, 0:1],
                                   t["bo_sb"][:, 512:1024], start=True,
                                   stop=True, skip_group_check=True))
        blk.tensor(tn2)
        tn_g = c.cnt["tn"]

        def ve2(v, tn_g=tn_g):
            wge(v, c, "tn", tn_g)
            v.tensor_copy(t["gvec"][:, :], t["qb1"][:, 0:8])
            c.tick("ve", v.tensor_copy(t["bconst_b"][0:1, :],
                                       t["qb2"][0:1, 0:1024]))
        blk.vector(ve2)
        ve_c = c.cnt["ve"]

        def se1(s, ve_c=ve_c):
            wge(s, c, "ve", ve_c)
            last = None
            for i in range(8):
                sl = t["ycT"][:, i * 1024:(i + 1) * 1024]
                last = s.activation(sl, sl, AF.Identity,
                                    bias=t["zero128"][:, :],
                                    scale=t["gvec"][:, i:i + 1])
            c.tick("se", last)
        blk.scalar(se1)


def phase_oproj_add(c, row, xsrc, xdst, ydst):
    t = c.t
    with c.nc.Block() as blk:
        def opload(g):
            for p in range(8):
                c.dma(g, "dW", t["hh"][:, p * T:(p + 1) * T],
                      t["opri"][(row * 8 + p) * 128:(row * 8 + p) * 128 + 128, :])
            g.wait_ge(c.sem["dW"], c.cnt["dW"])
        blk.gpsimd(opload)
        opld = c.cnt["dW"]

        ld = [0] * NT
        mm = [0] * NT
        ve_x3 = [0] * NT
        st_x = [0] * NT
        st_done = [0] * NT
        for tt in range(NT):
            col = row * NT + tt
            xb = t["xt%d" % (tt % 3)]
            x3 = t["x3t%d" % (tt % 2)]
            bk = (tt % 2) * 2
            rowslice = slice(row * T + tt * 128, row * T + tt * 128 + 128)

            def load(sy, tt=tt, xb=xb, rowslice=rowslice):
                if tt >= 3:
                    wge(sy, c, "ve", ve_x3[tt - 3])
                ld[tt] = c.dma(sy, "dX", xb[:, :], xsrc[rowslice, :])
            blk.sync(load)

            def mms(tn, tt=tt, bk=bk):
                if tt == 0:
                    wge(tn, c, "dW", opld)
                if tt >= 2:
                    wge(tn, c, "ve", ve_x3[tt - 2])
                last = None
                for of in range(2):
                    for p in range(8):
                        tn.matmul(t["pb%d" % (bk + of)][:, :],
                                  t["hh"][:, p * T + tt * 128:p * T + (tt + 1) * 128],
                                  t["ycT"][:, p * 1024 + of * 512:p * 1024 + (of + 1) * 512],
                                  start=(p == 0), stop=False,
                                  skip_group_check=True)
                    last = tn.matmul(t["pb%d" % (bk + of)][:, :],
                                     t["onesrow_b"][:, :],
                                     t["bconst_b"][0:1, of * 512:(of + 1) * 512],
                                     start=False, stop=True,
                                     skip_group_check=True)
                mm[tt] = c.tick("tn", last)
            blk.tensor(mms)

            def vex3(v, tt=tt, bk=bk, xb=xb, x3=x3):
                wge(v, c, "tn", mm[tt])
                wge(v, c, "dX", ld[tt])
                if tt >= 2:
                    wge(v, c, "dO", st_x[tt - 2])
                    wge(v, c, "ve", st_done[tt - 2])
                v.tensor_add(x3[:, 0:512], t["pb%d" % bk][:, :], xb[:, 0:512])
                ve_x3[tt] = c.tick("ve", v.tensor_add(
                    x3[:, 512:1024], t["pb%d" % (bk + 1)][:, :],
                    xb[:, 512:1024]))
            blk.vector(vex3)

            def store(sy, tt=tt, x3=x3, rowslice=rowslice):
                wge(sy, c, "ve", ve_x3[tt])
                st_x[tt] = c.dma(sy, "dO", xdst[rowslice, :], x3[:, :])
            blk.sync(store)

            def se1(s, tt=tt, x3=x3):
                wge(s, c, "ve", ve_x3[tt])
                if tt >= 2:
                    wge(s, c, "ve", st_done[tt - 2])
                c.tick("se", s.activation(
                    t["sq_scr"][:, :], x3[:, :], AF.Square,
                    bias=t["zero128"][:, :],
                    accum_out=t["acc_sq%d" % (tt % 2)][:, :]))
            blk.scalar(se1)
            se_acc = c.cnt["se"]

            def ve1(v, tt=tt, col=col, x3=x3, se_acc=se_acc):
                v.reduce_sum(t["stat_sum"][:, col:col + 1], x3[:, :], axis=AX)
                v.scalar_tensor_tensor(t["gp_scr"][:, :], x3[:, :], 1.0,
                                       t["rbc_s"][:, :], ALU.mult, ALU.mult,
                                       accum_out=t["xr_t"][:, col:col + 1])
                wge(v, c, "se", se_acc)
                st_done[tt] = c.tick("ve", v.tensor_copy(
                    t["stat_ssq"][:, col:col + 1],
                    t["acc_sq%d" % (tt % 2)][:, :]))
            blk.vector(ve1)

        emit_stats_pass2(c, blk, row, "ffn")
        emit_apply_y(c, blk, row, xdst, ydst, "ffn")



def build_l1():
    nc = bass.Bass("TRN2", target_bir_lowering=False)
    c = Ctx(nc)
    xin = c.dr("xin", [RPC * T, D], F32, "ExternalInput")
    x2o = c.dr("x2o", [RPC * T, D], F32, "ExternalOutput")
    oprime = c.dr("oprime", [16 * 128, T], BF16, "ExternalOutput")
    normpart = c.dr("normpart", [16, 1], F32, "ExternalOutput")
    y2_d = c.dr("y2_d", [RPC * T, D], BF16)
    alloc_common(c)
    alloc_attn(c)
    phase_consts(c)
    phase_attn_consts(c)
    for r in range(RPC):
        phase_lnpass(c, r, xin, "ffn")
    phase_coor(c)
    for r in range(RPC):
        phase_mask(c, r)
    for r in range(RPC):
        phase_scatter(c, r)
        phase_heavy(c, r)
        phase_light_m1(c, r)
        phase_light2_add(c, r, xin, x2o, "attn", ydst=y2_d)
    for r in range(RPC):
        phase_qkv(c, r, y2_d)
        phase_sav(c, r, oprime)
    phase_normfin(c, normpart)
    return nc


def build_l2():
    nc = bass.Bass("TRN2", target_bir_lowering=False)
    c = Ctx(nc)
    x2i = c.dr("x2i", [RPC * T, D], F32, "ExternalInput")
    xout = c.dr("xout", [RPC * T, D], F32, "ExternalOutput")
    alloc_common(c)
    alloc_l2(c)
    phase_consts(c)
    phase_l2prep(c)
    for r in range(RPC):
        phase_oproj_add(c, r, x2i, xout, c.t["ynorm_d"])
    phase_coor(c)
    for r in range(RPC):
        phase_mask(c, r)
    for r in range(RPC):
        phase_scatter(c, r)
        phase_heavy(c, r)
        phase_light_m1(c, r)
        phase_light2_add(c, r, xout, xout, "none")
    return nc



def ffn_host_inputs(inputs):
    w1l = np.asarray(inputs["L0_ffn_w1_l"], np.float32)
    w2l = np.asarray(inputs["L0_ffn_w2_l"], np.float32)
    w1h = np.asarray(inputs["L0_ffn_w1_h"], np.float32)
    w2h = np.asarray(inputs["L0_ffn_w2_h"], np.float32)
    return {
        "w1lp": np.ascontiguousarray(
            w1l.T.reshape(8, 128, DFFL).transpose(1, 0, 2).reshape(128, 8 * DFFL)),
        "w2lp": np.ascontiguousarray(
            w2l.T.reshape(4, 128, D).transpose(1, 0, 2).reshape(128, 4 * D)),
        "b1lp": np.ascontiguousarray(
            np.asarray(inputs["L0_ffn_b1_l"], np.float32).reshape(4, 128).T),
        "b2lp": np.asarray(inputs["L0_ffn_b2_l"], np.float32).reshape(1, D).copy(),
        "w1hp": np.ascontiguousarray(
            w1h.T.reshape(8, 128, 32, 128).transpose(1, 2, 0, 3).reshape(128, 32 * D)),
        "w2hp": np.ascontiguousarray(
            w2h.T.reshape(32, 128, 2, 512).transpose(1, 0, 2, 3).reshape(128, 32 * D)),
        "b1hp": np.ascontiguousarray(
            np.asarray(inputs["L0_ffn_b1_h"], np.float32).reshape(32, 128).T),
        "b2hp": np.asarray(inputs["L0_ffn_b2_h"], np.float32).reshape(1, D).copy(),
        "glp": np.ascontiguousarray(
            np.asarray(inputs["L0_ffn_gamma_l"], np.float32).reshape(8, 128).T),
        "ghp": np.ascontiguousarray(
            np.asarray(inputs["L0_ffn_gamma_h"], np.float32).reshape(8, 128).T),
        "rbc": np.ascontiguousarray(np.broadcast_to(
            np.asarray(inputs["L0_ffn_routing_token"], np.float32), (128, D))),
        "trif": np.triu(np.ones((128, 128), np.float32)),
    }


def attn_host_inputs(inputs):
    Wq = np.asarray(inputs["L0_attn_Wq"], np.float32)
    Wk = np.asarray(inputs["L0_attn_Wk"], np.float32)
    Wv = np.asarray(inputs["L0_attn_Wv"], np.float32)
    bq = np.asarray(inputs["L0_attn_bq"], np.float32)
    bk = np.asarray(inputs["L0_attn_bk"], np.float32)
    bv = np.asarray(inputs["L0_attn_bv"], np.float32)
    qk = lambda W: W.reshape(8, 2, 8, 128, 64).transpose(3, 0, 2, 1, 4).reshape(128, 8, 1024)
    wqk = np.concatenate([qk(Wq), qk(Wk)], axis=2).reshape(128, 16384)
    return {
        "wqkp": np.ascontiguousarray(wqk),
        "wvp": np.ascontiguousarray(
            Wv.reshape(2, 8, 8, 128, 64).transpose(3, 0, 2, 1, 4).reshape(128, 8192)),
        "bqp": np.ascontiguousarray(bq.reshape(8, 2, 64).transpose(1, 2, 0).reshape(128, 8)),
        "bkp": np.ascontiguousarray(bk.reshape(8, 2, 64).transpose(1, 2, 0).reshape(128, 8)),
        "bvp": np.ascontiguousarray(bv.reshape(1, 1024)),
    }


def l2_host_inputs(inputs):
    Wo = np.asarray(inputs["L0_attn_Wo"], np.float32)
    elo = np.zeros((1, 128), np.float32)
    elo[0, 0:64] = 1.0
    ehi = np.zeros((1, 128), np.float32)
    ehi[0, 64:128] = 1.0
    return {
        "wop": np.ascontiguousarray(
            Wo.reshape(8, 2, 64, D).transpose(1, 2, 0, 3).reshape(128, 8192)),
        "bop": np.asarray(inputs["L0_attn_bo"], np.float32).copy(),
        "gatep": np.asarray(inputs["L0_attn_gate"], np.float32).reshape(16, 1).copy(),
        "elo": elo, "ehi": ehi,
    }


def kernel(**inputs):
    x = np.asarray(inputs["L0_attn_hidden_states"], np.float32)
    ffn_in = ffn_host_inputs(inputs)
    attn_in = attn_host_inputs(inputs)
    l2_in = l2_host_inputs(inputs)

    nc1 = build_l1()
    maps1 = []
    for core in range(8):
        m = dict(ffn_in)
        m.update(attn_in)
        m["xin"] = np.ascontiguousarray(
            x[2 * core:2 * core + 2].reshape(RPC * T, D))
        maps1.append(m)
    res1 = run_bass_kernel_spmd(nc1, maps1, core_ids=list(range(8)))

    allnorm = np.ascontiguousarray(
        np.stack([res1.results[cc]["normpart"][:, 0] for cc in range(8)]))
    nc2 = build_l2()
    maps2 = []
    for core in range(8):
        m = dict(ffn_in)
        m.update(l2_in)
        m["x2i"] = res1.results[core]["x2o"]
        m["opri"] = res1.results[core]["oprime"]
        m["allnorm"] = allnorm
        maps2.append(m)
    res2 = run_bass_kernel_spmd(nc2, maps2, core_ids=list(range(8)))
    out = np.concatenate(
        [r["xout"].reshape(RPC, T, D) for r in res2.results], axis=0)
    return out
